# version 20
# baseline (speedup 1.0000x reference)
import os
import numpy as np
import ml_dtypes
import concourse.bass as bass
import concourse.mybir as mybir
from concourse.bass_utils import run_bass_kernel_spmd
from contextlib import ExitStack

F32 = mybir.dt.float32
BF16 = mybir.dt.bfloat16
AF = mybir.ActivationFunctionType
ALU = mybir.AluOpType
AX = mybir.AxisListType

NCORES = 8
H = 1024
T = 4096
NT = 32
NTS = 16
NST = 4
ST_TT = 8
FC = 8
NHP = 8
NU = 66
NEG = -1e30

KSTAGE = os.environ.get("KSTAGE", "full")
KMOE = os.environ.get("KMOE", "sparse")
NGRAN = 192
GTOK = 256
NSLAB = NGRAN * GTOK
KNT = int(os.environ.get("KNT", "0"))

ENGS = ("sync", "scalar", "vector", "tensor", "gpsimd")
SEMS = ("ts", "vs", "cs",
        "lc", "lx0", "lx1", "lxr", "lt0", "lt1", "la0", "la1",
        "gw0", "gw1", "gwo", "gm0", "gm1", "gn0", "gn1", "sx", "sr0", "sr1", "st",
        "sy0", "sy1", "sz", "gs0", "gs1", "gp0", "gp1", "ux0", "ux1",
        "ua0", "ua1", "ud0", "ud1", "go0", "go1", "pcw",
        "gg0", "gg1", "gh0", "gh1", "gf0", "gf1")


class KB:
    def __init__(self):
        self.ops = {e: [] for e in ENGS}
        self.cnt = {}
        self.sems = {}

    def sem(self, stack, nc, name):
        self.sems[name] = stack.enter_context(nc.semaphore(name))
        self.cnt[name] = 0

    def W(self, eng, sem, val=None):
        if isinstance(sem, tuple):
            sem, val = sem
        if val is None:
            val = self.cnt[sem]
        if not val or val <= 0:
            return
        s = self.sems[sem]
        self.ops[eng].append(lambda e, s=s, v=val: e.wait_ge(s, v))

    def O(self, eng, fname, *args, inc=None, **kw):
        semh, amt = None, 0
        if inc is not None:
            nm, amt = inc
            self.cnt[nm] += amt
            semh = self.sems[nm]

        def ap(a):
            if a is None or isinstance(a, (int, float, bass.AP)):
                return a
            if hasattr(a, "shape") and hasattr(a, "__getitem__"):
                return a[tuple(slice(None) for _ in a.shape)]
            return a

        args = tuple(ap(a) for a in args)
        kw = {k: (ap(v) if k in ("out", "in_", "bias", "scale", "accum_out") else v)
              for k, v in kw.items()}

        def f(e, fname=fname, args=args, kw=kw, semh=semh, amt=amt):
            ins = getattr(e, fname)(*args, **kw)
            if semh is not None:
                ins.then_inc(semh, amt)
        self.ops[eng].append(f)

    def mark(self, sem):
        return (sem, self.cnt[sem])

    def barrier(self):
        snap = {s: self.cnt[s] for s in SEMS}
        for e in ENGS:
            for s in SEMS:
                self.W(e, s, snap[s])


def prod(xs):
    r = 1
    for v in xs:
        r *= v
    return r


def _view(ap, shape):
    if len(shape) == 2:
        return ap
    if len(shape) == 3:
        return ap.rearrange("p (a b) -> p a b", a=shape[1])
    return ap.rearrange("p (a b c) -> p a b c", a=shape[1], b=shape[2])


def build_program():
    nc = bass.Bass("TRN2", target_bir_lowering=False)

    L = {}
    L["d_x"] = nc.dram_tensor("x", [T, H], F32, kind="ExternalInput")
    L["d_wq"] = nc.dram_tensor("wq", [NHP, 128, FC, 128], F32, kind="ExternalInput")
    L["d_wk"] = nc.dram_tensor("wk", [NHP, 128, FC, 128], F32, kind="ExternalInput")
    L["d_wv"] = nc.dram_tensor("wv", [NHP, 128, FC, 128], F32, kind="ExternalInput")
    L["d_wo"] = nc.dram_tensor("wo", [128, FC, H], F32, kind="ExternalInput")
    L["d_rw"] = nc.dram_tensor("rw", [128, FC, 64], F32, kind="ExternalInput")
    L["d_bqk"] = nc.dram_tensor("bqk", [128, 16], F32, kind="ExternalInput")
    L["d_vb"] = nc.dram_tensor("vb", [128, H], F32, kind="ExternalInput")
    L["d_ob"] = nc.dram_tensor("ob", [128, H], F32, kind="ExternalInput")
    L["d_eb"] = nc.dram_tensor("eb", [128, 64], F32, kind="ExternalInput")
    L["d_i64"] = nc.dram_tensor("i64", [128, 64], F32, kind="ExternalInput")
    L["d_i8"] = nc.dram_tensor("i8", [128, 8], F32, kind="ExternalInput")
    L["d_idf"] = nc.dram_tensor("idf", [128, 128], F32, kind="ExternalInput")
    L["d_idb"] = nc.dram_tensor("idb", [128, 128], BF16, kind="ExternalInput")
    L["d_ntri"] = nc.dram_tensor("ntri", [128, 128], F32, kind="ExternalInput")
    if KSTAGE not in ("ln", "attn", "route"):
        L["d_wg"] = nc.dram_tensor("wg", [NU, 128, FC, 512], F32, kind="ExternalInput")
        L["d_wu"] = nc.dram_tensor("wu", [NU, 128, FC, 512], F32, kind="ExternalInput")
        L["d_wd"] = nc.dram_tensor("wd", [NU, 128, 4, 1024], F32, kind="ExternalInput")
    if KMOE == "sparse" and KSTAGE == "full":
        L["d_ustr"] = nc.dram_tensor("ustr", [128, 128], F32, kind="ExternalInput")
        L["d_onesm"] = nc.dram_tensor("onesm", [128, 128], F32, kind="ExternalInput")
        L["d_ei"] = nc.dram_tensor("ei", [128, 64], F32, kind="ExternalInput")
        L["d_gio"] = nc.dram_tensor("gio", [128, NGRAN], F32, kind="ExternalInput")
        L["d_iop"] = nc.dram_tensor("iop", [128, 1], F32, kind="ExternalInput")
        L["d_xntm"] = nc.dram_tensor("xntm", [T, H], BF16, kind="Internal")
        L["d_slab"] = nc.dram_tensor("slab", [NSLAB, H], BF16, kind="Internal")
        L["d_ybig"] = nc.dram_tensor("ybig", [NSLAB, H], BF16, kind="Internal")
        L["d_wgub"] = nc.dram_tensor("wgub", [64 * 128, 8192], BF16, kind="Internal")
        L["d_wdb"] = nc.dram_tensor("wdb", [64 * 128, 4096], BF16, kind="Internal")
    L["d_xnscr"] = nc.dram_tensor("xnscr", [128, NT, FC, 128], BF16, kind="Internal")
    L["d_rscr"] = nc.dram_tensor("rscr", [NT, 128, H], F32, kind="Internal")
    L["d_y"] = nc.dram_tensor("y", [T, H], F32, kind="ExternalOutput")

    kb = KB()
    with ExitStack() as stk:
        def sb(name, shape, dt=F32):
            return stk.enter_context(nc.sbuf_tensor(name, shape, dt))

        for nm, shape, dt in (
            ("rw_sb", [128, FC * 64], F32), ("bqk_sb", [128, 16], F32),
            ("vb_sb", [128, H], F32), ("ob_sb", [128, H], F32),
            ("eb_sb", [128, 64], F32), ("i64_sb", [128, 64], F32),
            ("i8_sb", [128, 8], F32), ("idf_sb", [128, 128], F32),
            ("idb_sb", [128, 128], BF16), ("ntri_sb", [128, 128], F32),
            ("scores", [128, NT * 64], F32), ("coef", [128, NT * 64], F32),
            ("sfc", [128, 1024], F32), ("eqb", [128, 1024], F32),
            ("idxb", [128, 1024], F32),
            ("gm1", [128, 128], F32), ("gcnt", [128, 128], F32),
            ("gm2r", [128, 128], F32), ("gge", [128, 128], F32),
            ("gdd", [128, 128], F32), ("gsb", [128, 128], F32),
            ("gmk", [128, 128], F32), ("grA", [128, 128], F32),
            ("grB", [128, 128], F32),
            ("m16", [128, 16], F32), ("m16b", [128, 16], F32),
            ("ws16", [128, 16], F32),
            ("mxc", [128, 8], F32),
            ("ns2", [128, 2], F32), ("nm2", [128, 2], F32), ("mm2", [128, 2], F32),
            ("var2", [128, 2], F32), ("sd2", [128, 2], F32), ("rr2", [128, 2], F32),
            ("sq2", [128, 2], F32), ("nmx2", [128, 2], F32), ("rs2", [128, 2], F32),
            ("rc2", [128, 2], F32),
            ("topke", [128, 256], F32), ("topkw", [128, 256], F32),
            ("rk8", [128, 256], F32), ("rankb2", [128, 2048], F32),
            ("ustr_sb", [128, 128], F32), ("onesm_sb", [128, 128], F32),
            ("ei_sb", [128, 64], F32), ("gio_sb", [128, NGRAN], F32),
            ("iop_sb", [128, 1], F32),
        ):
            L[nm] = sb(nm, shape, dt)

        AR = 73728
        arena = sb("arena", [128, AR], BF16)

        def cbf(off, shape):
            return _view(arena[:, off:off + prod(shape[1:])], shape)

        def cf32(off, shape):
            n = prod(shape[1:])
            return _view(arena[:, off:off + 2 * n].bitcast(F32), shape)

        L["xn_sb"] = cbf(0, [128, NTS, FC, 128])
        L["wo_bf"] = cbf(16384, [128, FC * H])
        L["wq_bf"] = [cbf(24576 + i * 1024, [128, 1024]) for i in range(2)]
        L["wk_bf"] = [cbf(26624 + i * 1024, [128, 1024]) for i in range(2)]
        L["wv_bf"] = [cbf(28672 + i * 1024, [128, 1024]) for i in range(2)]
        L["qf"] = [cbf(30720 + i * 2048, [128, 2048]) for i in range(2)]
        L["kf"] = [cbf(34816 + i * 2048, [128, 2048]) for i in range(2)]
        L["vt"] = [cbf(38912 + i * 2048, [128, 2048]) for i in range(2)]
        L["Pb2"] = [cbf(43008, [128, 2048]), cbf(63488, [128, 2048])]
        L["PT2"] = [cbf(45056, [128, 2048]), cbf(71680, [128, 2048])]
        L["comb"] = cbf(47104, [128, NHP, NTS, 128])
        L["xn_stage"] = [cbf(63488 + i * 1024, [128, FC, 128]) for i in range(2)]
        L["x_tile"] = [cf32(65536 + i * 2048, [128, 1024]) for i in range(2)]
        L["xn_f32"] = cf32(69632, [128, 1024])
        L["xnT"] = cf32(71680, [128, 1024])
        L["R_dbg"] = [cf32(30720 + i * 2048, [128, 1024]) for i in range(2)]
        L["xtm"] = [cbf(69632 + i * 1024, [128, 1024]) for i in range(2)]
        L["srowi"] = L["eqb"][:, 0:256].bitcast(mybir.dt.int32)
        L["bs8"] = L["eqb"][:, 512:768]
        L["qb"] = L["sfc"][:, 0:256]
        L["rb2"] = L["sfc"][:, 256:512]
        L["srow"] = L["sfc"][:, 512:768]
        L["prow"] = L["sfc"][:, 768:1024]
        L["windexi"] = L["idxb"][:, 0:NGRAN].bitcast(mybir.dt.int32)
        L["xtm_ln"] = [cbf(43008 + i * 1024, [128, 1024]) for i in range(2)]
        L["acc"] = cf32(0, [128, ST_TT * 1024])
        L["xnst"] = [cbf(16384 + i * 8192, [128, ST_TT, FC, 128]) for i in range(2)]
        L["hbuf"] = [cbf(32768 + i * 4096, [128, 4096]) for i in range(2)]
        L["wg_bf"] = [cbf(40960 + i * 8192, [128, 4096]) for i in range(2)]
        L["wu_bf"] = [cbf(45056 + i * 8192, [128, 4096]) for i in range(2)]
        L["wgu_bf"] = [cbf(40960 + i * 8192, [128, 8192]) for i in range(2)]
        L["wd_bf"] = [cbf(57344 + i * 4096, [128, 4096]) for i in range(2)]
        L["sG"] = [cf32(65536 + i * 1024, [128, 512]) for i in range(2)]
        L["R2"] = [cf32(69632 + i * 2048, [128, 1024]) for i in range(2)]
        L["xnT_g"] = [cbf(i * 2048, [128, FC * GTOK]) for i in range(2)]
        L["hbuf2"] = [cbf(4096 + i * 1024, [128, 4 * GTOK]) for i in range(2)]
        L["ystage"] = [cbf(6144 + i * 1024, [128, 1024]) for i in range(2)]
        L["bstage"] = [cf32(11264 + i * 2048, [128, 1024]) for i in range(2)]
        L["xs"] = [cbf(16384 + i * 1024, [128, 1024]) for i in range(4)]
        L["yb_sb"] = [cbf(32768 + i * 8192, [128, 8192]) for i in range(4)]
        L["outb"] = [cf32(65536 + i * 2048, [128, 1024]) for i in range(2)]
        L["zini"] = cbf(47104, [128, 8192])

        psa = stk.enter_context(nc.psum_tensor("psa", [128, 4096], F32))
        L["psa_raw"] = psa
        L["pstr"] = [psa[:, 0:128], psa[:, 512:640]]
        L["psr"] = psa[:, 1024:1088]
        L["psS"] = psa[:, 0:2048]
        L["psq"] = psa[:, 2048:2560]
        L["psk"] = psa[:, 2560:3072]
        L["psVC"] = psa[:, 3072:3584]
        L["psT"] = psa[:, 3584:4096].bitcast(BF16)[:, 0:512]
        L["psG"] = [psa[:, 0:512], psa[:, 512:1024]]
        L["psU"] = [psa[:, 1024:1536], psa[:, 1536:2048]]
        L["psY"] = [psa[:, 2048:3072], psa[:, 3072:4096]]
        L["psYg"] = [psa[:, 2048:2560], psa[:, 2560:3072]]
        L["psTb"] = [psa[:, 3072:3584].bitcast(BF16),
                     psa[:, 3584:4096].bitcast(BF16)]

        for s in SEMS:
            kb.sem(stk, nc, s)

        emit_phase0(kb, L)
        emit_ln(kb, L)
        if KSTAGE != "ln":
            sparse = (KSTAGE == "full" and KMOE == "sparse")
            if sparse:
                L["hz"] = {}
                L["prep_gen"] = gen_sparse_prep(kb, L, L["hz"])
            emit_attention(kb, L)
            if sparse:
                g = L.get("prep_gen")
                while g is not None:
                    try:
                        next(g)
                    except StopIteration:
                        g = None
                        L["prep_gen"] = None
            else:
                emit_routing(kb, L)
            if KSTAGE == "route":
                emit_route_debug(kb, L)
            if KSTAGE == "full":
                kb.barrier()
                if sparse:
                    emit_moe_sparse(kb, L)
                else:
                    emit_moe(kb, L)
        for s_ in ("st", "sr0", "sr1", "sx"):
            kb.W("sync", s_, kb.cnt[s_])
            kb.W("gpsimd", s_, kb.cnt[s_])

        with nc.Block() as block:
            @block.sync
            def _(eng):
                for f in kb.ops["sync"]:
                    f(eng)

            @block.scalar
            def _(eng):
                for f in kb.ops["scalar"]:
                    f(eng)

            @block.vector
            def _(eng):
                for f in kb.ops["vector"]:
                    f(eng)

            @block.tensor
            def _(eng):
                for f in kb.ops["tensor"]:
                    f(eng)

            @block.gpsimd
            def _(eng):
                for f in kb.ops["gpsimd"]:
                    f(eng)

    return nc


def emit_phase0(kb, L):
    W, O, C = kb.W, kb.O, kb.cnt
    for dst, src in ((L["rw_sb"], L["d_rw"][:, :, :]), (L["bqk_sb"], L["d_bqk"][:, :]),
                     (L["vb_sb"], L["d_vb"][:, :]), (L["ob_sb"], L["d_ob"][:, :]),
                     (L["eb_sb"], L["d_eb"][:, :]), (L["i64_sb"], L["d_i64"][:, :]),
                     (L["i8_sb"], L["d_i8"][:, :]), (L["idf_sb"], L["d_idf"][:, :]),
                     (L["idb_sb"], L["d_idb"][:, :]), (L["ntri_sb"], L["d_ntri"][:, :])):
        O("sync", "dma_start", out=dst[:, :], in_=src, inc=("lc", 16))
    if KMOE == "sparse" and KSTAGE == "full":
        for dst, src in ((L["ustr_sb"], L["d_ustr"][:, :]), (L["onesm_sb"], L["d_onesm"][:, :]),
                         (L["ei_sb"], L["d_ei"][:, :]), (L["gio_sb"], L["d_gio"][:, :]),
                         (L["iop_sb"], L["d_iop"][:, :])):
            O("sync", "dma_start", out=dst[:, :], in_=src, inc=("lc", 16))
    L["lc_all"] = kb.mark("lc")
    if KSTAGE != "ln":
        O("gpsimd", "dma_start",
          out=_view(L["wo_bf"], [128, FC, H]), in_=L["d_wo"][:, :, :], inc=("gwo", 16))
        L["gwo_all"] = kb.mark("gwo")
    if KMOE == "sparse" and KSTAGE == "full":
        chunks = [(L["d_wg"], L["d_wgub"][:, 0:4096]),
                  (L["d_wu"], L["d_wgub"][:, 4096:8192]),
                  (L["d_wd"], L["d_wdb"][:, :])]
        pc_jobs = []
        for src, dst in chunks:
            s2 = src[:, :, :, :].rearrange("u p a b -> (u p) (a b)")
            for e2 in range(32):
                pc_jobs.append((dst[(2 * e2) * 128:(2 * e2 + 2) * 128, :],
                                s2[(2 + 2 * e2) * 128:(4 + 2 * e2) * 128, :]))
        pc_jobs = [pc_jobs[j + t * 32] for j in range(32) for t in range(3)]

        def pc_emit(n):
            for _ in range(n):
                if pc_jobs:
                    dst, src = pc_jobs.pop(0)
                    O("gpsimd", "dma_start", out=dst, in_=src, inc=("pcw", 16))
        L["pc_emit"] = pc_emit
        L["pcw_all"] = ("pcw", 96 * 16)
        gd01 = []
        for pi2 in range(2):
            g = "gw%d" % (pi2 % 2)
            for dd, bb in ((L["d_wq"], L["wq_bf"]), (L["d_wk"], L["wk_bf"]),
                           (L["d_wv"], L["wv_bf"])):
                O("gpsimd", "dma_start",
                  out=_view(bb[pi2 % 2], [128, FC, 128]), in_=dd[pi2, :, :, :],
                  inc=(g, 16))
            gd01.append(kb.mark(g))
        L["gd_w01"] = gd01
        pc_emit(8)
        emit_sparse_init(kb, L)


def emit_ln(kb, L):
    W, O, C = kb.W, kb.O, kb.cnt
    d_x, d_xnscr, d_y = L["d_x"], L["d_xnscr"], L["d_y"]
    x_tile, xn_f32, xnT, R_sb = L["x_tile"], L["xn_f32"], L["xnT"], L["R_dbg"]
    xn_sb, xn_stage = L["xn_sb"], L["xn_stage"]
    pstr, psr = L["pstr"], L["psr"]
    rw_sb, scores = L["rw_sb"], L["scores"]

    O("sync", "dma_start", out=x_tile[0], in_=d_x[0:128, :], inc=("lx0", 16))
    ld_x = [kb.mark("lx0")]
    cs_tile_end = []
    vs_tile_end = []
    ts_tile_end = []
    st_dbg = []
    sx_marks = []
    sy_marks = []
    vs_xtm = []

    ln_nt = KNT if (KNT and KSTAGE == "ln") else NT
    for tt in range(ln_nt):
        par = tt % 2
        seq = tt // NTS
        tloc = tt % NTS
        if tt + 1 < NT:
            if tt >= 1:
                W("sync", "cs", cs_tile_end[tt - 1])
                W("sync", "vs", vs_tile_end[tt - 1])
            O("sync", "dma_start", out=x_tile[(tt + 1) % 2],
              in_=d_x[(tt + 1) * 128:(tt + 2) * 128, :], inc=("lx%d" % ((tt + 1) % 2), 16))
            ld_x.append(kb.mark("lx%d" % ((tt + 1) % 2)))

        nsv = L["ns2"][:, par:par + 1]
        nmv = L["nm2"][:, par:par + 1]
        mmv = L["mm2"][:, par:par + 1]
        vav = L["var2"][:, par:par + 1]
        sdv = L["sd2"][:, par:par + 1]
        rrv = L["rr2"][:, par:par + 1]
        sqv = L["sq2"][:, par:par + 1]

        W("vector", ld_x[tt])
        O("vector", "reduce_sum", nsv, x_tile[par], axis=AX.X, negate=True, inc=("vs", 1))
        W("vector", "vs")
        O("vector", "tensor_scalar_mul", nmv, nsv, 1.0 / H, inc=("vs", 1))
        W("vector", "vs")
        O("vector", "tensor_scalar", mmv, nmv, nmv, -1e-5, ALU.mult, ALU.add,
          inc=("vs", 1))
        vs_m2 = C["vs"]
        W("scalar", ld_x[tt])
        if tt >= 1:
            W("scalar", "ts", ts_tile_end[tt - 1])
            if KSTAGE == "ln":
                W("scalar", "vs", vs_tile_end[tt - 1] + 1)
            if KMOE == "sparse" and KSTAGE == "full":
                W("scalar", "vs", vs_xtm[tt - 1])
        O("scalar", "activation", xn_f32, x_tile[par], AF.Square,
          accum_out=sqv, inc=("cs", 1))
        W("vector", "cs")
        W("vector", "vs", vs_m2)
        O("vector", "scalar_tensor_tensor", vav, sqv, 1.0 / H, mmv,
          ALU.mult, ALU.subtract, inc=("vs", 1))
        W("scalar", "vs")
        O("scalar", "activation", sdv, vav, AF.Sqrt, inc=("cs", 1))
        W("vector", "cs")
        O("vector", "reciprocal", rrv, sdv, inc=("vs", 1))
        W("vector", "vs")
        O("vector", "tensor_scalar", xn_f32, x_tile[par], nmv, rrv,
          ALU.add, ALU.mult, inc=("vs", 1))
        vs_xn = C["vs"]
        if KMOE == "sparse" and KSTAGE == "full":
            if tt >= 2:
                W("vector", sy_marks[tt - 2])
            W("vector", "vs", vs_xn)
            O("vector", "tensor_copy", L["xtm_ln"][par], xn_f32, inc=("vs", 1))
            vs_xtm.append(C["vs"])
            W("sync", "vs", C["vs"])
            O("sync", "dma_start", out=L["d_xntm"][tt * 128:(tt + 1) * 128, :],
              in_=L["xtm_ln"][par], inc=("sy%d" % par, 16))
            sy_marks.append(kb.mark("sy%d" % par))

        cs_copy = []
        vs_copy = []
        if tt == 0:
            W("tensor", L["lc_all"])
        if seq == 1 and tt >= NTS + 2:
            W("vector", "sx", sx_marks[tt - 1][1])
        for c in range(FC):
            W("tensor", "vs", vs_xn)
            if c >= 2:
                W("tensor", "vs", cs_copy[c - 2])
            elif tt >= 1:
                W("tensor", "cs", cs_tile_end[tt - 1])
                W("tensor", "vs", vs_tile_end[tt - 1])
            O("tensor", "transpose", pstr[c % 2], xn_f32[:, c * 128:(c + 1) * 128],
              L["idf_sb"][:, :], inc=("ts", 1))
            ts_tr = C["ts"]
            W("vector", "ts", ts_tr)
            O("vector", "tensor_copy", xnT[:, c * 128:(c + 1) * 128], pstr[c % 2],
              inc=("vs", 1))
            cs_copy.append(C["vs"])
            W("vector", "vs", cs_copy[-1])
            dst = xn_sb[:, tloc, c, :] if seq == 0 else xn_stage[par][:, c, :]
            O("vector", "tensor_copy", dst, xnT[:, c * 128:(c + 1) * 128], inc=("vs", 1))
            vs_copy.append(C["vs"])
        W("tensor", "vs", cs_copy[-1])
        if tt >= 1:
            W("tensor", "cs", cs_tile_end[tt - 1])
        W("tensor", L["lc_all"])
        for c in range(FC):
            O("tensor", "matmul", psr, xnT[:, c * 128:(c + 1) * 128],
              rw_sb[:, c * 64:(c + 1) * 64], start=(c == 0), stop=(c == FC - 1),
              inc=(("ts", 1) if c == FC - 1 else None))
        ts_tile_end.append(C["ts"])
        W("scalar", "ts")
        O("scalar", "copy", scores[:, tt * 64:(tt + 1) * 64], psr, inc=("cs", 1))
        cs_tile_end.append(C["cs"])
        W("sync", "vs", vs_copy[-1])
        src = xn_sb[:, tloc, :, :] if seq == 0 else xn_stage[par][:, :, :]
        O("sync", "dma_start", out=d_xnscr[:, tt, :, :], in_=src, inc=("sx", 16))
        sx_marks.append(kb.mark("sx"))
        vs_tile_end.append(C["vs"])

        if KSTAGE == "ln":
            if tt >= 2:
                W("vector", st_dbg[tt - 2])
            O("vector", "tensor_copy", R_sb[par], xn_f32, inc=("vs", 1))
            W("sync", "vs")
            O("sync", "dma_start", out=d_y[tt * 128:(tt + 1) * 128, :],
              in_=R_sb[par], inc=("sr%d" % par, 16))
            st_dbg.append(kb.mark("sr%d" % par))
        else:
            st_dbg.append(None)

    L["ln_cs"] = C["cs"]
    L["ln_vs"] = C["vs"]
    L["ln_ts"] = C["ts"]
    L["sx_all"] = kb.mark("sx")


def emit_attention(kb, L):
    W, O, C = kb.W, kb.O, kb.cnt
    d_x, d_wq, d_wk, d_wv = L["d_x"], L["d_wq"], L["d_wk"], L["d_wv"]
    d_xnscr, d_rscr, d_y = L["d_xnscr"], L["d_rscr"], L["d_y"]
    xn_sb, wq_bf, wk_bf, wv_bf, wo_bf = (L["xn_sb"], L["wq_bf"], L["wk_bf"],
                                         L["wv_bf"], L["wo_bf"])
    qf, kf, vt, Pb2, PT2, comb = L["qf"], L["kf"], L["vt"], L["Pb2"], L["PT2"], L["comb"]
    x_tile = L["x_tile"]
    bqk_sb, vb_sb, ob_sb, ntri_sb, idb_sb = (L["bqk_sb"], L["vb_sb"], L["ob_sb"],
                                             L["ntri_sb"], L["idb_sb"])
    psS, psq, psk, psVC, psT = L["psS"], L["psq"], L["psk"], L["psVC"], L["psT"]
    nmx2, rs2, rc2 = L["nmx2"], L["rs2"], L["rc2"]

    pairs = [(s, hp) for s in range(2) for hp in range(NHP)]
    gd_w = {}
    W("vector", L["lc_all"])
    W("scalar", L["sx_all"])
    if KSTAGE == "full" and KMOE == "sparse":
        W("scalar", ("sy0", kb.cnt["sy0"]))
        W("scalar", ("sy1", kb.cnt["sy1"]))
        W("scalar", L["sz_all"])

    def load_w(pi):
        s2, hp2 = pairs[pi]
        g = "gw%d" % (hp2 % 2)
        for dd, bb in ((d_wq, wq_bf), (d_wk, wk_bf), (d_wv, wv_bf)):
            O("gpsimd", "dma_start",
              out=_view(bb[hp2 % 2], [128, FC, 128]), in_=dd[hp2, :, :, :],
              inc=(g, 16))
        gd_w[pi] = kb.mark(g)

    if "gd_w01" in L:
        gd_w[0], gd_w[1] = L["gd_w01"]
    else:
        load_w(0)
        load_w(1)

    ts_qkv_done = {}
    ts_S_done = {}
    ts_pv_done = {}
    cs_last_exp = 0
    vs_last_psS = L["ln_vs"]
    cs_last_pt = 0
    cs_last_psc = 0
    prev_vs_qk = 0
    vs_R = []
    sr_R = []

    for pi, (s, hp) in enumerate(pairs):
        par = hp % 2
        if s == 1 and hp == 0:
            W("sync", "ts", ts_qkv_done[(0, NHP - 1)])
            W("sync", L["sx_all"])
            O("sync", "dma_start", out=xn_sb[:, :, :, :],
              in_=d_xnscr[:, NTS:NT, :, :], inc=("lxr", 16))
            L["lxr_all"] = kb.mark("lxr")

        W("tensor", gd_w[pi])
        if s == 0:
            W("tensor", "vs", L["ln_vs"])
            W("tensor", "cs", L["ln_cs"])
        else:
            W("tensor", L["lxr_all"])
        key2 = pairs[pi - 2] if pi >= 2 else None
        for tc in range(4):
            if tc > 0:
                W("tensor", "vs", C["vs"])
            else:
                W("tensor", "vs", max(prev_vs_qk, L.get("vs_hook", 0)))
            for c in range(FC):
                O("tensor", "matmul", psq, wq_bf[par][:, c * 128:(c + 1) * 128],
                  xn_sb[:, tc * 4:(tc + 1) * 4, c, :],
                  start=(c == 0), stop=(c == FC - 1),
                  inc=(("ts", 1) if c == FC - 1 else None))
            ts_q = C["ts"]
            for c in range(FC):
                O("tensor", "matmul", psk, wk_bf[par][:, c * 128:(c + 1) * 128],
                  xn_sb[:, tc * 4:(tc + 1) * 4, c, :],
                  start=(c == 0), stop=(c == FC - 1),
                  inc=(("ts", 1) if c == FC - 1 else None))
            ts_k = C["ts"]
            W("vector", "ts", ts_q)
            if key2 is not None and tc == 0:
                W("vector", "ts", ts_S_done[key2])
            O("vector", "tensor_scalar", qf[par][:, tc * 512:(tc + 1) * 512],
              psq, bqk_sb[:, hp:hp + 1], 0.125, ALU.add, ALU.mult, inc=("vs", 1))
            W("vector", "ts", ts_k)
            O("vector", "tensor_scalar_add", kf[par][:, tc * 512:(tc + 1) * 512],
              psk, bqk_sb[:, 8 + hp:8 + hp + 1], inc=("vs", 1))
        prev_vs_qk = C["vs"]
        W("tensor", "cs", cs_last_psc)
        vb4 = vb_sb[:, hp * 128:hp * 128 + 128] \
            .rearrange("p (o f) -> p o f", o=1).broadcast_to((128, 4, 128))
        vs_v_batch = []
        for ttv in range(NTS):
            slot = ttv % 4
            if slot == 0 and ttv >= 4:
                W("tensor", "vs", vs_v_batch[-1])
            for c in range(FC):
                O("tensor", "matmul", psVC[:, slot * 128:(slot + 1) * 128],
                  xn_sb[:, ttv, c, :], wv_bf[par][:, c * 128:(c + 1) * 128],
                  start=(c == 0), stop=(c == FC - 1),
                  inc=(("ts", 1) if c == FC - 1 else None))
            if slot == 3:
                ts_v = C["ts"]
                W("vector", "ts", ts_v)
                if key2 is not None and ttv == 3:
                    W("vector", "ts", ts_pv_done[key2])
                O("vector", "tensor_tensor",
                  _view(vt[par][:, (ttv - 3) * 128:(ttv + 1) * 128], [128, 4, 128]),
                  _view(psVC[:, 0:512], [128, 4, 128]),
                  vb4, ALU.add, inc=("vs", 1))
                vs_v_batch.append(C["vs"])
        ts_qkv_done[(s, hp)] = C["ts"]
        vs_v_last = C["vs"]
        if pi + 2 < len(pairs):
            W("gpsimd", "ts", ts_qkv_done[(s, hp)])
            load_w(pi + 2)
        if "pc_emit" in L:
            L["pc_emit"](6)

        vs_qk = prev_vs_qk
        pend = None

        def emit_ptpv(j0, h0, vsn0, pp0):
            nonlocal cs_last_pt, cs_last_psc
            cs_pt = []
            for ki in range(j0 + 1):
                g = ki % 4
                if ki == 0:
                    W("tensor", "vs", vsn0)
                    W("tensor", "cs", cs_last_pt)
                if g == 0 and ki >= 4:
                    W("tensor", "cs", cs_pt[-1])
                O("tensor", "transpose", psT[:, g * 128:(g + 1) * 128],
                  Pb2[pp0][:, ki * 128:(ki + 1) * 128], idb_sb[:, :],
                  inc=(("ts", 1) if (g == 3 or ki == j0) else None))
                if g == 3 or ki == j0:
                    k0 = (ki // 4) * 4
                    nn = (ki - k0 + 1) * 128
                    W("scalar", "ts", C["ts"])
                    O("scalar", "copy", PT2[pp0][:, k0 * 128:k0 * 128 + nn],
                      psT[:, 0:nn], inc=("cs", 1))
                    cs_pt.append(C["cs"])
            cs_last_pt = cs_pt[-1]
            if j0 == 0 and h0 == 0:
                W("tensor", "vs", vs_v_last)
            if h0 == 0:
                W("tensor", "cs", cs_last_psc)
            for ki in range(j0 + 1):
                if ki % 4 == 0:
                    W("tensor", "cs", cs_pt[ki // 4])
                O("tensor", "matmul", psVC[64 * h0:64 * h0 + 64, 0:128],
                  vt[par][:, ki * 128 + 64 * h0:ki * 128 + 64 * h0 + 64],
                  PT2[pp0][:, ki * 128:(ki + 1) * 128],
                  start=(ki == 0), stop=(ki == j0),
                  inc=(("ts", 1) if ki == j0 else None))
            if h0 == 1:
                W("scalar", "ts", C["ts"])
                O("scalar", "copy", comb[:, hp, j0, :], psVC[:, 0:128], inc=("cs", 1))
                cs_last_psc = C["cs"]

        mxc = L["mxc"]

        def emit_rcscale(pj, ph, pmpar):
            W("vector", "cs", cs_last_exp)
            O("vector", "reciprocal", rc2[:, pmpar:pmpar + 1],
              rs2[:, pmpar:pmpar + 1], inc=("vs", 1))
            W("vector", "vs")
            O("vector", "tensor_scalar_mul", Pb2[pmpar][:, 0:(pj + 1) * 128],
              Pb2[pmpar][:, 0:(pj + 1) * 128], rc2[:, pmpar:pmpar + 1],
              inc=("vs", 1))
            return C["vs"]

        cs_exp_par = [cs_last_exp, cs_last_exp]
        vs_psS_par = [vs_last_psS, vs_last_psS]
        for j in range(NTS):
            for h in range(2):
                mpar = (j * 2 + h) % 2
                kmax = (j + 1) * 128
                base = mpar * 1024 if kmax <= 1024 else 0
                W("tensor", "vs", vs_qk)
                if kmax <= 1024:
                    W("tensor", "cs", cs_exp_par[mpar])
                    W("tensor", "vs", vs_psS_par[mpar])
                else:
                    W("tensor", "cs", cs_last_exp)
                    W("tensor", "vs", vs_last_psS)
                nch = (kmax + 511) // 512
                for ci in range(nch):
                    kc = ci * 512
                    n = min(512, kmax - kc)
                    O("tensor", "matmul", psS[:, base + kc:base + kc + n],
                      qf[par][64 * h:64 * h + 64, j * 128:(j + 1) * 128],
                      kf[par][64 * h:64 * h + 64, kc:kc + n],
                      start=True, stop=True, inc=("ts", 1))
                    W("vector", "ts", C["ts"])
                    O("vector", "reduce_max",
                      mxc[:, mpar * 4 + ci:mpar * 4 + ci + 1],
                      psS[:, base + kc:base + kc + n], axis=AX.X, inc=("vs", 1))
                W("vector", "vs")
                O("vector", "tensor_tensor",
                  psS[:, base + j * 128:base + j * 128 + 128],
                  psS[:, base + j * 128:base + j * 128 + 128],
                  ntri_sb[:, :], ALU.add, inc=("vs", 1))
                vs_psS_par[mpar] = C["vs"]
                vs_last_psS = C["vs"]
                W("vector", "vs")
                O("vector", "reduce_max", nmx2[:, mpar:mpar + 1],
                  mxc[:, mpar * 4:mpar * 4 + nch], axis=AX.X, negate=True,
                  inc=("vs", 1))
                vs_comb = C["vs"]
                if pend is not None:
                    vs_norm_prev = emit_rcscale(*pend)
                W("scalar", "vs", vs_comb)
                O("scalar", "activation", Pb2[mpar][:, 0:kmax],
                  psS[:, base:base + kmax], AF.Exp,
                  bias=nmx2[:, mpar:mpar + 1], accum_out=rs2[:, mpar:mpar + 1],
                  inc=("cs", 1))
                cs_exp_par[mpar] = C["cs"]
                cs_last_exp = C["cs"]
                if pend is not None:
                    pj, ph, pmpar = pend
                    emit_ptpv(pj, ph, vs_norm_prev, pmpar)
                pend = (j, h, mpar)
        vs_norm_prev = emit_rcscale(*pend)
        emit_ptpv(pend[0], pend[1], vs_norm_prev, pend[2])
        ts_S_done[(s, hp)] = C["ts"]
        ts_pv_done[(s, hp)] = C["ts"]
        gen = L.get("prep_gen")
        if gen is not None:
            hz = L["hz"]
            hz["prev_vs_qk"] = prev_vs_qk
            hz["cs_last_psc"] = cs_last_psc
            hz["cs_last_pt"] = cs_last_pt
            hz["cs_last_exp"] = cs_last_exp
            hz["vs_last_psS"] = vs_last_psS
            try:
                next(gen)
            except StopIteration:
                L["prep_gen"] = None
            L["vs_hook"] = C["vs"]
            cs_last_psc = max(cs_last_psc, hz.get("cs_last_psc", 0))
            vs_last_psS = max(vs_last_psS, hz.get("vs_last_psS", 0))

        if hp == NHP - 1:
            cs_comb = C["cs"]
            for tt in range(NTS):
                par2 = tt % 2
                nR = len(vs_R)
                if nR >= 2:
                    W("sync", "vs", vs_R[nR - 2])
                    W("sync", sr_R[nR - 2])
                else:
                    W("sync", "cs", L["ln_cs"])
                    W("sync", "vs", L["ln_vs"])
                O("sync", "dma_start", out=x_tile[par2],
                  in_=d_x[(s * NTS + tt) * 128:(s * NTS + tt + 1) * 128, :],
                  inc=("lx%d" % par2, 16))
                ld_xt = kb.mark("lx%d" % par2)
                psO = psS[:, (tt % 2) * 1024:(tt % 2) * 1024 + 1024]
                W("tensor", "cs", cs_comb)
                W("tensor", "cs", cs_last_exp)
                W("tensor", "vs", vs_last_psS)
                W("tensor", L["gwo_all"])
                if nR >= 2:
                    W("tensor", "vs", vs_R[nR - 2])
                for half in range(2):
                    for c in range(FC):
                        O("tensor", "matmul", psO[:, half * 512:half * 512 + 512],
                          comb[:, c, tt, :],
                          wo_bf[:, c * 1024 + half * 512:c * 1024 + half * 512 + 512],
                          start=(c == 0), stop=(c == FC - 1),
                          inc=(("ts", 1) if (c == FC - 1 and half == 1) else None))
                ts_o = C["ts"]
                W("vector", "ts", ts_o)
                W("vector", ld_xt)
                if nR >= 2:
                    W("vector", sr_R[nR - 2])
                O("vector", "tensor_tensor", psO, psO, ob_sb[:, :], ALU.add,
                  inc=("vs", 1))
                W("vector", "vs")
                O("vector", "tensor_tensor", x_tile[par2], psO, x_tile[par2],
                  ALU.add, inc=("vs", 1))
                vs_R.append(C["vs"])
                W("sync", "vs", C["vs"])
                O("sync", "dma_start", out=d_rscr[s * NTS + tt, :, :], in_=x_tile[par2],
                  inc=("sr%d" % par2, 16))
                if KSTAGE == "attn":
                    O("sync", "dma_start",
                      out=d_y[(s * NTS + tt) * 128:(s * NTS + tt + 1) * 128, :],
                      in_=x_tile[par2], inc=("sr%d" % par2, 16))
                sr_R.append(kb.mark("sr%d" % par2))
            if s == 0:
                vs_last_psS = max(vs_last_psS, vs_R[-1])
    L["sr_all"] = (kb.mark("sr0"), kb.mark("sr1"))


def emit_routing(kb, L):
    W, O, C = kb.W, kb.O, kb.cnt
    scores, coef = L["scores"], L["coef"]
    sfc, eqb, idxb = L["sfc"], L["eqb"], L["idxb"]
    gm1, gcnt, gm2r, gge, gdd, gsb, gmk, grA, grB = (
        L["gm1"], L["gcnt"], L["gm2r"], L["gge"], L["gdd"], L["gsb"], L["gmk"],
        L["grA"], L["grB"])
    m16, m16b, ws16 = L["m16"], L["m16b"], L["ws16"]
    eb_sb, i64_sb, i8_sb = L["eb_sb"], L["i64_sb"], L["i8_sb"]

    def _ap2(ap):
        return ap if isinstance(ap, bass.AP) else ap[:, :]

    def hv3(ap, a):
        return _ap2(ap).rearrange("p (a b) -> p a b", a=a)

    def hv4(ap, a, b):
        return _ap2(ap).rearrange("p (a b c) -> p a b c", a=a, b=b)

    def VO(fname, *args, **kw):
        W("vector", "vs")
        O("vector", fname, *args, inc=("vs", 1), **kw)

    W("vector", "cs", L["ln_cs"])
    W("vector", L["lc_all"])
    W("scalar", "cs", L["ln_cs"])
    W("scalar", "vs", L["ln_vs"])
    O("scalar", "activation", scores[:, 0:1024], scores[:, 0:1024], AF.Sigmoid,
      inc=("cs", 1))
    O("scalar", "activation", scores[:, 1024:2048], scores[:, 1024:2048],
      AF.Sigmoid, inc=("cs", 1))
    W("vector", "cs")
    for half in range(2):
        sc = scores[:, half * 1024:half * 1024 + 1024]
        cf = coef[:, half * 1024:half * 1024 + 1024]
        ebb = hv3(eb_sb, 1).broadcast_to((128, 16, 64))
        VO("tensor_tensor", hv3(sfc, 16), hv3(sc, 16), ebb, ALU.add)
        VO("tensor_reduce", hv4(gm1, 16, 8), hv4(sfc, 16, 8), AX.X, ALU.max)
        g1b = hv4(gm1, 16, 8).broadcast_to((128, 16, 8, 8))
        VO("tensor_tensor", hv4(eqb, 16, 8), hv4(sfc, 16, 8), g1b, ALU.is_equal)
        VO("tensor_reduce", hv4(gcnt, 16, 8), hv4(eqb, 16, 8), AX.X, ALU.add)
        VO("tensor_scalar_mul", idxb[:, :], eqb[:, :], NEG)
        VO("tensor_add", idxb[:, :], idxb[:, :], sfc[:, :])
        VO("tensor_reduce", hv4(gm2r, 16, 8), hv4(idxb, 16, 8), AX.X, ALU.max)
        VO("tensor_single_scalar", gcnt[:, :], gcnt[:, :], 1.5, ALU.is_ge)
        VO("tensor_sub", gdd[:, :], gm1[:, :], gm2r[:, :])
        VO("tensor_mul", gge[:, :], gcnt[:, :], gdd[:, :])
        VO("tensor_add", gm2r[:, :], gm2r[:, :], gge[:, :])
        VO("tensor_add", gsb[:, :], gm1[:, :], gm2r[:, :])
        VO("memset", gmk[:, :], 0.0)
        i8b = hv3(i8_sb, 1).broadcast_to((128, 16, 8))
        for it in range(4):
            VO("tensor_reduce", hv3(m16, 16), hv3(gsb, 16), AX.X, ALU.max)
            mb = hv3(m16, 16).broadcast_to((128, 16, 8))
            VO("tensor_tensor", hv3(grA, 16), hv3(gsb, 16), mb, ALU.is_equal)
            VO("tensor_tensor", hv3(grB, 16), hv3(grA, 16), i8b, ALU.mult)
            VO("tensor_scalar_add", grB[:, :], grB[:, :], 127.0)
            VO("tensor_reduce", hv3(m16b, 16), hv3(grB, 16), AX.X, ALU.min)
            mnb = hv3(m16b, 16).broadcast_to((128, 16, 8))
            VO("tensor_tensor", hv3(grA, 16), hv3(grB, 16), mnb, ALU.is_equal)
            VO("tensor_add", gmk[:, :], gmk[:, :], grA[:, :])
            VO("tensor_scalar_mul", grA[:, :], grA[:, :], NEG)
            VO("tensor_add", gsb[:, :], gsb[:, :], grA[:, :])
        gmb = hv4(gmk, 16, 8).broadcast_to((128, 16, 8, 8))
        VO("tensor_tensor", hv4(sfc, 16, 8), hv4(sfc, 16, 8), gmb, ALU.mult)
        VO("memset", cf, 0.0)
        VO("memset", ws16[:, :], 0.0)
        i64b = hv3(i64_sb, 1).broadcast_to((128, 16, 64))
        for it in range(8):
            VO("tensor_reduce", hv3(m16, 16), hv3(sfc, 16), AX.X, ALU.max)
            mb = hv3(m16, 16).broadcast_to((128, 16, 64))
            VO("tensor_tensor", hv3(eqb, 16), hv3(sfc, 16), mb, ALU.is_equal)
            VO("tensor_tensor", hv3(idxb, 16), hv3(eqb, 16), i64b, ALU.mult)
            VO("tensor_scalar_add", idxb[:, :], idxb[:, :], 127.0)
            VO("tensor_reduce", hv3(m16b, 16), hv3(idxb, 16), AX.X, ALU.min)
            mnb = hv3(m16b, 16).broadcast_to((128, 16, 64))
            VO("tensor_tensor", hv3(eqb, 16), hv3(idxb, 16), mnb, ALU.is_equal)
            VO("tensor_tensor", hv3(idxb, 16), hv3(sc, 16), hv3(eqb, 16), ALU.mult)
            VO("tensor_add", cf, cf, idxb[:, :])
            VO("tensor_reduce", hv3(m16, 16), hv3(idxb, 16), AX.X, ALU.add)
            VO("tensor_add", ws16[:, :], ws16[:, :], m16[:, :])
            VO("tensor_scalar_mul", eqb[:, :], eqb[:, :], NEG)
            VO("tensor_add", sfc[:, :], sfc[:, :], eqb[:, :])
        VO("tensor_scalar_add", m16[:, :], ws16[:, :], 1e-20)
        VO("reciprocal", m16b[:, :], m16[:, :])
        VO("tensor_scalar_mul", m16[:, :], m16b[:, :], 2.5)
        mcb = hv3(m16, 16).broadcast_to((128, 16, 64))
        VO("tensor_tensor", hv3(cf, 16), hv3(cf, 16), mcb, ALU.mult)
    L["vs_routing"] = C["vs"]


def emit_sparse_init(kb, L):
    W, O, C = kb.W, kb.O, kb.cnt
    O("vector", "memset", L["zini"], 0.0, inc=("vs", 1))
    vs_z = C["vs"]
    W("scalar", "vs", vs_z)
    for b in range(NSLAB * H // (128 * 8192)):
        O("scalar", "dma_start",
          out=L["d_slab"][b * 1024:(b + 1) * 1024, :]
          .rearrange("a b -> (a b)").rearrange("(p f) -> p f", p=128),
          in_=L["zini"], inc=("sz", 16))
    L["sz_all"] = kb.mark("sz")


def gen_sparse_prep(kb, L, hz):
    W, O, C = kb.W, kb.O, kb.cnt
    scores, coef = L["scores"], L["coef"]
    sfc, eqb, idxb = L["sfc"], L["eqb"], L["idxb"]
    gm1, gcnt, gm2r, gge, gdd, gsb, gmk, grA, grB = (
        L["gm1"], L["gcnt"], L["gm2r"], L["gge"], L["gdd"], L["gsb"], L["gmk"],
        L["grA"], L["grB"])
    m16, m16b, ws16 = L["m16"], L["m16b"], L["ws16"]
    eb_sb, i64_sb, i8_sb = L["eb_sb"], L["i64_sb"], L["i8_sb"]
    topke, topkw = L["topke"], L["topkw"]
    rankb = L["rankb2"][:, :]

    def _ap2(ap):
        return ap if isinstance(ap, bass.AP) else ap[:, :]

    def hv3(ap, a):
        return _ap2(ap).rearrange("p (a b) -> p a b", a=a)

    def hv4(ap, a, b):
        return _ap2(ap).rearrange("p (a b c) -> p a b c", a=a, b=b)

    def VO(fname, *args, **kw):
        W("vector", "vs")
        O("vector", fname, *args, inc=("vs", 1), **kw)

    W("vector", "cs", L["ln_cs"])
    W("vector", L["lc_all"])
    W("scalar", "cs", L["ln_cs"])
    W("scalar", "vs", L["ln_vs"])
    O("scalar", "activation", scores[:, 0:1024], scores[:, 0:1024], AF.Sigmoid,
      inc=("cs", 1))
    O("scalar", "activation", scores[:, 1024:2048], scores[:, 1024:2048],
      AF.Sigmoid, inc=("cs", 1))
    W("vector", "cs")
    for half in range(2):
        sc = scores[:, half * 1024:half * 1024 + 1024]
        cf = coef[:, half * 1024:half * 1024 + 1024]
        tke = hv4(topke, 2, 16)[:, half, :, :]
        tkw = hv4(topkw, 2, 16)[:, half, :, :]
        ebb = hv3(eb_sb, 1).broadcast_to((128, 16, 64))
        VO("tensor_tensor", hv3(sfc, 16), hv3(sc, 16), ebb, ALU.add)
        VO("tensor_reduce", hv4(gm1, 16, 8), hv4(sfc, 16, 8), AX.X, ALU.max)
        g1b = hv4(gm1, 16, 8).broadcast_to((128, 16, 8, 8))
        VO("tensor_tensor", hv4(eqb, 16, 8), hv4(sfc, 16, 8), g1b, ALU.is_equal)
        VO("tensor_reduce", hv4(gcnt, 16, 8), hv4(eqb, 16, 8), AX.X, ALU.add)
        VO("tensor_scalar_mul", idxb[:, :], eqb[:, :], NEG)
        VO("tensor_add", idxb[:, :], idxb[:, :], sfc[:, :])
        VO("tensor_reduce", hv4(gm2r, 16, 8), hv4(idxb, 16, 8), AX.X, ALU.max)
        VO("tensor_single_scalar", gcnt[:, :], gcnt[:, :], 1.5, ALU.is_ge)
        VO("tensor_sub", gdd[:, :], gm1[:, :], gm2r[:, :])
        VO("tensor_mul", gge[:, :], gcnt[:, :], gdd[:, :])
        VO("tensor_add", gm2r[:, :], gm2r[:, :], gge[:, :])
        VO("tensor_add", gsb[:, :], gm1[:, :], gm2r[:, :])
        VO("memset", gmk[:, :], 0.0)
        i8b = hv3(i8_sb, 1).broadcast_to((128, 16, 8))
        for it in range(4):
            VO("tensor_reduce", hv3(m16, 16), hv3(gsb, 16), AX.X, ALU.max)
            mb = hv3(m16, 16).broadcast_to((128, 16, 8))
            VO("tensor_tensor", hv3(grA, 16), hv3(gsb, 16), mb, ALU.is_equal)
            VO("tensor_tensor", hv3(grB, 16), hv3(grA, 16), i8b, ALU.mult)
            VO("tensor_scalar_add", grB[:, :], grB[:, :], 127.0)
            VO("tensor_reduce", hv3(m16b, 16), hv3(grB, 16), AX.X, ALU.min)
            mnb = hv3(m16b, 16).broadcast_to((128, 16, 8))
            VO("tensor_tensor", hv3(grA, 16), hv3(grB, 16), mnb, ALU.is_equal)
            VO("tensor_add", gmk[:, :], gmk[:, :], grA[:, :])
            VO("tensor_scalar_mul", grA[:, :], grA[:, :], NEG)
            VO("tensor_add", gsb[:, :], gsb[:, :], grA[:, :])
        yield
        gmb = hv4(gmk, 16, 8).broadcast_to((128, 16, 8, 8))
        VO("tensor_tensor", hv4(sfc, 16, 8), hv4(sfc, 16, 8), gmb, ALU.mult)
        VO("memset", cf, 0.0)
        VO("memset", ws16[:, :], 0.0)
        i64b = hv3(i64_sb, 1).broadcast_to((128, 16, 64))
        eib = hv3(L["ei_sb"], 1).broadcast_to((128, 16, 64))
        for it in range(8):
            VO("tensor_reduce", hv3(m16, 16), hv3(sfc, 16), AX.X, ALU.max)
            mb = hv3(m16, 16).broadcast_to((128, 16, 64))
            VO("tensor_tensor", hv3(eqb, 16), hv3(sfc, 16), mb, ALU.is_equal)
            VO("tensor_tensor", hv3(idxb, 16), hv3(eqb, 16), i64b, ALU.mult)
            VO("tensor_scalar_add", idxb[:, :], idxb[:, :], 127.0)
            VO("tensor_reduce", hv3(m16b, 16), hv3(idxb, 16), AX.X, ALU.min)
            mnb = hv3(m16b, 16).broadcast_to((128, 16, 64))
            VO("tensor_tensor", hv3(eqb, 16), hv3(idxb, 16), mnb, ALU.is_equal)
            VO("tensor_tensor", hv3(idxb, 16), hv3(eqb, 16), eib, ALU.mult)
            VO("tensor_reduce", tke[:, :, it:it + 1], hv3(idxb, 16), AX.X, ALU.add)
            VO("tensor_tensor", hv3(idxb, 16), hv3(sc, 16), hv3(eqb, 16), ALU.mult)
            VO("tensor_add", cf, cf, idxb[:, :])
            VO("tensor_reduce", tkw[:, :, it:it + 1], hv3(idxb, 16), AX.X, ALU.add)
            VO("tensor_tensor", ws16[:, :], ws16[:, :],
               tkw[:, :, it:it + 1].broadcast_to((128, 16, 1)), ALU.add)
            VO("tensor_scalar_mul", eqb[:, :], eqb[:, :], NEG)
            VO("tensor_add", sfc[:, :], sfc[:, :], eqb[:, :])
            if it == 3:
                yield
        VO("tensor_scalar_add", m16[:, :], ws16[:, :], 1e-20)
        VO("reciprocal", m16b[:, :], m16[:, :])
        VO("tensor_scalar_mul", m16[:, :], m16b[:, :], 2.5)
        mcb = hv3(m16, 16).broadcast_to((128, 16, 64))
        VO("tensor_tensor", hv3(cf, 16), hv3(cf, 16), mcb, ALU.mult)
        mkb = hv3(m16, 16).broadcast_to((128, 16, 8))
        VO("tensor_tensor", tkw, tkw, mkb, ALU.mult)
        yield
    L["vs_routing"] = C["vs"]

    psa = L["psa_raw"]
    psRank = [psa[:, 2048:2112], psa[:, 2560:2624]]
    psTot = [psa[:, 3072:3136], psa[:, 3584:3648]]
    tot_sb = grA[:, 0:64]
    selt = grB[:, 0:64]
    W("tensor", "vs", hz["prev_vs_qk"])
    W("tensor", "cs", hz["cs_last_psc"])
    W("tensor", "cs", hz["cs_last_pt"])
    W("tensor", L["lc_all"])
    VO("memset", tot_sb, 0.0)
    rank_evac = []
    tot_cp = []
    for tt in range(NT):
        W("vector", "vs")
        O("vector", "tensor_single_scalar", selt, coef[:, tt * 64:(tt + 1) * 64],
          0.0, ALU.is_gt, inc=("vs", 1))
        vs_sel = C["vs"]
        W("tensor", "vs", vs_sel)
        if tt >= 2:
            W("tensor", "vs", rank_evac[tt - 2])
        O("tensor", "matmul", psRank[tt % 2], L["ustr_sb"], selt,
          start=True, stop=True, inc=("ts", 1))
        if tt >= 1:
            W("tensor", "vs", tot_cp[tt - 1])
        O("tensor", "matmul", psTot[tt % 2], L["onesm_sb"], selt,
          start=True, stop=True, inc=("ts", 1))
        ts_mm = C["ts"]
        W("vector", "ts", ts_mm)
        O("vector", "tensor_add", rankb[:, tt * 64:(tt + 1) * 64],
          psRank[tt % 2], tot_sb, inc=("vs", 1))
        rank_evac.append(C["vs"])
        W("vector", "vs")
        O("vector", "tensor_add", tot_sb, tot_sb, psTot[tt % 2], inc=("vs", 1))
        tot_cp.append(C["vs"])
    hz["vs_last_psS"] = max(hz.get("vs_last_psS", 0), C["vs"])
    yield

    tot = grA[:, 0:64]
    gb = gm1[:, 0:64]
    tmp64 = gcnt[:, 0:64]
    icA = gm2r[:, 0:64]
    icB = gge[:, 0:64]
    so64 = gdd[:, 0:64]
    base64 = gsb[:, 0:64]
    VO("tensor_single_scalar", gb, tot, 0.5, ALU.is_ge)
    for m in range(1, 16):
        VO("tensor_single_scalar", tmp64, tot, 256.0 * m + 0.5, ALU.is_ge)
        VO("tensor_add", gb, gb, tmp64)
    cur, nxt = icA, icB
    VO("tensor_copy", cur, gb)
    for k in (1, 2, 4, 8, 16, 32):
        VO("tensor_copy", nxt[:, 0:k], cur[:, 0:k])
        VO("tensor_add", nxt[:, k:64], cur[:, k:64], cur[:, 0:64 - k])
        cur, nxt = nxt, cur
    fin = cur
    VO("tensor_sub", so64, fin, gb)
    VO("tensor_scalar_mul", base64, so64, float(GTOK))
    yield

    qb, rb2, srow, prow = L["qb"], L["rb2"], L["srow"], L["prow"]
    bs8v = L["bs8"]
    for k in range(8):
        for half in range(2):
            tkec = hv4(topke, 2, 16)[:, half, :, k:k + 1].broadcast_to((128, 16, 64))
            eib = hv3(L["ei_sb"], 1).broadcast_to((128, 16, 64))
            VO("tensor_tensor", hv3(sfc, 16), eib, tkec, ALU.is_equal)
            VO("tensor_tensor", hv3(idxb, 16), hv3(sfc, 16),
               hv3(rankb[:, half * 1024:half * 1024 + 1024], 16), ALU.mult)
            VO("tensor_reduce", hv4(L["rk8"], 2, 16)[:, half, :, k:k + 1],
               hv3(idxb, 16), AX.X, ALU.add)
            baseb = hv3(base64, 1).broadcast_to((128, 16, 64))
            VO("tensor_tensor", hv3(idxb, 16), hv3(sfc, 16), baseb, ALU.mult)
            VO("tensor_reduce", hv4(bs8v, 2, 16)[:, half, :, k:k + 1],
               hv3(idxb, 16), AX.X, ALU.add)
        if k == 3:
            yield
    yield
    VO("tensor_add", srow, bs8v, L["rk8"])
    VO("tensor_copy", L["srowi"], srow)
    vs_srow = C["vs"]
    yield

    W("sync", "ts", L["ln_ts"])
    W("sync", "cs", L["ln_cs"])
    W("sync", ("sy0", C["sy0"]))
    W("sync", ("sy1", C["sy1"]))
    W("gpsimd", "vs", vs_srow)
    W("gpsimd", L["sz_all"])
    gs_marks = {}
    for tt in range(NT):
        par = tt % 2
        if tt >= 2:
            W("sync", gs_marks[tt - 2])
        O("sync", "dma_start", out=L["xtm"][par],
          in_=L["d_xntm"][tt * 128:(tt + 1) * 128, :], inc=("ux%d" % par, 16))
        W("gpsimd", kb.mark("ux%d" % par))
        for k in range(8):
            O("gpsimd", "indirect_dma_start",
              out=L["d_slab"][:, :],
              out_offset=bass.IndirectOffsetOnAxis(L["srowi"][:, tt * 8 + k:tt * 8 + k + 1], 0),
              in_=L["xtm"][par], in_offset=None, inc=("gs%d" % par, 16))
        gs_marks[tt] = kb.mark("gs%d" % par)
        if tt % 4 == 3 and tt < NT - 1:
            yield
    L["scat_all"] = [kb.mark("gs0"), kb.mark("gs1")]

    eidb = rankb[:, 0:NGRAN]
    tmpe = rankb[:, NGRAN:2 * NGRAN]
    wdxf = rankb[:, 2 * NGRAN:3 * NGRAN]
    giov = hv3(L["gio_sb"], 1)
    VO("memset", eidb, 0.0)
    for e in range(64):
        icc = hv3(fin[:, e:e + 1], 1).broadcast_to((128, 1, NGRAN))
        VO("tensor_tensor", hv3(tmpe, 1), giov, icc, ALU.is_ge)
        VO("tensor_add", eidb, eidb, tmpe)
        if e % 32 == 31:
            yield
    VO("tensor_single_scalar", eidb, eidb, 63.0, ALU.min)
    iopb = hv3(L["iop_sb"], 1).broadcast_to((128, 1, NGRAN))
    VO("scalar_tensor_tensor", hv3(wdxf, 1), hv3(eidb, 1), 128.0, iopb,
       ALU.mult, ALU.add)
    VO("tensor_copy", L["windexi"], wdxf)
    L["vs_build"] = C["vs"]


def emit_route_debug(kb, L):
    W, O, C = kb.W, kb.O, kb.cnt
    W("vector", "sr0", C["sr0"])
    W("vector", "sr1", C["sr1"])
    W("vector", "vs")
    O("vector", "tensor_copy", L["R_dbg"][0], L["coef"][:, 0:1024], inc=("vs", 1))
    O("vector", "tensor_copy", L["R_dbg"][1], L["coef"][:, 1024:2048], inc=("vs", 1))
    W("sync", "vs")
    O("sync", "dma_start", out=L["d_y"][0:128, :], in_=L["R_dbg"][0], inc=("st", 16))
    O("sync", "dma_start", out=L["d_y"][128:256, :], in_=L["R_dbg"][1], inc=("st", 16))


def emit_moe(kb, L):
    W, O, C = kb.W, kb.O, kb.cnt
    d_wg, d_wu, d_wd = L["d_wg"], L["d_wu"], L["d_wd"]
    d_xnscr, d_rscr, d_y = L["d_xnscr"], L["d_rscr"], L["d_y"]
    xnst, hbuf, wg_bf, wu_bf, wd_bf = (L["xnst"], L["hbuf"], L["wg_bf"],
                                       L["wu_bf"], L["wd_bf"])
    acc, sGb, R2, coef = L["acc"], L["sG"], L["R2"], L["coef"]
    psG, psU, psY = L["psG"], L["psU"], L["psY"]

    seqU = [(st, u) for st in range(NST) for u in range(NU)]

    O("sync", "dma_start", out=xnst[0][:, :, :, :], in_=d_xnscr[:, 0:ST_TT, :, :],
      inc=("lt0", 16))
    ld_xnst = {0: kb.mark("lt0")}
    gd_u = {}

    gd_dn = {}

    def load_gu(gi):
        st2, u2 = seqU[gi]
        g = "gm%d" % (gi % 2)
        for dd, bb in ((d_wg, wg_bf), (d_wu, wu_bf)):
            O("gpsimd", "dma_start", out=bb[u2 % 2],
              in_=dd[u2, :, :, :].rearrange("p a b -> p (a b)"), inc=(g, 16))
        gd_u[gi] = kb.mark(g)

    def load_dn(gi):
        st2, u2 = seqU[gi]
        g = "gn%d" % (gi % 2)
        O("gpsimd", "dma_start", out=wd_bf[u2 % 2],
          in_=d_wd[u2, :, :, :].rearrange("p a b -> p (a b)"), inc=(g, 16))
        gd_dn[gi] = kb.mark(g)

    load_gu(0)
    load_gu(1)
    load_dn(0)
    load_dn(1)
    O("sync", "dma_start", out=R2[0], in_=d_rscr[0, :, :], inc=("la0", 16))
    ld_R = {(0, 0): kb.mark("la0")}
    O("sync", "dma_start", out=R2[1], in_=d_rscr[1, :, :], inc=("la1", 16))
    ld_R[(0, 1)] = kb.mark("la1")

    ts_gu_done = {}
    ts_dn_done = {}
    vs_h_done = {}
    vs_evac = {}
    st_full = {}
    cs_slot = [0, 0]
    vs_slot = [0, 0]
    last_gu_of_st = {}

    def emit_gu(gi):
        st, u = seqU[gi]
        par = u % 2
        W("tensor", gd_u[gi])
        W("tensor", ld_xnst[st])
        if gi == 0:
            W("tensor", "vs", L["vs_routing"])
        for ic in range(4):
            for tc in range(2):
                slot = (ic * 2 + tc) % 2
                W("tensor", "cs", cs_slot[slot])
                W("tensor", "vs", vs_slot[slot])
                for c in range(FC):
                    O("tensor", "matmul", psG[slot],
                      wg_bf[par][:, c * 512 + ic * 128:c * 512 + ic * 128 + 128],
                      xnst[st % 2][:, tc * 4:(tc + 1) * 4, c, :],
                      start=(c == 0), stop=(c == FC - 1),
                      inc=(("ts", 1) if c == FC - 1 else None))
                ts_g = C["ts"]
                for c in range(FC):
                    O("tensor", "matmul", psU[slot],
                      wu_bf[par][:, c * 512 + ic * 128:c * 512 + ic * 128 + 128],
                      xnst[st % 2][:, tc * 4:(tc + 1) * 4, c, :],
                      start=(c == 0), stop=(c == FC - 1),
                      inc=(("ts", 1) if c == FC - 1 else None))
                ts_u = C["ts"]
                W("scalar", "ts", ts_g)
                O("scalar", "activation", sGb[slot], psG[slot], AF.Silu, inc=("cs", 1))
                cs_slot[slot] = C["cs"]
                W("vector", "cs", cs_slot[slot])
                W("vector", "ts", ts_u)
                if gi >= 2:
                    W("vector", "ts", ts_dn_done[gi - 2])
                O("vector", "tensor_tensor",
                  hbuf[par][:, ic * 1024 + tc * 512:ic * 1024 + tc * 512 + 512],
                  sGb[slot], psU[slot], ALU.mult, inc=("vs", 1))
                vs_slot[slot] = C["vs"]
        ts_gu_done[gi] = C["ts"]
        vs_h_done[gi] = C["vs"]
        last_gu_of_st[st] = C["ts"]

    def emit_dn(gi):
        st, u = seqU[gi]
        par = u % 2
        e = u - 2
        W("tensor", gd_dn[gi])
        for tt in range(ST_TT):
            yp = tt % 2
            W("tensor", "vs", vs_h_done[gi])
            if (gi, tt - 2) in vs_evac:
                W("tensor", "vs", vs_evac[(gi, tt - 2)])
            if gi >= 1:
                W("tensor", "vs", vs_evac[(gi - 1, ST_TT - 2 + yp)])
            for ic in range(4):
                for half in range(2):
                    O("tensor", "matmul", psY[yp][:, half * 512:half * 512 + 512],
                      hbuf[par][:, ic * 1024 + tt * 128:ic * 1024 + tt * 128 + 128],
                      wd_bf[par][:, ic * 1024 + half * 512:ic * 1024 + half * 512 + 512],
                      start=(ic == 0), stop=(ic == 3),
                      inc=(("ts", 1) if (ic == 3 and half == 1) else None))
            ts_y = C["ts"]
            accs = acc[:, tt * 1024:(tt + 1) * 1024]
            W("vector", "ts", ts_y)
            tg = st * ST_TT + tt
            if u == 0:
                W("vector", ld_R[(st, tt)])
                if st > 0:
                    W("vector", st_full[st - 1])
                O("vector", "tensor_tensor", accs, psY[yp], R2[tt % 2], ALU.add,
                  inc=("vs", 1))
            elif u == 1:
                W("vector", "vs", vs_evac[(gi - 1, tt)])
                O("vector", "tensor_tensor", accs, psY[yp], accs, ALU.add,
                  inc=("vs", 1))
            else:
                W("vector", "vs", vs_evac[(gi - 1, tt)])
                O("vector", "scalar_tensor_tensor", accs, psY[yp],
                  coef[:, tg * 64 + e:tg * 64 + e + 1], accs, ALU.mult, ALU.add,
                  inc=("vs", 1))
            vs_evac[(gi, tt)] = C["vs"]
            if u == 0 and tt + 2 < ST_TT:
                W("sync", "vs", C["vs"])
                O("sync", "dma_start", out=R2[tt % 2],
                  in_=d_rscr[st * ST_TT + tt + 2, :, :], inc=("la%d" % (tt % 2), 16))
                ld_R[(st, tt + 2)] = kb.mark("la%d" % (tt % 2))
        ts_dn_done[gi] = C["ts"]
        if u == NU - 1:
            for tt in range(ST_TT):
                W("sync", "vs", vs_evac[(gi, tt)])
                O("sync", "dma_start",
                  out=d_y[(st * ST_TT + tt) * 128:(st * ST_TT + tt + 1) * 128, :],
                  in_=acc[:, tt * 1024:(tt + 1) * 1024], inc=("st", 16))
            st_full[st] = kb.mark("st")
            if st + 1 < NST:
                for r0 in range(2):
                    W("sync", "vs", vs_evac[(st * NU, ST_TT - 2 + r0)])
                    O("sync", "dma_start", out=R2[r0],
                      in_=d_rscr[(st + 1) * ST_TT + r0, :, :], inc=("la%d" % r0, 16))
                    ld_R[(st + 1, r0)] = kb.mark("la%d" % r0)

    for gi in range(len(seqU)):
        st, u = seqU[gi]
        if u == 0 and st + 1 < NST:
            if st >= 1:
                W("sync", "ts", last_gu_of_st[st - 1])
            O("sync", "dma_start", out=xnst[(st + 1) % 2][:, :, :, :],
              in_=d_xnscr[:, (st + 1) * ST_TT:(st + 2) * ST_TT, :, :],
              inc=("lt%d" % ((st + 1) % 2), 16))
            ld_xnst[st + 1] = kb.mark("lt%d" % ((st + 1) % 2))
        emit_gu(gi)
        if gi + 2 < len(seqU):
            W("gpsimd", "ts", ts_gu_done[gi])
            load_gu(gi + 2)
        if gi >= 1:
            emit_dn(gi - 1)
            if gi + 1 < len(seqU):
                W("gpsimd", "ts", ts_dn_done[gi - 1])
                load_dn(gi + 1)
    emit_dn(len(seqU) - 1)


def emit_moe_sparse(kb, L):
    W, O, C = kb.W, kb.O, kb.cnt
    d_wg, d_wu, d_wd = L["d_wg"], L["d_wu"], L["d_wd"]
    d_xnscr, d_rscr, d_y = L["d_xnscr"], L["d_rscr"], L["d_y"]
    xnst, wg_bf, wu_bf, wd_bf = L["xnst"], L["wg_bf"], L["wu_bf"], L["wd_bf"]
    acc, sGb, R2 = L["acc"], L["sG"], L["R2"]
    psG, psU, psY = L["psG"], L["psU"], L["psY"]
    hbuf, hb2 = L["hbuf"], L["hbuf2"]
    xnT_g, ystage, xs = L["xnT_g"], L["ystage"], L["xs"]
    psYg, psTb = L["psYg"], L["psTb"]
    idb_sb = L["idb_sb"]
    d_slab, d_ybig = L["d_slab"], L["d_ybig"]
    wgubv, wdbv = L["d_wgub"][:, :], L["d_wdb"][:, :]

    units = [("sh", st, u) for st in range(NST) for u in range(2)] + \
            [("gr", g, 0) for g in range(NGRAN)]
    NG = len(units)

    O("sync", "dma_start", out=xnst[0][:, :, :, :], in_=d_xnscr[:, 0:ST_TT, :, :],
      inc=("lt0", 16))
    ld_xnst = {0: kb.mark("lt0")}
    gd_u = {}
    gd_dn = {}

    def load_gu(gi):
        kind = units[gi]
        g = "gm%d" % (gi % 2)
        if kind[0] == "sh":
            for dd, bb in ((d_wg, wg_bf), (d_wu, wu_bf)):
                O("gpsimd", "dma_start", out=bb[gi % 2],
                  in_=dd[kind[2], :, :, :].rearrange("p a b -> p (a b)"), inc=(g, 16))
        else:
            gr = kind[1]
            g = "gg%d" % (gi % 2)
            if gr == 0:
                W("gpsimd", L["pcw_all"])
                W("gpsimd", "vs", L["vs_build"])
            O("gpsimd", "indirect_dma_start", out=L["wgu_bf"][gi % 2],
              out_offset=None, in_=wgubv,
              in_offset=bass.IndirectOffsetOnAxis(L["windexi"][:, gr:gr + 1], 0),
              inc=(g, 16))
        gd_u[gi] = kb.mark(g)

    def load_dn(gi):
        kind = units[gi]
        g = "gn%d" % (gi % 2)
        if kind[0] == "sh":
            O("gpsimd", "dma_start", out=wd_bf[gi % 2],
              in_=d_wd[kind[2], :, :, :].rearrange("p a b -> p (a b)"), inc=(g, 16))
        else:
            gr = kind[1]
            g = "gh%d" % (gi % 2)
            O("gpsimd", "indirect_dma_start", out=wd_bf[gi % 2], out_offset=None,
              in_=wdbv,
              in_offset=bass.IndirectOffsetOnAxis(L["windexi"][:, gr:gr + 1], 0),
              inc=(g, 16))
        gd_dn[gi] = kb.mark(g)

    load_gu(0)
    load_gu(1)
    load_dn(0)
    load_dn(1)
    O("sync", "dma_start", out=R2[0], in_=d_rscr[0, :, :], inc=("la0", 16))
    ld_R = {(0, 0): kb.mark("la0")}
    O("sync", "dma_start", out=R2[1], in_=d_rscr[1, :, :], inc=("la1", 16))
    ld_R[(0, 1)] = kb.mark("la1")

    ts_gu_done = {}
    ts_dn_done = {}
    vs_h_done = {}
    vs_evac = {}
    st_full = {}
    cs_slot = [0, 0]
    vs_slot = [0, 0]
    last_gu_of_st = {}
    ua_marks = {}
    tr_last = {0: 0, 1: 0}
    go_last = {0: ("go0", 0), 1: ("go1", 0)}

    def load_granule(gi):
        gr = units[gi][1]
        par = gi % 2
        if gr < 2:
            for m in L["scat_all"]:
                W("sync", m)
            W("sync", st_full[NST - 1])
        else:
            W("sync", "ts", ts_gu_done[gi - 2])
        g = "ua%d" % par
        for i in range(2):
            O("sync", "dma_start", out=xs[par * 2 + i],
              in_=d_slab[gr * GTOK + i * 128:gr * GTOK + (i + 1) * 128, :],
              inc=(g, 16))
        ua_marks[gi] = kb.mark(g)

    def emit_gu(gi):
        kind = units[gi]
        par = gi % 2
        if kind[0] == "sh":
            W("tensor", gd_u[gi])
            W("tensor", ld_xnst[kind[1]])
            ntc = 2
        else:
            W("tensor", ua_marks[gi])
            if units[gi - 1][0] == "sh":
                W("tensor", "vs", C["vs"])
            cs_tr = []
            for k2 in range(4):
                i, c0 = k2 // 2, (k2 % 2) * 4
                bank = psTb[k2 % 2]
                if k2 >= 2:
                    W("tensor", "cs", cs_tr[k2 - 2])
                else:
                    W("tensor", "cs", tr_last[k2 % 2])
                for cc in range(4):
                    O("tensor", "transpose", bank[:, cc * 128:(cc + 1) * 128],
                      xs[par * 2 + i][:, (c0 + cc) * 128:(c0 + cc + 1) * 128],
                      idb_sb[:, :], inc=(("ts", 1) if cc == 3 else None))
                W("scalar", "ts", C["ts"])
                O("scalar", "copy",
                  _view(xnT_g[par], [128, FC, GTOK])[:, c0:c0 + 4,
                                                     i * 128:(i + 1) * 128],
                  _view(bank[:, 0:512], [128, 4, 128]), inc=("cs", 1))
                cs_tr.append(C["cs"])
                tr_last[k2 % 2] = C["cs"]
            W("tensor", gd_u[gi])
            W("tensor", "cs", cs_tr[-1])
            ntc = 1
        for ic in range(4):
            for tc in range(ntc):
                slot = (ic * ntc + tc) % 2
                W("tensor", "cs", cs_slot[slot])
                W("tensor", "vs", vs_slot[slot])
                if kind[0] == "sh":
                    nel = 512
                else:
                    nel = GTOK
                for c in range(FC):
                    if kind[0] == "sh":
                        rr = xnst[kind[1] % 2][:, tc * 4:(tc + 1) * 4, c, :]
                    else:
                        rr = xnT_g[par][:, c * GTOK:(c + 1) * GTOK]
                    O("tensor", "matmul", psG[slot][:, 0:nel],
                      wg_bf[par][:, c * 512 + ic * 128:c * 512 + ic * 128 + 128],
                      rr, start=(c == 0), stop=(c == FC - 1),
                      inc=(("ts", 1) if c == FC - 1 else None))
                ts_g = C["ts"]
                for c in range(FC):
                    if kind[0] == "sh":
                        rr = xnst[kind[1] % 2][:, tc * 4:(tc + 1) * 4, c, :]
                    else:
                        rr = xnT_g[par][:, c * GTOK:(c + 1) * GTOK]
                    O("tensor", "matmul", psU[slot][:, 0:nel],
                      wu_bf[par][:, c * 512 + ic * 128:c * 512 + ic * 128 + 128],
                      rr, start=(c == 0), stop=(c == FC - 1),
                      inc=(("ts", 1) if c == FC - 1 else None))
                ts_u = C["ts"]
                W("scalar", "ts", ts_g)
                O("scalar", "activation", sGb[slot][:, 0:nel], psG[slot][:, 0:nel],
                  AF.Silu, inc=("cs", 1))
                cs_slot[slot] = C["cs"]
                W("vector", "cs", cs_slot[slot])
                W("vector", "ts", ts_u)
                if gi >= 2:
                    W("vector", "ts", ts_dn_done[gi - 2])
                if kind[0] == "sh":
                    hdst = hbuf[par][:, ic * 1024 + tc * 512:ic * 1024 + tc * 512 + 512]
                else:
                    hdst = hb2[par][:, ic * GTOK:(ic + 1) * GTOK]
                O("vector", "tensor_tensor", hdst, sGb[slot][:, 0:nel],
                  psU[slot][:, 0:nel], ALU.mult, inc=("vs", 1))
                vs_slot[slot] = C["vs"]
        ts_gu_done[gi] = C["ts"]
        vs_h_done[gi] = C["vs"]
        if kind[0] == "sh":
            last_gu_of_st[kind[1]] = C["ts"]

    def emit_dn(gi):
        kind = units[gi]
        par = gi % 2
        W("tensor", gd_dn[gi])
        if kind[0] == "sh":
            st, u = kind[1], kind[2]
            for tt in range(ST_TT):
                yp = tt % 2
                W("tensor", "vs", vs_h_done[gi])
                if (gi, tt - 2) in vs_evac:
                    W("tensor", "vs", vs_evac[(gi, tt - 2)])
                if gi >= 1:
                    W("tensor", "vs", vs_evac[(gi - 1, ST_TT - 2 + yp)])
                for ic in range(4):
                    for half in range(2):
                        O("tensor", "matmul", psY[yp][:, half * 512:half * 512 + 512],
                          hbuf[par][:, ic * 1024 + tt * 128:ic * 1024 + tt * 128 + 128],
                          wd_bf[par][:, ic * 1024 + half * 512:ic * 1024 + half * 512 + 512],
                          start=(ic == 0), stop=(ic == 3),
                          inc=(("ts", 1) if (ic == 3 and half == 1) else None))
                ts_y = C["ts"]
                accs = acc[:, tt * 1024:(tt + 1) * 1024]
                W("vector", "ts", ts_y)
                if u == 0:
                    W("vector", ld_R[(st, tt)])
                    if st > 0:
                        W("vector", st_full[st - 1])
                    O("vector", "tensor_tensor", accs, psY[yp], R2[tt % 2], ALU.add,
                      inc=("vs", 1))
                else:
                    W("vector", "vs", vs_evac[(gi - 1, tt)])
                    O("vector", "tensor_tensor", accs, psY[yp], accs, ALU.add,
                      inc=("vs", 1))
                vs_evac[(gi, tt)] = C["vs"]
                if u == 0 and tt + 2 < ST_TT:
                    W("sync", "vs", C["vs"])
                    O("sync", "dma_start", out=R2[tt % 2],
                      in_=d_rscr[st * ST_TT + tt + 2, :, :], inc=("la%d" % (tt % 2), 16))
                    ld_R[(st, tt + 2)] = kb.mark("la%d" % (tt % 2))
            ts_dn_done[gi] = C["ts"]
            if u == 1:
                for tt in range(ST_TT):
                    W("sync", "vs", vs_evac[(gi, tt)])
                    O("sync", "dma_start",
                      out=d_y[(st * ST_TT + tt) * 128:(st * ST_TT + tt + 1) * 128, :],
                      in_=acc[:, tt * 1024:(tt + 1) * 1024], inc=("st", 16))
                st_full[st] = kb.mark("st")
                if st + 1 < NST:
                    for r0 in range(2):
                        W("sync", "vs", vs_evac[(gi - 1, ST_TT - 2 + r0)])
                        O("sync", "dma_start", out=R2[r0],
                          in_=d_rscr[(st + 1) * ST_TT + r0, :, :], inc=("la%d" % r0, 16))
                        ld_R[(st + 1, r0)] = kb.mark("la%d" % r0)
        else:
            gr = kind[1]
            for half in range(2):
                for i in range(2):
                    W("tensor", "vs", vs_h_done[gi])
                    if half == 0:
                        if gi >= 1:
                            pk = (gi - 1,
                                  (ST_TT if units[gi - 1][0] == "sh" else 4) - 2 + i)
                            W("tensor", "vs", vs_evac[pk])
                    else:
                        W("tensor", "vs", vs_evac[(gi, i)])
                    for ic in range(4):
                        O("tensor", "matmul", psYg[i],
                          hb2[par][:, ic * GTOK + i * 128:ic * GTOK + i * 128 + 128],
                          wd_bf[par][:, ic * 1024 + half * 512:ic * 1024 + half * 512 + 512],
                          start=(ic == 0), stop=(ic == 3),
                          inc=(("ts", 1) if ic == 3 else None))
                    ts_y = C["ts"]
                    W("vector", "ts", ts_y)
                    if half == 0:
                        W("vector", go_last[i])
                    O("vector", "tensor_copy",
                      ystage[i][:, half * 512:half * 512 + 512], psYg[i],
                      inc=("vs", 1))
                    vs_evac[(gi, half * 2 + i)] = C["vs"]
                    if half == 1:
                        W("sync", "vs", C["vs"])
                        O("sync", "dma_start",
                          out=d_ybig[gr * GTOK + i * 128:gr * GTOK + (i + 1) * 128, :],
                          in_=ystage[i], inc=("go%d" % i, 16))
                        go_last[i] = kb.mark("go%d" % i)
            ts_dn_done[gi] = C["ts"]

    for gi in range(NG):
        kind = units[gi]
        if kind[0] == "sh" and kind[2] == 0 and kind[1] + 1 < NST:
            st = kind[1]
            if st >= 1:
                W("sync", "ts", last_gu_of_st[st - 1])
            O("sync", "dma_start", out=xnst[(st + 1) % 2][:, :, :, :],
              in_=d_xnscr[:, (st + 1) * ST_TT:(st + 2) * ST_TT, :, :],
              inc=("lt%d" % ((st + 1) % 2), 16))
            ld_xnst[st + 1] = kb.mark("lt%d" % ((st + 1) % 2))
        if gi >= 1:
            emit_dn(gi - 1)
            if gi + 1 < NG:
                W("gpsimd", "ts", ts_dn_done[gi - 1])
                load_dn(gi + 1)
        if kind[0] == "gr" and kind[1] < 2:
            load_granule(gi)
        emit_gu(gi)
        if gi + 2 < NG:
            W("gpsimd", "ts", ts_gu_done[gi])
            load_gu(gi + 2)
            if units[gi + 2][0] == "gr" and units[gi + 2][1] >= 2:
                load_granule(gi + 2)
    emit_dn(NG - 1)
    L["go_all"] = [kb.mark("go0"), kb.mark("go1")]
    L["st_base"] = kb.mark("st")
    emit_final(kb, L)


def emit_final(kb, L):
    W, O, C = kb.W, kb.O, kb.cnt
    d_y, d_ybig = L["d_y"], L["d_ybig"]
    yb, outb, topkw = L["yb_sb"], L["outb"], L["topkw"]
    for m in L["go_all"]:
        W("gpsimd", m)
        W("sync", m)
    W("sync", L["st_base"])
    ts_all = kb.mark("ts")
    W("gpsimd", ts_all)
    vs_red = {}
    st_out = {}
    gk_m = {}
    la_m = {}
    for tt in range(NT):
        par = tt % 4
        if tt >= 4:
            W("gpsimd", "vs", vs_red[tt - 4])
        for k in range(8):
            O("gpsimd", "indirect_dma_start",
              out=yb[par][:, k * 1024:(k + 1) * 1024], out_offset=None,
              in_=d_ybig[:, :],
              in_offset=bass.IndirectOffsetOnAxis(
                  L["srowi"][:, tt * 8 + k:tt * 8 + k + 1], 0),
              inc=("gf%d" % (par % 2), 16))
            gk_m[(tt, k)] = kb.mark("gf%d" % (par % 2))
        if tt >= 2:
            W("sync", "vs", vs_red[tt - 2])
        O("sync", "dma_start", out=L["bstage"][tt % 2],
          in_=d_y[tt * 128:(tt + 1) * 128, :], inc=("la%d" % (tt % 2), 16))
        la_m[tt] = kb.mark("la%d" % (tt % 2))
        W("vector", la_m[tt])
        if tt >= 2:
            W("vector", st_out[tt - 2])
        W("vector", gk_m[(tt, 7)])
        for k in range(8):
            src = L["bstage"][tt % 2] if k == 0 else outb[tt % 2]
            W("vector", "vs")
            O("vector", "scalar_tensor_tensor", outb[tt % 2],
              yb[par][:, k * 1024:(k + 1) * 1024],
              topkw[:, tt * 8 + k:tt * 8 + k + 1], src,
              ALU.mult, ALU.add, inc=("vs", 1))
        vs_red[tt] = C["vs"]
        W("sync", "vs", C["vs"])
        O("sync", "dma_start", out=d_y[tt * 128:(tt + 1) * 128, :], in_=outb[tt % 2],
          inc=("st", 16))
        st_out[tt] = kb.mark("st")



def _prep(inputs):
    x = np.ascontiguousarray(np.asarray(inputs["L0_attn_encodings"], dtype=np.float32)
                             .reshape(NCORES, T, H))
    aw = np.asarray(inputs["L0_attn_attn_w"], dtype=np.float32)
    ab = np.asarray(inputs["L0_attn_attn_b"], dtype=np.float32)
    ow = np.asarray(inputs["L0_attn_out_w"], dtype=np.float32)
    obv = np.asarray(inputs["L0_attn_out_b"], dtype=np.float32)
    rww = np.asarray(inputs["L0_ffn_router_w"], dtype=np.float32)
    ebv = np.asarray(inputs["L0_ffn_e_bias"], dtype=np.float32)
    Wg = np.asarray(inputs["L0_ffn_Wg"], dtype=np.float32)
    Wu = np.asarray(inputs["L0_ffn_Wu"], dtype=np.float32)
    Wd = np.asarray(inputs["L0_ffn_Wd"], dtype=np.float32)
    SWg = np.asarray(inputs["L0_ffn_SWg"], dtype=np.float32)
    SWu = np.asarray(inputs["L0_ffn_SWu"], dtype=np.float32)
    SWd = np.asarray(inputs["L0_ffn_SWd"], dtype=np.float32)

    a3 = aw.reshape(16, 3, 64, H)
    WQ = a3[:, 0].reshape(H, H)
    WK = a3[:, 1].reshape(H, H)
    WV = a3[:, 2].reshape(H, H)
    b3 = ab.reshape(16, 3, 64)
    bq = b3[:, 0].reshape(H)
    bk = b3[:, 1].reshape(H)
    bv = b3[:, 2].reshape(H)

    def whp(Wm):
        return np.ascontiguousarray(Wm.reshape(8, 128, 8, 128).transpose(0, 3, 2, 1))

    wq = whp(WQ)
    wk = whp(WK)
    wv = whp(WV)
    wo = np.ascontiguousarray(ow.reshape(H, 8, 128).transpose(2, 1, 0))
    rw = np.ascontiguousarray(rww.reshape(64, 8, 128).transpose(2, 1, 0))
    bqk = np.ascontiguousarray(
        np.concatenate([bq.reshape(8, 128).T, bk.reshape(8, 128).T], axis=1))
    vb = np.ascontiguousarray(np.broadcast_to(bv, (128, H)))
    ob = np.ascontiguousarray(np.broadcast_to(obv, (128, H)))
    eb = np.ascontiguousarray(np.broadcast_to(ebv, (128, 64)))
    i64 = np.ascontiguousarray(
        np.broadcast_to(np.arange(64, dtype=np.float32) - 127.0, (128, 64)))
    i8 = np.ascontiguousarray(
        np.broadcast_to(np.arange(8, dtype=np.float32) - 127.0, (128, 8)))
    idf = np.eye(128, dtype=np.float32)
    idb = np.eye(128).astype(ml_dtypes.bfloat16)
    pr = np.arange(128)
    ntri = np.where(pr[None, :] <= pr[:, None], 0.0, -10000.0).astype(np.float32)

    swg_u = SWg.reshape(8, 128, 2, 512).transpose(2, 1, 0, 3)
    swu_u = SWu.reshape(8, 128, 2, 512).transpose(2, 1, 0, 3)
    swd_u = SWd.reshape(2, 4, 128, H).transpose(0, 2, 1, 3)
    wg_e = Wg.reshape(64, 8, 128, 512).transpose(0, 2, 1, 3)
    wu_e = Wu.reshape(64, 8, 128, 512).transpose(0, 2, 1, 3)
    wd_e = Wd.reshape(64, 4, 128, H).transpose(0, 2, 1, 3)
    wg = np.ascontiguousarray(np.concatenate([swg_u, wg_e], axis=0))
    wu = np.ascontiguousarray(np.concatenate([swu_u, wu_e], axis=0))
    wd = np.ascontiguousarray(np.concatenate([swd_u, wd_e], axis=0))

    shared = dict(wq=wq, wk=wk, wv=wv, wo=wo, rw=rw, bqk=bqk, vb=vb, ob=ob,
                  eb=eb, i64=i64, i8=i8, idf=idf, idb=idb, ntri=ntri)
    if KSTAGE not in ("ln", "attn", "route"):
        shared.update(wg=wg, wu=wu, wd=wd)
    if KMOE == "sparse" and KSTAGE == "full":
        kk = np.arange(128)
        ustr = (kk[:, None] < kk[None, :]).astype(np.float32)
        onesm = np.ones((128, 128), np.float32)
        eiv = np.ascontiguousarray(
            np.broadcast_to(np.arange(64, dtype=np.float32), (128, 64)))
        gio = np.ascontiguousarray(
            np.broadcast_to(np.arange(NGRAN, dtype=np.float32), (128, NGRAN)))
        iop = np.arange(128, dtype=np.float32).reshape(128, 1).copy()
        shared.update(ustr=ustr, onesm=onesm, ei=eiv, gio=gio, iop=iop)
    in_maps = []
    for c in range(NCORES):
        m = dict(shared)
        m["x"] = np.ascontiguousarray(x[c])
        in_maps.append(m)
    return in_maps


def kernel(**inputs):
    in_maps = _prep(inputs)
    nc = build_program()
    res = run_bass_kernel_spmd(nc, in_maps, core_ids=list(range(NCORES)))
    out = np.concatenate([np.asarray(r["y"], dtype=np.float32) for r in res.results],
                         axis=0)
    return out.reshape(16, 2048, 1024)


# version 21
# speedup vs baseline: 1.1106x; 1.1106x over previous
import os
import numpy as np
import ml_dtypes
import concourse.bass as bass
import concourse.mybir as mybir
from concourse.bass_utils import run_bass_kernel_spmd
from contextlib import ExitStack

F32 = mybir.dt.float32
BF16 = mybir.dt.bfloat16
AF = mybir.ActivationFunctionType
ALU = mybir.AluOpType
AX = mybir.AxisListType

NCORES = 8
H = 1024
T = 4096
NT = 32
NTS = 16
NST = 4
ST_TT = 8
FC = 8
NHP = 8
NU = 66
NEG = -1e30

KSTAGE = os.environ.get("KSTAGE", "full")
KMOE = os.environ.get("KMOE", "sparse")
NGRAN = 192
GTOK = 256
NSLAB = NGRAN * GTOK
KNT = int(os.environ.get("KNT", "0"))

ENGS = ("sync", "scalar", "vector", "tensor", "gpsimd")
SEMS = ("ts", "vs", "cs",
        "lc", "lx0", "lx1", "lxr", "lt0", "lt1", "la0", "la1",
        "gw0", "gw1", "gwo", "gm0", "gm1", "gn0", "gn1", "sx", "sr0", "sr1", "st",
        "sy0", "sy1", "sz", "gs0", "gs1", "gp0", "gp1", "ux0", "ux1",
        "ua0", "ua1", "ud0", "ud1", "go0", "go1", "pcw",
        "gg0", "gg1", "gh0", "gh1", "gf0", "gf1")


class KB:
    def __init__(self):
        self.ops = {e: [] for e in ENGS}
        self.cnt = {}
        self.sems = {}

    def sem(self, stack, nc, name):
        self.sems[name] = stack.enter_context(nc.semaphore(name))
        self.cnt[name] = 0

    def W(self, eng, sem, val=None):
        if isinstance(sem, tuple):
            sem, val = sem
        if val is None:
            val = self.cnt[sem]
        if not val or val <= 0:
            return
        s = self.sems[sem]
        self.ops[eng].append(lambda e, s=s, v=val: e.wait_ge(s, v))

    def O(self, eng, fname, *args, inc=None, **kw):
        semh, amt = None, 0
        if inc is not None:
            nm, amt = inc
            self.cnt[nm] += amt
            semh = self.sems[nm]

        def ap(a):
            if a is None or isinstance(a, (int, float, bass.AP)):
                return a
            if hasattr(a, "shape") and hasattr(a, "__getitem__"):
                return a[tuple(slice(None) for _ in a.shape)]
            return a

        args = tuple(ap(a) for a in args)
        kw = {k: (ap(v) if k in ("out", "in_", "bias", "scale", "accum_out") else v)
              for k, v in kw.items()}

        def f(e, fname=fname, args=args, kw=kw, semh=semh, amt=amt):
            ins = getattr(e, fname)(*args, **kw)
            if semh is not None:
                ins.then_inc(semh, amt)
        self.ops[eng].append(f)

    def mark(self, sem):
        return (sem, self.cnt[sem])

    def barrier(self):
        snap = {s: self.cnt[s] for s in SEMS}
        for e in ENGS:
            for s in SEMS:
                self.W(e, s, snap[s])


def prod(xs):
    r = 1
    for v in xs:
        r *= v
    return r


def _view(ap, shape):
    if len(shape) == 2:
        return ap
    if len(shape) == 3:
        return ap.rearrange("p (a b) -> p a b", a=shape[1])
    return ap.rearrange("p (a b c) -> p a b c", a=shape[1], b=shape[2])


def build_program():
    nc = bass.Bass("TRN2", target_bir_lowering=False)

    L = {}
    L["d_x"] = nc.dram_tensor("x", [T, H], F32, kind="ExternalInput")
    L["d_wq"] = nc.dram_tensor("wq", [NHP, 128, FC, 128], F32, kind="ExternalInput")
    L["d_wk"] = nc.dram_tensor("wk", [NHP, 128, FC, 128], F32, kind="ExternalInput")
    L["d_wv"] = nc.dram_tensor("wv", [NHP, 128, FC, 128], F32, kind="ExternalInput")
    L["d_wo"] = nc.dram_tensor("wo", [128, FC, H], F32, kind="ExternalInput")
    L["d_rw"] = nc.dram_tensor("rw", [128, FC, 64], F32, kind="ExternalInput")
    L["d_bqk"] = nc.dram_tensor("bqk", [128, 16], F32, kind="ExternalInput")
    L["d_vb"] = nc.dram_tensor("vb", [128, H], F32, kind="ExternalInput")
    L["d_ob"] = nc.dram_tensor("ob", [128, H], F32, kind="ExternalInput")
    L["d_eb"] = nc.dram_tensor("eb", [128, 64], F32, kind="ExternalInput")
    L["d_i64"] = nc.dram_tensor("i64", [128, 64], F32, kind="ExternalInput")
    L["d_i8"] = nc.dram_tensor("i8", [128, 8], F32, kind="ExternalInput")
    L["d_idf"] = nc.dram_tensor("idf", [128, 128], F32, kind="ExternalInput")
    L["d_idb"] = nc.dram_tensor("idb", [128, 128], BF16, kind="ExternalInput")
    L["d_ntri"] = nc.dram_tensor("ntri", [128, 128], F32, kind="ExternalInput")
    if KSTAGE not in ("ln", "attn", "route"):
        L["d_wg"] = nc.dram_tensor("wg", [NU, 128, FC, 512], F32, kind="ExternalInput")
        L["d_wu"] = nc.dram_tensor("wu", [NU, 128, FC, 512], F32, kind="ExternalInput")
        L["d_wd"] = nc.dram_tensor("wd", [NU, 128, 4, 1024], F32, kind="ExternalInput")
    if KMOE == "sparse" and KSTAGE == "full":
        L["d_ustr"] = nc.dram_tensor("ustr", [128, 128], F32, kind="ExternalInput")
        L["d_onesm"] = nc.dram_tensor("onesm", [128, 128], F32, kind="ExternalInput")
        L["d_ei"] = nc.dram_tensor("ei", [128, 64], F32, kind="ExternalInput")
        L["d_gio"] = nc.dram_tensor("gio", [128, NGRAN], F32, kind="ExternalInput")
        L["d_iop"] = nc.dram_tensor("iop", [128, 1], F32, kind="ExternalInput")
        L["d_xntm"] = nc.dram_tensor("xntm", [T, H], BF16, kind="Internal")
        L["d_slab"] = nc.dram_tensor("slab", [NSLAB, H], BF16, kind="Internal")
        L["d_ybig"] = nc.dram_tensor("ybig", [NSLAB, H], BF16, kind="Internal")
        L["d_wgub"] = nc.dram_tensor("wgub", [64 * 128, 8192], BF16, kind="Internal")
        L["d_wdb"] = nc.dram_tensor("wdb", [64 * 128, 4096], BF16, kind="Internal")
    L["d_xnscr"] = nc.dram_tensor("xnscr", [128, NT, FC, 128], BF16, kind="Internal")
    L["d_rscr"] = nc.dram_tensor("rscr", [NT, 128, H], F32, kind="Internal")
    L["d_y"] = nc.dram_tensor("y", [T, H], F32, kind="ExternalOutput")

    kb = KB()
    with ExitStack() as stk:
        def sb(name, shape, dt=F32):
            return stk.enter_context(nc.sbuf_tensor(name, shape, dt))

        for nm, shape, dt in (
            ("rw_sb", [128, FC * 64], F32), ("bqk_sb", [128, 16], F32),
            ("vb_sb", [128, H], F32), ("ob_sb", [128, H], F32),
            ("eb_sb", [128, 64], F32), ("i64_sb", [128, 64], F32),
            ("i8_sb", [128, 8], F32), ("idf_sb", [128, 128], F32),
            ("idb_sb", [128, 128], BF16), ("ntri_sb", [128, 128], F32),
            ("scores", [128, NT * 64], F32), ("coef", [128, NT * 64], F32),
            ("sfc", [128, 1024], F32), ("eqb", [128, 1024], F32),
            ("idxb", [128, 1024], F32),
            ("gm1", [128, 128], F32), ("gcnt", [128, 128], F32),
            ("gm2r", [128, 128], F32), ("gge", [128, 128], F32),
            ("gdd", [128, 128], F32), ("gsb", [128, 128], F32),
            ("gmk", [128, 128], F32), ("grA", [128, 128], F32),
            ("grB", [128, 128], F32),
            ("m16", [128, 16], F32), ("m16b", [128, 16], F32),
            ("ws16", [128, 16], F32),
            ("mxc", [128, 8], F32),
            ("ns2", [128, 2], F32), ("nm2", [128, 2], F32), ("mm2", [128, 2], F32),
            ("var2", [128, 2], F32), ("sd2", [128, 2], F32), ("rr2", [128, 2], F32),
            ("sq2", [128, 2], F32), ("nmx2", [128, 2], F32), ("rs2", [128, 2], F32),
            ("rc2", [128, 2], F32),
            ("topke", [128, 256], F32), ("topkw", [128, 256], F32),
            ("rk8", [128, 256], F32), ("rankb2", [128, 2048], F32),
            ("ustr_sb", [128, 128], F32), ("onesm_sb", [128, 128], F32),
            ("ei_sb", [128, 64], F32), ("gio_sb", [128, NGRAN], F32),
            ("iop_sb", [128, 1], F32),
        ):
            L[nm] = sb(nm, shape, dt)

        AR = 73728
        arena = sb("arena", [128, AR], BF16)

        def cbf(off, shape):
            return _view(arena[:, off:off + prod(shape[1:])], shape)

        def cf32(off, shape):
            n = prod(shape[1:])
            return _view(arena[:, off:off + 2 * n].bitcast(F32), shape)

        L["xn_sb"] = cbf(0, [128, NTS, FC, 128])
        L["wo_bf"] = cbf(16384, [128, FC * H])
        L["wq_bf"] = [cbf(24576 + i * 1024, [128, 1024]) for i in range(2)]
        L["wk_bf"] = [cbf(26624 + i * 1024, [128, 1024]) for i in range(2)]
        L["wv_bf"] = [cbf(28672 + i * 1024, [128, 1024]) for i in range(2)]
        L["qf"] = [cbf(30720 + i * 2048, [128, 2048]) for i in range(2)]
        L["kf"] = [cbf(34816 + i * 2048, [128, 2048]) for i in range(2)]
        L["vt"] = [cbf(38912 + i * 2048, [128, 2048]) for i in range(2)]
        L["Pb2"] = [cbf(43008, [128, 2048]), cbf(63488, [128, 2048])]
        L["PT2"] = [cbf(45056, [128, 2048]), cbf(71680, [128, 2048])]
        L["comb"] = cbf(47104, [128, NHP, NTS, 128])
        L["xn_stage"] = [cbf(63488 + i * 1024, [128, FC, 128]) for i in range(2)]
        L["x_tile"] = [cf32(65536 + i * 2048, [128, 1024]) for i in range(2)]
        L["xn_f32"] = cf32(69632, [128, 1024])
        L["xnT"] = cf32(71680, [128, 1024])
        L["R_dbg"] = [cf32(30720 + i * 2048, [128, 1024]) for i in range(2)]
        L["xtm"] = [cbf(69632 + i * 1024, [128, 1024]) for i in range(2)]
        L["srowi"] = L["eqb"][:, 0:256].bitcast(mybir.dt.int32)
        L["bs8"] = L["eqb"][:, 512:768]
        L["qb"] = L["sfc"][:, 0:256]
        L["rb2"] = L["sfc"][:, 256:512]
        L["srow"] = L["sfc"][:, 512:768]
        L["prow"] = L["sfc"][:, 768:1024]
        L["windexi"] = L["idxb"][:, 0:NGRAN].bitcast(mybir.dt.int32)
        L["xtm_ln"] = [cbf(43008 + i * 1024, [128, 1024]) for i in range(2)]
        L["acc"] = cf32(0, [128, ST_TT * 1024])
        L["xnst"] = [cbf(16384 + i * 8192, [128, ST_TT, FC, 128]) for i in range(2)]
        L["hbuf"] = [cbf(32768 + i * 4096, [128, 4096]) for i in range(2)]
        L["wg_bf"] = [cbf(40960 + i * 8192, [128, 4096]) for i in range(2)]
        L["wu_bf"] = [cbf(45056 + i * 8192, [128, 4096]) for i in range(2)]
        L["wgu_bf"] = [cbf(40960 + i * 8192, [128, 8192]) for i in range(2)]
        L["wd_bf"] = [cbf(57344 + i * 4096, [128, 4096]) for i in range(2)]
        L["sG"] = [cf32(65536 + i * 1024, [128, 512]) for i in range(2)]
        L["R2"] = [cf32(69632 + i * 2048, [128, 1024]) for i in range(2)]
        L["xnT_g"] = [cbf(i * 2048, [128, FC * GTOK]) for i in range(2)]
        L["hbuf2"] = [cbf(4096 + i * 1024, [128, 4 * GTOK]) for i in range(2)]
        L["ystage"] = [cbf(6144 + i * 1024, [128, 1024]) for i in range(2)]
        L["bstage"] = [cf32(11264 + i * 2048, [128, 1024]) for i in range(2)]
        L["xs"] = [cbf(16384 + i * 1024, [128, 1024]) for i in range(4)]
        L["yb_sb"] = [cbf(32768 + i * 8192, [128, 8192]) for i in range(4)]
        L["outb"] = [cf32(65536 + i * 2048, [128, 1024]) for i in range(2)]
        L["zini"] = cbf(47104, [128, 8192])

        psa = stk.enter_context(nc.psum_tensor("psa", [128, 4096], F32))
        L["psa_raw"] = psa
        L["pstr"] = [psa[:, 0:128], psa[:, 512:640]]
        L["psr"] = psa[:, 1024:1088]
        L["psS"] = psa[:, 0:2048]
        L["psq"] = psa[:, 2048:2560]
        L["psk"] = psa[:, 2560:3072]
        L["psVC"] = psa[:, 3072:3584]
        L["psT"] = psa[:, 3584:4096].bitcast(BF16)[:, 0:512]
        L["psG"] = [psa[:, 0:512], psa[:, 512:1024]]
        L["psU"] = [psa[:, 1024:1536], psa[:, 1536:2048]]
        L["psY"] = [psa[:, 2048:3072], psa[:, 3072:4096]]
        L["psYg"] = [psa[:, 2048:2560], psa[:, 2560:3072]]
        L["psTb"] = [psa[:, 3072:3584].bitcast(BF16),
                     psa[:, 3584:4096].bitcast(BF16)]

        for s in SEMS:
            kb.sem(stk, nc, s)

        emit_phase0(kb, L)
        emit_ln(kb, L)
        if KSTAGE != "ln":
            sparse = (KSTAGE == "full" and KMOE == "sparse")
            if sparse:
                L["hz"] = {}
                L["prep_gen"] = gen_sparse_prep(kb, L, L["hz"])
            emit_attention(kb, L)
            if sparse:
                g = L.get("prep_gen")
                while g is not None:
                    try:
                        next(g)
                    except StopIteration:
                        g = None
                        L["prep_gen"] = None
            else:
                emit_routing(kb, L)
            if KSTAGE == "route":
                emit_route_debug(kb, L)
            if KSTAGE == "full":
                kb.barrier()
                if sparse:
                    emit_moe_sparse(kb, L)
                else:
                    emit_moe(kb, L)
        for s_ in ("st", "sr0", "sr1", "sx"):
            kb.W("sync", s_, kb.cnt[s_])
            kb.W("gpsimd", s_, kb.cnt[s_])

        with nc.Block() as block:
            @block.sync
            def _(eng):
                for f in kb.ops["sync"]:
                    f(eng)

            @block.scalar
            def _(eng):
                for f in kb.ops["scalar"]:
                    f(eng)

            @block.vector
            def _(eng):
                for f in kb.ops["vector"]:
                    f(eng)

            @block.tensor
            def _(eng):
                for f in kb.ops["tensor"]:
                    f(eng)

            @block.gpsimd
            def _(eng):
                for f in kb.ops["gpsimd"]:
                    f(eng)

    return nc


def emit_phase0(kb, L):
    W, O, C = kb.W, kb.O, kb.cnt
    for dst, src in ((L["rw_sb"], L["d_rw"][:, :, :]), (L["bqk_sb"], L["d_bqk"][:, :]),
                     (L["vb_sb"], L["d_vb"][:, :]), (L["ob_sb"], L["d_ob"][:, :]),
                     (L["eb_sb"], L["d_eb"][:, :]), (L["i64_sb"], L["d_i64"][:, :]),
                     (L["i8_sb"], L["d_i8"][:, :]), (L["idf_sb"], L["d_idf"][:, :]),
                     (L["idb_sb"], L["d_idb"][:, :]), (L["ntri_sb"], L["d_ntri"][:, :])):
        O("sync", "dma_start", out=dst[:, :], in_=src, inc=("lc", 16))
    if KMOE == "sparse" and KSTAGE == "full":
        for dst, src in ((L["ustr_sb"], L["d_ustr"][:, :]), (L["onesm_sb"], L["d_onesm"][:, :]),
                         (L["ei_sb"], L["d_ei"][:, :]), (L["gio_sb"], L["d_gio"][:, :]),
                         (L["iop_sb"], L["d_iop"][:, :])):
            O("sync", "dma_start", out=dst[:, :], in_=src, inc=("lc", 16))
    L["lc_all"] = kb.mark("lc")
    if KSTAGE != "ln":
        O("gpsimd", "dma_start",
          out=_view(L["wo_bf"], [128, FC, H]), in_=L["d_wo"][:, :, :], inc=("gwo", 16))
        L["gwo_all"] = kb.mark("gwo")
    if KMOE == "sparse" and KSTAGE == "full":
        chunks = [(L["d_wg"], L["d_wgub"][:, 0:4096]),
                  (L["d_wu"], L["d_wgub"][:, 4096:8192]),
                  (L["d_wd"], L["d_wdb"][:, :])]
        pc_jobs = []
        for src, dst in chunks:
            s2 = src[:, :, :, :].rearrange("u p a b -> (u p) (a b)")
            for e2 in range(32):
                pc_jobs.append((dst[(2 * e2) * 128:(2 * e2 + 2) * 128, :],
                                s2[(2 + 2 * e2) * 128:(4 + 2 * e2) * 128, :]))
        pc_jobs = [pc_jobs[j + t * 32] for j in range(32) for t in range(3)]

        def pc_emit(n):
            for _ in range(n):
                if pc_jobs:
                    dst, src = pc_jobs.pop(0)
                    O("gpsimd", "dma_start", out=dst, in_=src, inc=("pcw", 16))
        L["pc_emit"] = pc_emit
        L["pcw_all"] = ("pcw", 96 * 16)
        gd01 = []
        for pi2 in range(2):
            g = "gw%d" % (pi2 % 2)
            for dd, bb in ((L["d_wq"], L["wq_bf"]), (L["d_wk"], L["wk_bf"]),
                           (L["d_wv"], L["wv_bf"])):
                O("gpsimd", "dma_start",
                  out=_view(bb[pi2 % 2], [128, FC, 128]), in_=dd[pi2, :, :, :],
                  inc=(g, 16))
            gd01.append(kb.mark(g))
        L["gd_w01"] = gd01
        pc_emit(8)
        emit_sparse_init(kb, L)


def emit_ln(kb, L):
    W, O, C = kb.W, kb.O, kb.cnt
    d_x, d_xnscr, d_y = L["d_x"], L["d_xnscr"], L["d_y"]
    x_tile, xn_f32, xnT, R_sb = L["x_tile"], L["xn_f32"], L["xnT"], L["R_dbg"]
    xn_sb, xn_stage = L["xn_sb"], L["xn_stage"]
    pstr, psr = L["pstr"], L["psr"]
    rw_sb, scores = L["rw_sb"], L["scores"]

    O("sync", "dma_start", out=x_tile[0], in_=d_x[0:128, :], inc=("lx0", 16))
    ld_x = [kb.mark("lx0")]
    cs_tile_end = []
    vs_tile_end = []
    ts_tile_end = []
    st_dbg = []
    sx_marks = []
    sy_marks = []
    vs_xtm = []

    ln_nt = KNT if (KNT and KSTAGE == "ln") else NT
    for tt in range(ln_nt):
        par = tt % 2
        seq = tt // NTS
        tloc = tt % NTS
        if tt + 1 < NT:
            if tt >= 1:
                W("sync", "cs", cs_tile_end[tt - 1])
                W("sync", "vs", vs_tile_end[tt - 1])
            O("sync", "dma_start", out=x_tile[(tt + 1) % 2],
              in_=d_x[(tt + 1) * 128:(tt + 2) * 128, :], inc=("lx%d" % ((tt + 1) % 2), 16))
            ld_x.append(kb.mark("lx%d" % ((tt + 1) % 2)))

        nsv = L["ns2"][:, par:par + 1]
        nmv = L["nm2"][:, par:par + 1]
        mmv = L["mm2"][:, par:par + 1]
        vav = L["var2"][:, par:par + 1]
        sdv = L["sd2"][:, par:par + 1]
        rrv = L["rr2"][:, par:par + 1]
        sqv = L["sq2"][:, par:par + 1]

        W("vector", ld_x[tt])
        O("vector", "reduce_sum", nsv, x_tile[par], axis=AX.X, negate=True, inc=("vs", 1))
        W("vector", "vs")
        O("vector", "tensor_scalar_mul", nmv, nsv, 1.0 / H, inc=("vs", 1))
        W("vector", "vs")
        O("vector", "tensor_scalar", mmv, nmv, nmv, -1e-5, ALU.mult, ALU.add,
          inc=("vs", 1))
        vs_m2 = C["vs"]
        W("scalar", ld_x[tt])
        if tt >= 1:
            W("scalar", "ts", ts_tile_end[tt - 1])
            if KSTAGE == "ln":
                W("scalar", "vs", vs_tile_end[tt - 1] + 1)
            if KMOE == "sparse" and KSTAGE == "full":
                W("scalar", "vs", vs_xtm[tt - 1])
        O("scalar", "activation", xn_f32, x_tile[par], AF.Square,
          accum_out=sqv, inc=("cs", 1))
        W("vector", "cs")
        W("vector", "vs", vs_m2)
        O("vector", "scalar_tensor_tensor", vav, sqv, 1.0 / H, mmv,
          ALU.mult, ALU.subtract, inc=("vs", 1))
        W("scalar", "vs")
        O("scalar", "activation", sdv, vav, AF.Sqrt, inc=("cs", 1))
        W("vector", "cs")
        O("vector", "reciprocal", rrv, sdv, inc=("vs", 1))
        W("vector", "vs")
        O("vector", "tensor_scalar", xn_f32, x_tile[par], nmv, rrv,
          ALU.add, ALU.mult, inc=("vs", 1))
        vs_xn = C["vs"]
        if KMOE == "sparse" and KSTAGE == "full":
            if tt >= 2:
                W("vector", sy_marks[tt - 2])
            W("vector", "vs", vs_xn)
            O("vector", "tensor_copy", L["xtm_ln"][par], xn_f32, inc=("vs", 1))
            vs_xtm.append(C["vs"])
            W("sync", "vs", C["vs"])
            O("sync", "dma_start", out=L["d_xntm"][tt * 128:(tt + 1) * 128, :],
              in_=L["xtm_ln"][par], inc=("sy%d" % par, 16))
            sy_marks.append(kb.mark("sy%d" % par))

        cs_copy = []
        vs_copy = []
        if tt == 0:
            W("tensor", L["lc_all"])
        if seq == 1 and tt >= NTS + 2:
            W("vector", "sx", sx_marks[tt - 1][1])
        for c in range(FC):
            W("tensor", "vs", vs_xn)
            if c >= 2:
                W("tensor", "vs", cs_copy[c - 2])
            elif tt >= 1:
                W("tensor", "cs", cs_tile_end[tt - 1])
                W("tensor", "vs", vs_tile_end[tt - 1])
            O("tensor", "transpose", pstr[c % 2], xn_f32[:, c * 128:(c + 1) * 128],
              L["idf_sb"][:, :], inc=("ts", 1))
            ts_tr = C["ts"]
            W("vector", "ts", ts_tr)
            O("vector", "tensor_copy", xnT[:, c * 128:(c + 1) * 128], pstr[c % 2],
              inc=("vs", 1))
            cs_copy.append(C["vs"])
            W("vector", "vs", cs_copy[-1])
            dst = xn_sb[:, tloc, c, :] if seq == 0 else xn_stage[par][:, c, :]
            O("vector", "tensor_copy", dst, xnT[:, c * 128:(c + 1) * 128], inc=("vs", 1))
            vs_copy.append(C["vs"])
        W("tensor", "vs", cs_copy[-1])
        if tt >= 1:
            W("tensor", "cs", cs_tile_end[tt - 1])
        W("tensor", L["lc_all"])
        for c in range(FC):
            O("tensor", "matmul", psr, xnT[:, c * 128:(c + 1) * 128],
              rw_sb[:, c * 64:(c + 1) * 64], start=(c == 0), stop=(c == FC - 1),
              inc=(("ts", 1) if c == FC - 1 else None))
        ts_tile_end.append(C["ts"])
        W("scalar", "ts")
        O("scalar", "copy", scores[:, tt * 64:(tt + 1) * 64], psr, inc=("cs", 1))
        cs_tile_end.append(C["cs"])
        W("sync", "vs", vs_copy[-1])
        src = xn_sb[:, tloc, :, :] if seq == 0 else xn_stage[par][:, :, :]
        O("sync", "dma_start", out=d_xnscr[:, tt, :, :], in_=src, inc=("sx", 16))
        sx_marks.append(kb.mark("sx"))
        vs_tile_end.append(C["vs"])

        if KSTAGE == "ln":
            if tt >= 2:
                W("vector", st_dbg[tt - 2])
            O("vector", "tensor_copy", R_sb[par], xn_f32, inc=("vs", 1))
            W("sync", "vs")
            O("sync", "dma_start", out=d_y[tt * 128:(tt + 1) * 128, :],
              in_=R_sb[par], inc=("sr%d" % par, 16))
            st_dbg.append(kb.mark("sr%d" % par))
        else:
            st_dbg.append(None)

    L["ln_cs"] = C["cs"]
    L["ln_vs"] = C["vs"]
    L["ln_ts"] = C["ts"]
    L["sx_all"] = kb.mark("sx")


def emit_attention(kb, L):
    W, O, C = kb.W, kb.O, kb.cnt
    d_x, d_wq, d_wk, d_wv = L["d_x"], L["d_wq"], L["d_wk"], L["d_wv"]
    d_xnscr, d_rscr, d_y = L["d_xnscr"], L["d_rscr"], L["d_y"]
    xn_sb, wq_bf, wk_bf, wv_bf, wo_bf = (L["xn_sb"], L["wq_bf"], L["wk_bf"],
                                         L["wv_bf"], L["wo_bf"])
    qf, kf, vt, Pb2, PT2, comb = L["qf"], L["kf"], L["vt"], L["Pb2"], L["PT2"], L["comb"]
    x_tile = L["x_tile"]
    bqk_sb, vb_sb, ob_sb, ntri_sb, idb_sb = (L["bqk_sb"], L["vb_sb"], L["ob_sb"],
                                             L["ntri_sb"], L["idb_sb"])
    psS, psq, psk, psVC, psT = L["psS"], L["psq"], L["psk"], L["psVC"], L["psT"]
    nmx2, rs2, rc2 = L["nmx2"], L["rs2"], L["rc2"]

    pairs = [(s, hp) for s in range(2) for hp in range(NHP)]
    gd_w = {}
    W("vector", L["lc_all"])
    W("scalar", L["sx_all"])
    if KSTAGE == "full" and KMOE == "sparse":
        W("scalar", ("sy0", kb.cnt["sy0"]))
        W("scalar", ("sy1", kb.cnt["sy1"]))
        W("scalar", L["sz_all"])

    def load_w(pi):
        s2, hp2 = pairs[pi]
        g = "gw%d" % (hp2 % 2)
        for dd, bb in ((d_wq, wq_bf), (d_wk, wk_bf), (d_wv, wv_bf)):
            O("gpsimd", "dma_start",
              out=_view(bb[hp2 % 2], [128, FC, 128]), in_=dd[hp2, :, :, :],
              inc=(g, 16))
        gd_w[pi] = kb.mark(g)

    if "gd_w01" in L:
        gd_w[0], gd_w[1] = L["gd_w01"]
    else:
        load_w(0)
        load_w(1)

    ts_qkv_done = {}
    ts_S_done = {}
    ts_pv_done = {}
    cs_last_exp = 0
    vs_last_psS = L["ln_vs"]
    cs_last_pt = 0
    cs_last_psc = 0
    prev_vs_qk = 0
    vs_R = []
    sr_R = []

    for pi, (s, hp) in enumerate(pairs):
        par = hp % 2
        if s == 1 and hp == 0:
            W("sync", "ts", ts_qkv_done[(0, NHP - 1)])
            W("sync", L["sx_all"])
            O("sync", "dma_start", out=xn_sb[:, :, :, :],
              in_=d_xnscr[:, NTS:NT, :, :], inc=("lxr", 16))
            L["lxr_all"] = kb.mark("lxr")

        W("tensor", gd_w[pi])
        if s == 0:
            W("tensor", "vs", L["ln_vs"])
            W("tensor", "cs", L["ln_cs"])
        else:
            W("tensor", L["lxr_all"])
        key2 = pairs[pi - 2] if pi >= 2 else None
        for tc in range(4):
            if tc > 0:
                W("tensor", "vs", C["vs"])
            else:
                W("tensor", "vs", max(prev_vs_qk, L.get("vs_hook", 0)))
            for c in range(FC):
                O("tensor", "matmul", psq, wq_bf[par][:, c * 128:(c + 1) * 128],
                  xn_sb[:, tc * 4:(tc + 1) * 4, c, :],
                  start=(c == 0), stop=(c == FC - 1),
                  inc=(("ts", 1) if c == FC - 1 else None))
            ts_q = C["ts"]
            for c in range(FC):
                O("tensor", "matmul", psk, wk_bf[par][:, c * 128:(c + 1) * 128],
                  xn_sb[:, tc * 4:(tc + 1) * 4, c, :],
                  start=(c == 0), stop=(c == FC - 1),
                  inc=(("ts", 1) if c == FC - 1 else None))
            ts_k = C["ts"]
            W("vector", "ts", ts_q)
            if key2 is not None and tc == 0:
                W("vector", "ts", ts_S_done[key2])
            O("vector", "tensor_scalar", qf[par][:, tc * 512:(tc + 1) * 512],
              psq, bqk_sb[:, hp:hp + 1], 0.125, ALU.add, ALU.mult, inc=("vs", 1))
            W("vector", "ts", ts_k)
            O("vector", "tensor_scalar_add", kf[par][:, tc * 512:(tc + 1) * 512],
              psk, bqk_sb[:, 8 + hp:8 + hp + 1], inc=("vs", 1))
        prev_vs_qk = C["vs"]
        W("tensor", "cs", cs_last_psc)
        vb4 = vb_sb[:, hp * 128:hp * 128 + 128] \
            .rearrange("p (o f) -> p o f", o=1).broadcast_to((128, 4, 128))
        vs_v_batch = []
        for ttv in range(NTS):
            slot = ttv % 4
            if slot == 0 and ttv >= 4:
                W("tensor", "vs", vs_v_batch[-1])
            for c in range(FC):
                O("tensor", "matmul", psVC[:, slot * 128:(slot + 1) * 128],
                  xn_sb[:, ttv, c, :], wv_bf[par][:, c * 128:(c + 1) * 128],
                  start=(c == 0), stop=(c == FC - 1),
                  inc=(("ts", 1) if c == FC - 1 else None))
            if slot == 3:
                ts_v = C["ts"]
                W("vector", "ts", ts_v)
                if key2 is not None and ttv == 3:
                    W("vector", "ts", ts_pv_done[key2])
                O("vector", "tensor_tensor",
                  _view(vt[par][:, (ttv - 3) * 128:(ttv + 1) * 128], [128, 4, 128]),
                  _view(psVC[:, 0:512], [128, 4, 128]),
                  vb4, ALU.add, inc=("vs", 1))
                vs_v_batch.append(C["vs"])
        ts_qkv_done[(s, hp)] = C["ts"]
        vs_v_last = C["vs"]
        if pi + 2 < len(pairs):
            W("gpsimd", "ts", ts_qkv_done[(s, hp)])
            load_w(pi + 2)
        if "pc_emit" in L:
            L["pc_emit"](6)

        vs_qk = prev_vs_qk
        pend = None

        def emit_ptpv(j0, h0, vsn0, pb, pp0):
            nonlocal cs_last_pt, cs_last_psc
            cs_pt = []
            ts_tr = 0
            for ki in range(j0 + 1):
                g = ki % 4
                if ki == 0:
                    W("tensor", "vs", vsn0)
                    W("tensor", "cs", cs_last_pt)
                if g == 0 and ki >= 4:
                    W("tensor", "cs", cs_pt[-1])
                O("tensor", "transpose", psT[:, g * 128:(g + 1) * 128],
                  pb[:, ki * 128:(ki + 1) * 128], idb_sb[:, :],
                  inc=(("ts", 1) if (g == 3 or ki == j0) else None))
                if g == 3 or ki == j0:
                    k0 = (ki // 4) * 4
                    nn = (ki - k0 + 1) * 128
                    ts_tr = C["ts"]
                    W("scalar", "ts", C["ts"])
                    O("scalar", "copy", PT2[pp0][:, k0 * 128:k0 * 128 + nn],
                      psT[:, 0:nn], inc=("cs", 1))
                    cs_pt.append(C["cs"])
            cs_last_pt = cs_pt[-1]
            if j0 == 0 and h0 == 0:
                W("tensor", "vs", vs_v_last)
            if h0 == 0:
                W("tensor", "cs", cs_last_psc)
            for ki in range(j0 + 1):
                if ki % 4 == 0:
                    W("tensor", "cs", cs_pt[ki // 4])
                O("tensor", "matmul", psVC[64 * h0:64 * h0 + 64, 0:128],
                  vt[par][:, ki * 128 + 64 * h0:ki * 128 + 64 * h0 + 64],
                  PT2[pp0][:, ki * 128:(ki + 1) * 128],
                  start=(ki == 0), stop=(ki == j0),
                  inc=(("ts", 1) if ki == j0 else None))
            if h0 == 1:
                W("scalar", "ts", C["ts"])
                O("scalar", "copy", comb[:, hp, j0, :], psVC[:, 0:128], inc=("cs", 1))
                cs_last_psc = C["cs"]
            return ts_tr

        mxc = L["mxc"]

        def pb_ap(si, j2, n2):
            if (j2 + 1) * 128 <= 1024:
                return Pb2[si % 2][:, ((si % 4) // 2) * 1024:
                                   ((si % 4) // 2) * 1024 + n2]
            return Pb2[si % 2][:, 0:n2]

        def emit_rcscale(si, pj, pmpar):
            W("vector", "cs", cs_last_exp)
            O("vector", "reciprocal", rc2[:, pmpar:pmpar + 1],
              rs2[:, pmpar:pmpar + 1], inc=("vs", 1))
            W("vector", "vs")
            O("vector", "tensor_scalar_mul", pb_ap(si, pj, (pj + 1) * 128),
              pb_ap(si, pj, (pj + 1) * 128), rc2[:, pmpar:pmpar + 1],
              inc=("vs", 1))
            return C["vs"]

        cs_exp_par = [cs_last_exp, cs_last_exp]
        vs_psS_par = [vs_last_psS, vs_last_psS]
        pendq = []
        vsn = {}
        ts_tr_end = {}
        for si, (j, h) in enumerate((j2, h2) for j2 in range(NTS)
                                    for h2 in range(2)):
            mpar = si % 2
            kmax = (j + 1) * 128
            base = mpar * 1024 if kmax <= 1024 else 0
            W("tensor", "vs", vs_qk)
            if kmax <= 1024:
                W("tensor", "cs", cs_exp_par[mpar])
                W("tensor", "vs", vs_psS_par[mpar])
            else:
                W("tensor", "cs", cs_last_exp)
                W("tensor", "vs", vs_last_psS)
            nch = (kmax + 511) // 512
            for ci in range(nch):
                kc = ci * 512
                n = min(512, kmax - kc)
                O("tensor", "matmul", psS[:, base + kc:base + kc + n],
                  qf[par][64 * h:64 * h + 64, j * 128:(j + 1) * 128],
                  kf[par][64 * h:64 * h + 64, kc:kc + n],
                  start=True, stop=True, inc=("ts", 1))
                W("vector", "ts", C["ts"])
                O("vector", "reduce_max",
                  mxc[:, mpar * 4 + ci:mpar * 4 + ci + 1],
                  psS[:, base + kc:base + kc + n], axis=AX.X, inc=("vs", 1))
            W("vector", "vs")
            O("vector", "tensor_tensor",
              psS[:, base + j * 128:base + j * 128 + 128],
              psS[:, base + j * 128:base + j * 128 + 128],
              ntri_sb[:, :], ALU.add, inc=("vs", 1))
            vs_psS_par[mpar] = C["vs"]
            vs_last_psS = C["vs"]
            W("vector", "vs")
            O("vector", "reduce_max", nmx2[:, mpar:mpar + 1],
              mxc[:, mpar * 4:mpar * 4 + nch], axis=AX.X, negate=True,
              inc=("vs", 1))
            vs_comb = C["vs"]
            if pendq:
                psi, pj, ph = pendq[-1]
                vsn[psi] = emit_rcscale(psi, pj, psi % 2)
            if len(pendq) == 2:
                psi, pj, ph = pendq.pop(0)
                ts_tr_end[psi] = emit_ptpv(pj, ph, vsn[psi],
                                           pb_ap(psi, pj, (pj + 1) * 128),
                                           psi % 2)
            W("scalar", "vs", vs_comb)
            if si - 2 in ts_tr_end:
                W("scalar", "ts", ts_tr_end[si - 2])
            O("scalar", "activation", pb_ap(si, j, kmax),
              psS[:, base:base + kmax], AF.Exp,
              bias=nmx2[:, mpar:mpar + 1], accum_out=rs2[:, mpar:mpar + 1],
              inc=("cs", 1))
            cs_exp_par[mpar] = C["cs"]
            cs_last_exp = C["cs"]
            pendq.append((si, j, h))
        psi, pj, ph = pendq[-1]
        vsn[psi] = emit_rcscale(psi, pj, psi % 2)
        for psi, pj, ph in pendq:
            emit_ptpv(pj, ph, vsn[psi], pb_ap(psi, pj, (pj + 1) * 128), psi % 2)
        ts_S_done[(s, hp)] = C["ts"]
        ts_pv_done[(s, hp)] = C["ts"]
        gen = L.get("prep_gen")
        if gen is not None:
            hz = L["hz"]
            hz["prev_vs_qk"] = prev_vs_qk
            hz["cs_last_psc"] = cs_last_psc
            hz["cs_last_pt"] = cs_last_pt
            hz["cs_last_exp"] = cs_last_exp
            hz["vs_last_psS"] = vs_last_psS
            try:
                next(gen)
            except StopIteration:
                L["prep_gen"] = None
            L["vs_hook"] = C["vs"]
            cs_last_psc = max(cs_last_psc, hz.get("cs_last_psc", 0))
            vs_last_psS = max(vs_last_psS, hz.get("vs_last_psS", 0))

        if hp == NHP - 1:
            cs_comb = C["cs"]
            for tt in range(NTS):
                par2 = tt % 2
                nR = len(vs_R)
                if nR >= 2:
                    W("sync", "vs", vs_R[nR - 2])
                    W("sync", sr_R[nR - 2])
                else:
                    W("sync", "cs", L["ln_cs"])
                    W("sync", "vs", L["ln_vs"])
                O("sync", "dma_start", out=x_tile[par2],
                  in_=d_x[(s * NTS + tt) * 128:(s * NTS + tt + 1) * 128, :],
                  inc=("lx%d" % par2, 16))
                ld_xt = kb.mark("lx%d" % par2)
                psO = psS[:, (tt % 2) * 1024:(tt % 2) * 1024 + 1024]
                W("tensor", "cs", cs_comb)
                W("tensor", "cs", cs_last_exp)
                W("tensor", "vs", vs_last_psS)
                W("tensor", L["gwo_all"])
                if nR >= 2:
                    W("tensor", "vs", vs_R[nR - 2])
                for half in range(2):
                    for c in range(FC):
                        O("tensor", "matmul", psO[:, half * 512:half * 512 + 512],
                          comb[:, c, tt, :],
                          wo_bf[:, c * 1024 + half * 512:c * 1024 + half * 512 + 512],
                          start=(c == 0), stop=(c == FC - 1),
                          inc=(("ts", 1) if (c == FC - 1 and half == 1) else None))
                ts_o = C["ts"]
                W("vector", "ts", ts_o)
                W("vector", ld_xt)
                if nR >= 2:
                    W("vector", sr_R[nR - 2])
                O("vector", "tensor_tensor", psO, psO, ob_sb[:, :], ALU.add,
                  inc=("vs", 1))
                W("vector", "vs")
                O("vector", "tensor_tensor", x_tile[par2], psO, x_tile[par2],
                  ALU.add, inc=("vs", 1))
                vs_R.append(C["vs"])
                W("sync", "vs", C["vs"])
                O("sync", "dma_start", out=d_rscr[s * NTS + tt, :, :], in_=x_tile[par2],
                  inc=("sr%d" % par2, 16))
                if KSTAGE == "attn":
                    O("sync", "dma_start",
                      out=d_y[(s * NTS + tt) * 128:(s * NTS + tt + 1) * 128, :],
                      in_=x_tile[par2], inc=("sr%d" % par2, 16))
                sr_R.append(kb.mark("sr%d" % par2))
            if s == 0:
                vs_last_psS = max(vs_last_psS, vs_R[-1])
    L["sr_all"] = (kb.mark("sr0"), kb.mark("sr1"))


def emit_routing(kb, L):
    W, O, C = kb.W, kb.O, kb.cnt
    scores, coef = L["scores"], L["coef"]
    sfc, eqb, idxb = L["sfc"], L["eqb"], L["idxb"]
    gm1, gcnt, gm2r, gge, gdd, gsb, gmk, grA, grB = (
        L["gm1"], L["gcnt"], L["gm2r"], L["gge"], L["gdd"], L["gsb"], L["gmk"],
        L["grA"], L["grB"])
    m16, m16b, ws16 = L["m16"], L["m16b"], L["ws16"]
    eb_sb, i64_sb, i8_sb = L["eb_sb"], L["i64_sb"], L["i8_sb"]

    def _ap2(ap):
        return ap if isinstance(ap, bass.AP) else ap[:, :]

    def hv3(ap, a):
        return _ap2(ap).rearrange("p (a b) -> p a b", a=a)

    def hv4(ap, a, b):
        return _ap2(ap).rearrange("p (a b c) -> p a b c", a=a, b=b)

    def VO(fname, *args, **kw):
        W("vector", "vs")
        O("vector", fname, *args, inc=("vs", 1), **kw)

    W("vector", "cs", L["ln_cs"])
    W("vector", L["lc_all"])
    W("scalar", "cs", L["ln_cs"])
    W("scalar", "vs", L["ln_vs"])
    O("scalar", "activation", scores[:, 0:1024], scores[:, 0:1024], AF.Sigmoid,
      inc=("cs", 1))
    O("scalar", "activation", scores[:, 1024:2048], scores[:, 1024:2048],
      AF.Sigmoid, inc=("cs", 1))
    W("vector", "cs")
    for half in range(2):
        sc = scores[:, half * 1024:half * 1024 + 1024]
        cf = coef[:, half * 1024:half * 1024 + 1024]
        ebb = hv3(eb_sb, 1).broadcast_to((128, 16, 64))
        VO("tensor_tensor", hv3(sfc, 16), hv3(sc, 16), ebb, ALU.add)
        VO("tensor_reduce", hv4(gm1, 16, 8), hv4(sfc, 16, 8), AX.X, ALU.max)
        g1b = hv4(gm1, 16, 8).broadcast_to((128, 16, 8, 8))
        VO("tensor_tensor", hv4(eqb, 16, 8), hv4(sfc, 16, 8), g1b, ALU.is_equal)
        VO("tensor_reduce", hv4(gcnt, 16, 8), hv4(eqb, 16, 8), AX.X, ALU.add)
        VO("tensor_scalar_mul", idxb[:, :], eqb[:, :], NEG)
        VO("tensor_add", idxb[:, :], idxb[:, :], sfc[:, :])
        VO("tensor_reduce", hv4(gm2r, 16, 8), hv4(idxb, 16, 8), AX.X, ALU.max)
        VO("tensor_single_scalar", gcnt[:, :], gcnt[:, :], 1.5, ALU.is_ge)
        VO("tensor_sub", gdd[:, :], gm1[:, :], gm2r[:, :])
        VO("tensor_mul", gge[:, :], gcnt[:, :], gdd[:, :])
        VO("tensor_add", gm2r[:, :], gm2r[:, :], gge[:, :])
        VO("tensor_add", gsb[:, :], gm1[:, :], gm2r[:, :])
        VO("memset", gmk[:, :], 0.0)
        i8b = hv3(i8_sb, 1).broadcast_to((128, 16, 8))
        for it in range(4):
            VO("tensor_reduce", hv3(m16, 16), hv3(gsb, 16), AX.X, ALU.max)
            mb = hv3(m16, 16).broadcast_to((128, 16, 8))
            VO("tensor_tensor", hv3(grA, 16), hv3(gsb, 16), mb, ALU.is_equal)
            VO("tensor_tensor", hv3(grB, 16), hv3(grA, 16), i8b, ALU.mult)
            VO("tensor_scalar_add", grB[:, :], grB[:, :], 127.0)
            VO("tensor_reduce", hv3(m16b, 16), hv3(grB, 16), AX.X, ALU.min)
            mnb = hv3(m16b, 16).broadcast_to((128, 16, 8))
            VO("tensor_tensor", hv3(grA, 16), hv3(grB, 16), mnb, ALU.is_equal)
            VO("tensor_add", gmk[:, :], gmk[:, :], grA[:, :])
            VO("tensor_scalar_mul", grA[:, :], grA[:, :], NEG)
            VO("tensor_add", gsb[:, :], gsb[:, :], grA[:, :])
        gmb = hv4(gmk, 16, 8).broadcast_to((128, 16, 8, 8))
        VO("tensor_tensor", hv4(sfc, 16, 8), hv4(sfc, 16, 8), gmb, ALU.mult)
        VO("memset", cf, 0.0)
        VO("memset", ws16[:, :], 0.0)
        i64b = hv3(i64_sb, 1).broadcast_to((128, 16, 64))
        for it in range(8):
            VO("tensor_reduce", hv3(m16, 16), hv3(sfc, 16), AX.X, ALU.max)
            mb = hv3(m16, 16).broadcast_to((128, 16, 64))
            VO("tensor_tensor", hv3(eqb, 16), hv3(sfc, 16), mb, ALU.is_equal)
            VO("tensor_tensor", hv3(idxb, 16), hv3(eqb, 16), i64b, ALU.mult)
            VO("tensor_scalar_add", idxb[:, :], idxb[:, :], 127.0)
            VO("tensor_reduce", hv3(m16b, 16), hv3(idxb, 16), AX.X, ALU.min)
            mnb = hv3(m16b, 16).broadcast_to((128, 16, 64))
            VO("tensor_tensor", hv3(eqb, 16), hv3(idxb, 16), mnb, ALU.is_equal)
            VO("tensor_tensor", hv3(idxb, 16), hv3(sc, 16), hv3(eqb, 16), ALU.mult)
            VO("tensor_add", cf, cf, idxb[:, :])
            VO("tensor_reduce", hv3(m16, 16), hv3(idxb, 16), AX.X, ALU.add)
            VO("tensor_add", ws16[:, :], ws16[:, :], m16[:, :])
            VO("tensor_scalar_mul", eqb[:, :], eqb[:, :], NEG)
            VO("tensor_add", sfc[:, :], sfc[:, :], eqb[:, :])
        VO("tensor_scalar_add", m16[:, :], ws16[:, :], 1e-20)
        VO("reciprocal", m16b[:, :], m16[:, :])
        VO("tensor_scalar_mul", m16[:, :], m16b[:, :], 2.5)
        mcb = hv3(m16, 16).broadcast_to((128, 16, 64))
        VO("tensor_tensor", hv3(cf, 16), hv3(cf, 16), mcb, ALU.mult)
    L["vs_routing"] = C["vs"]


def emit_sparse_init(kb, L):
    W, O, C = kb.W, kb.O, kb.cnt
    O("vector", "memset", L["zini"], 0.0, inc=("vs", 1))
    vs_z = C["vs"]
    W("scalar", "vs", vs_z)
    for b in range(NSLAB * H // (128 * 8192)):
        O("scalar", "dma_start",
          out=L["d_slab"][b * 1024:(b + 1) * 1024, :]
          .rearrange("a b -> (a b)").rearrange("(p f) -> p f", p=128),
          in_=L["zini"], inc=("sz", 16))
    L["sz_all"] = kb.mark("sz")


def gen_sparse_prep(kb, L, hz):
    W, O, C = kb.W, kb.O, kb.cnt
    scores, coef = L["scores"], L["coef"]
    sfc, eqb, idxb = L["sfc"], L["eqb"], L["idxb"]
    gm1, gcnt, gm2r, gge, gdd, gsb, gmk, grA, grB = (
        L["gm1"], L["gcnt"], L["gm2r"], L["gge"], L["gdd"], L["gsb"], L["gmk"],
        L["grA"], L["grB"])
    m16, m16b, ws16 = L["m16"], L["m16b"], L["ws16"]
    eb_sb, i64_sb, i8_sb = L["eb_sb"], L["i64_sb"], L["i8_sb"]
    topke, topkw = L["topke"], L["topkw"]
    rankb = L["rankb2"][:, :]

    def _ap2(ap):
        return ap if isinstance(ap, bass.AP) else ap[:, :]

    def hv3(ap, a):
        return _ap2(ap).rearrange("p (a b) -> p a b", a=a)

    def hv4(ap, a, b):
        return _ap2(ap).rearrange("p (a b c) -> p a b c", a=a, b=b)

    def VO(fname, *args, **kw):
        W("vector", "vs")
        O("vector", fname, *args, inc=("vs", 1), **kw)

    W("vector", "cs", L["ln_cs"])
    W("vector", L["lc_all"])
    W("scalar", "cs", L["ln_cs"])
    W("scalar", "vs", L["ln_vs"])
    O("scalar", "activation", scores[:, 0:1024], scores[:, 0:1024], AF.Sigmoid,
      inc=("cs", 1))
    O("scalar", "activation", scores[:, 1024:2048], scores[:, 1024:2048],
      AF.Sigmoid, inc=("cs", 1))
    W("vector", "cs")
    for half in range(2):
        sc = scores[:, half * 1024:half * 1024 + 1024]
        cf = coef[:, half * 1024:half * 1024 + 1024]
        tke = hv4(topke, 2, 16)[:, half, :, :]
        tkw = hv4(topkw, 2, 16)[:, half, :, :]
        ebb = hv3(eb_sb, 1).broadcast_to((128, 16, 64))
        VO("tensor_tensor", hv3(sfc, 16), hv3(sc, 16), ebb, ALU.add)
        VO("tensor_reduce", hv4(gm1, 16, 8), hv4(sfc, 16, 8), AX.X, ALU.max)
        g1b = hv4(gm1, 16, 8).broadcast_to((128, 16, 8, 8))
        VO("tensor_tensor", hv4(eqb, 16, 8), hv4(sfc, 16, 8), g1b, ALU.is_equal)
        VO("tensor_reduce", hv4(gcnt, 16, 8), hv4(eqb, 16, 8), AX.X, ALU.add)
        VO("tensor_scalar_mul", idxb[:, :], eqb[:, :], NEG)
        VO("tensor_add", idxb[:, :], idxb[:, :], sfc[:, :])
        VO("tensor_reduce", hv4(gm2r, 16, 8), hv4(idxb, 16, 8), AX.X, ALU.max)
        VO("tensor_single_scalar", gcnt[:, :], gcnt[:, :], 1.5, ALU.is_ge)
        VO("tensor_sub", gdd[:, :], gm1[:, :], gm2r[:, :])
        VO("tensor_mul", gge[:, :], gcnt[:, :], gdd[:, :])
        VO("tensor_add", gm2r[:, :], gm2r[:, :], gge[:, :])
        VO("tensor_add", gsb[:, :], gm1[:, :], gm2r[:, :])
        VO("memset", gmk[:, :], 0.0)
        i8b = hv3(i8_sb, 1).broadcast_to((128, 16, 8))
        for it in range(4):
            VO("tensor_reduce", hv3(m16, 16), hv3(gsb, 16), AX.X, ALU.max)
            mb = hv3(m16, 16).broadcast_to((128, 16, 8))
            VO("tensor_tensor", hv3(grA, 16), hv3(gsb, 16), mb, ALU.is_equal)
            VO("tensor_tensor", hv3(grB, 16), hv3(grA, 16), i8b, ALU.mult)
            VO("tensor_scalar_add", grB[:, :], grB[:, :], 127.0)
            VO("tensor_reduce", hv3(m16b, 16), hv3(grB, 16), AX.X, ALU.min)
            mnb = hv3(m16b, 16).broadcast_to((128, 16, 8))
            VO("tensor_tensor", hv3(grA, 16), hv3(grB, 16), mnb, ALU.is_equal)
            VO("tensor_add", gmk[:, :], gmk[:, :], grA[:, :])
            VO("tensor_scalar_mul", grA[:, :], grA[:, :], NEG)
            VO("tensor_add", gsb[:, :], gsb[:, :], grA[:, :])
        yield
        gmb = hv4(gmk, 16, 8).broadcast_to((128, 16, 8, 8))
        VO("tensor_tensor", hv4(sfc, 16, 8), hv4(sfc, 16, 8), gmb, ALU.mult)
        VO("memset", cf, 0.0)
        VO("memset", ws16[:, :], 0.0)
        i64b = hv3(i64_sb, 1).broadcast_to((128, 16, 64))
        eib = hv3(L["ei_sb"], 1).broadcast_to((128, 16, 64))
        for it in range(8):
            VO("tensor_reduce", hv3(m16, 16), hv3(sfc, 16), AX.X, ALU.max)
            mb = hv3(m16, 16).broadcast_to((128, 16, 64))
            VO("tensor_tensor", hv3(eqb, 16), hv3(sfc, 16), mb, ALU.is_equal)
            VO("tensor_tensor", hv3(idxb, 16), hv3(eqb, 16), i64b, ALU.mult)
            VO("tensor_scalar_add", idxb[:, :], idxb[:, :], 127.0)
            VO("tensor_reduce", hv3(m16b, 16), hv3(idxb, 16), AX.X, ALU.min)
            mnb = hv3(m16b, 16).broadcast_to((128, 16, 64))
            VO("tensor_tensor", hv3(eqb, 16), hv3(idxb, 16), mnb, ALU.is_equal)
            VO("tensor_tensor", hv3(idxb, 16), hv3(eqb, 16), eib, ALU.mult)
            VO("tensor_reduce", tke[:, :, it:it + 1], hv3(idxb, 16), AX.X, ALU.add)
            VO("tensor_tensor", hv3(idxb, 16), hv3(sc, 16), hv3(eqb, 16), ALU.mult)
            VO("tensor_add", cf, cf, idxb[:, :])
            VO("tensor_reduce", tkw[:, :, it:it + 1], hv3(idxb, 16), AX.X, ALU.add)
            VO("tensor_tensor", ws16[:, :], ws16[:, :],
               tkw[:, :, it:it + 1].broadcast_to((128, 16, 1)), ALU.add)
            VO("tensor_scalar_mul", eqb[:, :], eqb[:, :], NEG)
            VO("tensor_add", sfc[:, :], sfc[:, :], eqb[:, :])
            if it == 3:
                yield
        VO("tensor_scalar_add", m16[:, :], ws16[:, :], 1e-20)
        VO("reciprocal", m16b[:, :], m16[:, :])
        VO("tensor_scalar_mul", m16[:, :], m16b[:, :], 2.5)
        mcb = hv3(m16, 16).broadcast_to((128, 16, 64))
        VO("tensor_tensor", hv3(cf, 16), hv3(cf, 16), mcb, ALU.mult)
        mkb = hv3(m16, 16).broadcast_to((128, 16, 8))
        VO("tensor_tensor", tkw, tkw, mkb, ALU.mult)
        yield
    L["vs_routing"] = C["vs"]

    psa = L["psa_raw"]
    psRank = [psa[:, 2048:2112], psa[:, 2560:2624]]
    psTot = [psa[:, 3072:3136], psa[:, 3584:3648]]
    tot_sb = grA[:, 0:64]
    selt = grB[:, 0:64]
    W("tensor", "vs", hz["prev_vs_qk"])
    W("tensor", "cs", hz["cs_last_psc"])
    W("tensor", "cs", hz["cs_last_pt"])
    W("tensor", L["lc_all"])
    VO("memset", tot_sb, 0.0)
    rank_evac = []
    tot_cp = []
    for tt in range(NT):
        W("vector", "vs")
        O("vector", "tensor_single_scalar", selt, coef[:, tt * 64:(tt + 1) * 64],
          0.0, ALU.is_gt, inc=("vs", 1))
        vs_sel = C["vs"]
        W("tensor", "vs", vs_sel)
        if tt >= 2:
            W("tensor", "vs", rank_evac[tt - 2])
        O("tensor", "matmul", psRank[tt % 2], L["ustr_sb"], selt,
          start=True, stop=True, inc=("ts", 1))
        if tt >= 1:
            W("tensor", "vs", tot_cp[tt - 1])
        O("tensor", "matmul", psTot[tt % 2], L["onesm_sb"], selt,
          start=True, stop=True, inc=("ts", 1))
        ts_mm = C["ts"]
        W("vector", "ts", ts_mm)
        O("vector", "tensor_add", rankb[:, tt * 64:(tt + 1) * 64],
          psRank[tt % 2], tot_sb, inc=("vs", 1))
        rank_evac.append(C["vs"])
        W("vector", "vs")
        O("vector", "tensor_add", tot_sb, tot_sb, psTot[tt % 2], inc=("vs", 1))
        tot_cp.append(C["vs"])
    hz["vs_last_psS"] = max(hz.get("vs_last_psS", 0), C["vs"])
    yield

    tot = grA[:, 0:64]
    gb = gm1[:, 0:64]
    tmp64 = gcnt[:, 0:64]
    icA = gm2r[:, 0:64]
    icB = gge[:, 0:64]
    so64 = gdd[:, 0:64]
    base64 = gsb[:, 0:64]
    VO("tensor_single_scalar", gb, tot, 0.5, ALU.is_ge)
    for m in range(1, 16):
        VO("tensor_single_scalar", tmp64, tot, 256.0 * m + 0.5, ALU.is_ge)
        VO("tensor_add", gb, gb, tmp64)
    cur, nxt = icA, icB
    VO("tensor_copy", cur, gb)
    for k in (1, 2, 4, 8, 16, 32):
        VO("tensor_copy", nxt[:, 0:k], cur[:, 0:k])
        VO("tensor_add", nxt[:, k:64], cur[:, k:64], cur[:, 0:64 - k])
        cur, nxt = nxt, cur
    fin = cur
    VO("tensor_sub", so64, fin, gb)
    VO("tensor_scalar_mul", base64, so64, float(GTOK))
    yield

    qb, rb2, srow, prow = L["qb"], L["rb2"], L["srow"], L["prow"]
    bs8v = L["bs8"]
    for k in range(8):
        for half in range(2):
            tkec = hv4(topke, 2, 16)[:, half, :, k:k + 1].broadcast_to((128, 16, 64))
            eib = hv3(L["ei_sb"], 1).broadcast_to((128, 16, 64))
            VO("tensor_tensor", hv3(sfc, 16), eib, tkec, ALU.is_equal)
            VO("tensor_tensor", hv3(idxb, 16), hv3(sfc, 16),
               hv3(rankb[:, half * 1024:half * 1024 + 1024], 16), ALU.mult)
            VO("tensor_reduce", hv4(L["rk8"], 2, 16)[:, half, :, k:k + 1],
               hv3(idxb, 16), AX.X, ALU.add)
            baseb = hv3(base64, 1).broadcast_to((128, 16, 64))
            VO("tensor_tensor", hv3(idxb, 16), hv3(sfc, 16), baseb, ALU.mult)
            VO("tensor_reduce", hv4(bs8v, 2, 16)[:, half, :, k:k + 1],
               hv3(idxb, 16), AX.X, ALU.add)
        if k == 3:
            yield
    yield
    VO("tensor_add", srow, bs8v, L["rk8"])
    VO("tensor_copy", L["srowi"], srow)
    vs_srow = C["vs"]
    yield

    W("sync", "ts", L["ln_ts"])
    W("sync", "cs", L["ln_cs"])
    W("sync", ("sy0", C["sy0"]))
    W("sync", ("sy1", C["sy1"]))
    W("gpsimd", "vs", vs_srow)
    W("gpsimd", L["sz_all"])
    gs_marks = {}
    for tt in range(NT):
        par = tt % 2
        if tt >= 2:
            W("sync", gs_marks[tt - 2])
        O("sync", "dma_start", out=L["xtm"][par],
          in_=L["d_xntm"][tt * 128:(tt + 1) * 128, :], inc=("ux%d" % par, 16))
        W("gpsimd", kb.mark("ux%d" % par))
        for k in range(8):
            O("gpsimd", "indirect_dma_start",
              out=L["d_slab"][:, :],
              out_offset=bass.IndirectOffsetOnAxis(L["srowi"][:, tt * 8 + k:tt * 8 + k + 1], 0),
              in_=L["xtm"][par], in_offset=None, inc=("gs%d" % par, 16))
        gs_marks[tt] = kb.mark("gs%d" % par)
        if tt % 4 == 3 and tt < NT - 1:
            yield
    L["scat_all"] = [kb.mark("gs0"), kb.mark("gs1")]

    eidb = rankb[:, 0:NGRAN]
    tmpe = rankb[:, NGRAN:2 * NGRAN]
    wdxf = rankb[:, 2 * NGRAN:3 * NGRAN]
    giov = hv3(L["gio_sb"], 1)
    VO("memset", eidb, 0.0)
    for e in range(64):
        icc = hv3(fin[:, e:e + 1], 1).broadcast_to((128, 1, NGRAN))
        VO("tensor_tensor", hv3(tmpe, 1), giov, icc, ALU.is_ge)
        VO("tensor_add", eidb, eidb, tmpe)
        if e % 32 == 31:
            yield
    VO("tensor_single_scalar", eidb, eidb, 63.0, ALU.min)
    iopb = hv3(L["iop_sb"], 1).broadcast_to((128, 1, NGRAN))
    VO("scalar_tensor_tensor", hv3(wdxf, 1), hv3(eidb, 1), 128.0, iopb,
       ALU.mult, ALU.add)
    VO("tensor_copy", L["windexi"], wdxf)
    L["vs_build"] = C["vs"]


def emit_route_debug(kb, L):
    W, O, C = kb.W, kb.O, kb.cnt
    W("vector", "sr0", C["sr0"])
    W("vector", "sr1", C["sr1"])
    W("vector", "vs")
    O("vector", "tensor_copy", L["R_dbg"][0], L["coef"][:, 0:1024], inc=("vs", 1))
    O("vector", "tensor_copy", L["R_dbg"][1], L["coef"][:, 1024:2048], inc=("vs", 1))
    W("sync", "vs")
    O("sync", "dma_start", out=L["d_y"][0:128, :], in_=L["R_dbg"][0], inc=("st", 16))
    O("sync", "dma_start", out=L["d_y"][128:256, :], in_=L["R_dbg"][1], inc=("st", 16))


def emit_moe(kb, L):
    W, O, C = kb.W, kb.O, kb.cnt
    d_wg, d_wu, d_wd = L["d_wg"], L["d_wu"], L["d_wd"]
    d_xnscr, d_rscr, d_y = L["d_xnscr"], L["d_rscr"], L["d_y"]
    xnst, hbuf, wg_bf, wu_bf, wd_bf = (L["xnst"], L["hbuf"], L["wg_bf"],
                                       L["wu_bf"], L["wd_bf"])
    acc, sGb, R2, coef = L["acc"], L["sG"], L["R2"], L["coef"]
    psG, psU, psY = L["psG"], L["psU"], L["psY"]

    seqU = [(st, u) for st in range(NST) for u in range(NU)]

    O("sync", "dma_start", out=xnst[0][:, :, :, :], in_=d_xnscr[:, 0:ST_TT, :, :],
      inc=("lt0", 16))
    ld_xnst = {0: kb.mark("lt0")}
    gd_u = {}

    gd_dn = {}

    def load_gu(gi):
        st2, u2 = seqU[gi]
        g = "gm%d" % (gi % 2)
        for dd, bb in ((d_wg, wg_bf), (d_wu, wu_bf)):
            O("gpsimd", "dma_start", out=bb[u2 % 2],
              in_=dd[u2, :, :, :].rearrange("p a b -> p (a b)"), inc=(g, 16))
        gd_u[gi] = kb.mark(g)

    def load_dn(gi):
        st2, u2 = seqU[gi]
        g = "gn%d" % (gi % 2)
        O("gpsimd", "dma_start", out=wd_bf[u2 % 2],
          in_=d_wd[u2, :, :, :].rearrange("p a b -> p (a b)"), inc=(g, 16))
        gd_dn[gi] = kb.mark(g)

    load_gu(0)
    load_gu(1)
    load_dn(0)
    load_dn(1)
    O("sync", "dma_start", out=R2[0], in_=d_rscr[0, :, :], inc=("la0", 16))
    ld_R = {(0, 0): kb.mark("la0")}
    O("sync", "dma_start", out=R2[1], in_=d_rscr[1, :, :], inc=("la1", 16))
    ld_R[(0, 1)] = kb.mark("la1")

    ts_gu_done = {}
    ts_dn_done = {}
    vs_h_done = {}
    vs_evac = {}
    st_full = {}
    cs_slot = [0, 0]
    vs_slot = [0, 0]
    last_gu_of_st = {}

    def emit_gu(gi):
        st, u = seqU[gi]
        par = u % 2
        W("tensor", gd_u[gi])
        W("tensor", ld_xnst[st])
        if gi == 0:
            W("tensor", "vs", L["vs_routing"])
        for ic in range(4):
            for tc in range(2):
                slot = (ic * 2 + tc) % 2
                W("tensor", "cs", cs_slot[slot])
                W("tensor", "vs", vs_slot[slot])
                for c in range(FC):
                    O("tensor", "matmul", psG[slot],
                      wg_bf[par][:, c * 512 + ic * 128:c * 512 + ic * 128 + 128],
                      xnst[st % 2][:, tc * 4:(tc + 1) * 4, c, :],
                      start=(c == 0), stop=(c == FC - 1),
                      inc=(("ts", 1) if c == FC - 1 else None))
                ts_g = C["ts"]
                for c in range(FC):
                    O("tensor", "matmul", psU[slot],
                      wu_bf[par][:, c * 512 + ic * 128:c * 512 + ic * 128 + 128],
                      xnst[st % 2][:, tc * 4:(tc + 1) * 4, c, :],
                      start=(c == 0), stop=(c == FC - 1),
                      inc=(("ts", 1) if c == FC - 1 else None))
                ts_u = C["ts"]
                W("scalar", "ts", ts_g)
                O("scalar", "activation", sGb[slot], psG[slot], AF.Silu, inc=("cs", 1))
                cs_slot[slot] = C["cs"]
                W("vector", "cs", cs_slot[slot])
                W("vector", "ts", ts_u)
                if gi >= 2:
                    W("vector", "ts", ts_dn_done[gi - 2])
                O("vector", "tensor_tensor",
                  hbuf[par][:, ic * 1024 + tc * 512:ic * 1024 + tc * 512 + 512],
                  sGb[slot], psU[slot], ALU.mult, inc=("vs", 1))
                vs_slot[slot] = C["vs"]
        ts_gu_done[gi] = C["ts"]
        vs_h_done[gi] = C["vs"]
        last_gu_of_st[st] = C["ts"]

    def emit_dn(gi):
        st, u = seqU[gi]
        par = u % 2
        e = u - 2
        W("tensor", gd_dn[gi])
        for tt in range(ST_TT):
            yp = tt % 2
            W("tensor", "vs", vs_h_done[gi])
            if (gi, tt - 2) in vs_evac:
                W("tensor", "vs", vs_evac[(gi, tt - 2)])
            if gi >= 1:
                W("tensor", "vs", vs_evac[(gi - 1, ST_TT - 2 + yp)])
            for ic in range(4):
                for half in range(2):
                    O("tensor", "matmul", psY[yp][:, half * 512:half * 512 + 512],
                      hbuf[par][:, ic * 1024 + tt * 128:ic * 1024 + tt * 128 + 128],
                      wd_bf[par][:, ic * 1024 + half * 512:ic * 1024 + half * 512 + 512],
                      start=(ic == 0), stop=(ic == 3),
                      inc=(("ts", 1) if (ic == 3 and half == 1) else None))
            ts_y = C["ts"]
            accs = acc[:, tt * 1024:(tt + 1) * 1024]
            W("vector", "ts", ts_y)
            tg = st * ST_TT + tt
            if u == 0:
                W("vector", ld_R[(st, tt)])
                if st > 0:
                    W("vector", st_full[st - 1])
                O("vector", "tensor_tensor", accs, psY[yp], R2[tt % 2], ALU.add,
                  inc=("vs", 1))
            elif u == 1:
                W("vector", "vs", vs_evac[(gi - 1, tt)])
                O("vector", "tensor_tensor", accs, psY[yp], accs, ALU.add,
                  inc=("vs", 1))
            else:
                W("vector", "vs", vs_evac[(gi - 1, tt)])
                O("vector", "scalar_tensor_tensor", accs, psY[yp],
                  coef[:, tg * 64 + e:tg * 64 + e + 1], accs, ALU.mult, ALU.add,
                  inc=("vs", 1))
            vs_evac[(gi, tt)] = C["vs"]
            if u == 0 and tt + 2 < ST_TT:
                W("sync", "vs", C["vs"])
                O("sync", "dma_start", out=R2[tt % 2],
                  in_=d_rscr[st * ST_TT + tt + 2, :, :], inc=("la%d" % (tt % 2), 16))
                ld_R[(st, tt + 2)] = kb.mark("la%d" % (tt % 2))
        ts_dn_done[gi] = C["ts"]
        if u == NU - 1:
            for tt in range(ST_TT):
                W("sync", "vs", vs_evac[(gi, tt)])
                O("sync", "dma_start",
                  out=d_y[(st * ST_TT + tt) * 128:(st * ST_TT + tt + 1) * 128, :],
                  in_=acc[:, tt * 1024:(tt + 1) * 1024], inc=("st", 16))
            st_full[st] = kb.mark("st")
            if st + 1 < NST:
                for r0 in range(2):
                    W("sync", "vs", vs_evac[(st * NU, ST_TT - 2 + r0)])
                    O("sync", "dma_start", out=R2[r0],
                      in_=d_rscr[(st + 1) * ST_TT + r0, :, :], inc=("la%d" % r0, 16))
                    ld_R[(st + 1, r0)] = kb.mark("la%d" % r0)

    for gi in range(len(seqU)):
        st, u = seqU[gi]
        if u == 0 and st + 1 < NST:
            if st >= 1:
                W("sync", "ts", last_gu_of_st[st - 1])
            O("sync", "dma_start", out=xnst[(st + 1) % 2][:, :, :, :],
              in_=d_xnscr[:, (st + 1) * ST_TT:(st + 2) * ST_TT, :, :],
              inc=("lt%d" % ((st + 1) % 2), 16))
            ld_xnst[st + 1] = kb.mark("lt%d" % ((st + 1) % 2))
        emit_gu(gi)
        if gi + 2 < len(seqU):
            W("gpsimd", "ts", ts_gu_done[gi])
            load_gu(gi + 2)
        if gi >= 1:
            emit_dn(gi - 1)
            if gi + 1 < len(seqU):
                W("gpsimd", "ts", ts_dn_done[gi - 1])
                load_dn(gi + 1)
    emit_dn(len(seqU) - 1)


def emit_moe_sparse(kb, L):
    W, O, C = kb.W, kb.O, kb.cnt
    d_wg, d_wu, d_wd = L["d_wg"], L["d_wu"], L["d_wd"]
    d_xnscr, d_rscr, d_y = L["d_xnscr"], L["d_rscr"], L["d_y"]
    xnst, wg_bf, wu_bf, wd_bf = L["xnst"], L["wg_bf"], L["wu_bf"], L["wd_bf"]
    acc, sGb, R2 = L["acc"], L["sG"], L["R2"]
    psG, psU, psY = L["psG"], L["psU"], L["psY"]
    hbuf, hb2 = L["hbuf"], L["hbuf2"]
    xnT_g, ystage, xs = L["xnT_g"], L["ystage"], L["xs"]
    psYg, psTb = L["psYg"], L["psTb"]
    idb_sb = L["idb_sb"]
    d_slab, d_ybig = L["d_slab"], L["d_ybig"]
    wgubv, wdbv = L["d_wgub"][:, :], L["d_wdb"][:, :]

    units = [("sh", st, u) for st in range(NST) for u in range(2)] + \
            [("gr", g, 0) for g in range(NGRAN)]
    NG = len(units)

    O("sync", "dma_start", out=xnst[0][:, :, :, :], in_=d_xnscr[:, 0:ST_TT, :, :],
      inc=("lt0", 16))
    ld_xnst = {0: kb.mark("lt0")}
    gd_u = {}
    gd_dn = {}

    def load_gu(gi):
        kind = units[gi]
        g = "gm%d" % (gi % 2)
        if kind[0] == "sh":
            for dd, bb in ((d_wg, wg_bf), (d_wu, wu_bf)):
                O("gpsimd", "dma_start", out=bb[gi % 2],
                  in_=dd[kind[2], :, :, :].rearrange("p a b -> p (a b)"), inc=(g, 16))
        else:
            gr = kind[1]
            g = "gg%d" % (gi % 2)
            if gr == 0:
                W("gpsimd", L["pcw_all"])
                W("gpsimd", "vs", L["vs_build"])
            O("gpsimd", "indirect_dma_start", out=L["wgu_bf"][gi % 2],
              out_offset=None, in_=wgubv,
              in_offset=bass.IndirectOffsetOnAxis(L["windexi"][:, gr:gr + 1], 0),
              inc=(g, 16))
        gd_u[gi] = kb.mark(g)

    def load_dn(gi):
        kind = units[gi]
        g = "gn%d" % (gi % 2)
        if kind[0] == "sh":
            O("gpsimd", "dma_start", out=wd_bf[gi % 2],
              in_=d_wd[kind[2], :, :, :].rearrange("p a b -> p (a b)"), inc=(g, 16))
        else:
            gr = kind[1]
            g = "gh%d" % (gi % 2)
            O("gpsimd", "indirect_dma_start", out=wd_bf[gi % 2], out_offset=None,
              in_=wdbv,
              in_offset=bass.IndirectOffsetOnAxis(L["windexi"][:, gr:gr + 1], 0),
              inc=(g, 16))
        gd_dn[gi] = kb.mark(g)

    load_gu(0)
    load_gu(1)
    load_dn(0)
    load_dn(1)
    O("sync", "dma_start", out=R2[0], in_=d_rscr[0, :, :], inc=("la0", 16))
    ld_R = {(0, 0): kb.mark("la0")}
    O("sync", "dma_start", out=R2[1], in_=d_rscr[1, :, :], inc=("la1", 16))
    ld_R[(0, 1)] = kb.mark("la1")

    ts_gu_done = {}
    ts_dn_done = {}
    vs_h_done = {}
    vs_evac = {}
    st_full = {}
    cs_slot = [0, 0]
    vs_slot = [0, 0]
    last_gu_of_st = {}
    ua_marks = {}
    tr_last = {0: 0, 1: 0}
    go_last = {0: ("go0", 0), 1: ("go1", 0)}

    def load_granule(gi):
        gr = units[gi][1]
        par = gi % 2
        if gr < 2:
            for m in L["scat_all"]:
                W("sync", m)
            W("sync", st_full[NST - 1])
        else:
            W("sync", "ts", ts_gu_done[gi - 2])
        g = "ua%d" % par
        for i in range(2):
            O("sync", "dma_start", out=xs[par * 2 + i],
              in_=d_slab[gr * GTOK + i * 128:gr * GTOK + (i + 1) * 128, :],
              inc=(g, 16))
        ua_marks[gi] = kb.mark(g)

    def emit_gu(gi):
        kind = units[gi]
        par = gi % 2
        if kind[0] == "sh":
            W("tensor", gd_u[gi])
            W("tensor", ld_xnst[kind[1]])
            ntc = 2
        else:
            W("tensor", ua_marks[gi])
            if units[gi - 1][0] == "sh":
                W("tensor", "vs", C["vs"])
            cs_tr = []
            for k2 in range(4):
                i, c0 = k2 // 2, (k2 % 2) * 4
                bank = psTb[k2 % 2]
                if k2 >= 2:
                    W("tensor", "cs", cs_tr[k2 - 2])
                else:
                    W("tensor", "cs", tr_last[k2 % 2])
                for cc in range(4):
                    O("tensor", "transpose", bank[:, cc * 128:(cc + 1) * 128],
                      xs[par * 2 + i][:, (c0 + cc) * 128:(c0 + cc + 1) * 128],
                      idb_sb[:, :], inc=(("ts", 1) if cc == 3 else None))
                W("scalar", "ts", C["ts"])
                O("scalar", "copy",
                  _view(xnT_g[par], [128, FC, GTOK])[:, c0:c0 + 4,
                                                     i * 128:(i + 1) * 128],
                  _view(bank[:, 0:512], [128, 4, 128]), inc=("cs", 1))
                cs_tr.append(C["cs"])
                tr_last[k2 % 2] = C["cs"]
            W("tensor", gd_u[gi])
            W("tensor", "cs", cs_tr[-1])
            ntc = 1
        for ic in range(4):
            for tc in range(ntc):
                slot = (ic * ntc + tc) % 2
                W("tensor", "cs", cs_slot[slot])
                W("tensor", "vs", vs_slot[slot])
                if kind[0] == "sh":
                    nel = 512
                else:
                    nel = GTOK
                for c in range(FC):
                    if kind[0] == "sh":
                        rr = xnst[kind[1] % 2][:, tc * 4:(tc + 1) * 4, c, :]
                    else:
                        rr = xnT_g[par][:, c * GTOK:(c + 1) * GTOK]
                    O("tensor", "matmul", psG[slot][:, 0:nel],
                      wg_bf[par][:, c * 512 + ic * 128:c * 512 + ic * 128 + 128],
                      rr, start=(c == 0), stop=(c == FC - 1),
                      inc=(("ts", 1) if c == FC - 1 else None))
                ts_g = C["ts"]
                for c in range(FC):
                    if kind[0] == "sh":
                        rr = xnst[kind[1] % 2][:, tc * 4:(tc + 1) * 4, c, :]
                    else:
                        rr = xnT_g[par][:, c * GTOK:(c + 1) * GTOK]
                    O("tensor", "matmul", psU[slot][:, 0:nel],
                      wu_bf[par][:, c * 512 + ic * 128:c * 512 + ic * 128 + 128],
                      rr, start=(c == 0), stop=(c == FC - 1),
                      inc=(("ts", 1) if c == FC - 1 else None))
                ts_u = C["ts"]
                W("scalar", "ts", ts_g)
                O("scalar", "activation", sGb[slot][:, 0:nel], psG[slot][:, 0:nel],
                  AF.Silu, inc=("cs", 1))
                cs_slot[slot] = C["cs"]
                W("vector", "cs", cs_slot[slot])
                W("vector", "ts", ts_u)
                if gi >= 2:
                    W("vector", "ts", ts_dn_done[gi - 2])
                if kind[0] == "sh":
                    hdst = hbuf[par][:, ic * 1024 + tc * 512:ic * 1024 + tc * 512 + 512]
                else:
                    hdst = hb2[par][:, ic * GTOK:(ic + 1) * GTOK]
                O("vector", "tensor_tensor", hdst, sGb[slot][:, 0:nel],
                  psU[slot][:, 0:nel], ALU.mult, inc=("vs", 1))
                vs_slot[slot] = C["vs"]
        ts_gu_done[gi] = C["ts"]
        vs_h_done[gi] = C["vs"]
        if kind[0] == "sh":
            last_gu_of_st[kind[1]] = C["ts"]

    def emit_dn(gi):
        kind = units[gi]
        par = gi % 2
        W("tensor", gd_dn[gi])
        if kind[0] == "sh":
            st, u = kind[1], kind[2]
            for tt in range(ST_TT):
                yp = tt % 2
                W("tensor", "vs", vs_h_done[gi])
                if (gi, tt - 2) in vs_evac:
                    W("tensor", "vs", vs_evac[(gi, tt - 2)])
                if gi >= 1:
                    W("tensor", "vs", vs_evac[(gi - 1, ST_TT - 2 + yp)])
                for ic in range(4):
                    for half in range(2):
                        O("tensor", "matmul", psY[yp][:, half * 512:half * 512 + 512],
                          hbuf[par][:, ic * 1024 + tt * 128:ic * 1024 + tt * 128 + 128],
                          wd_bf[par][:, ic * 1024 + half * 512:ic * 1024 + half * 512 + 512],
                          start=(ic == 0), stop=(ic == 3),
                          inc=(("ts", 1) if (ic == 3 and half == 1) else None))
                ts_y = C["ts"]
                accs = acc[:, tt * 1024:(tt + 1) * 1024]
                W("vector", "ts", ts_y)
                if u == 0:
                    W("vector", ld_R[(st, tt)])
                    if st > 0:
                        W("vector", st_full[st - 1])
                    O("vector", "tensor_tensor", accs, psY[yp], R2[tt % 2], ALU.add,
                      inc=("vs", 1))
                else:
                    W("vector", "vs", vs_evac[(gi - 1, tt)])
                    O("vector", "tensor_tensor", accs, psY[yp], accs, ALU.add,
                      inc=("vs", 1))
                vs_evac[(gi, tt)] = C["vs"]
                if u == 0 and tt + 2 < ST_TT:
                    W("sync", "vs", C["vs"])
                    O("sync", "dma_start", out=R2[tt % 2],
                      in_=d_rscr[st * ST_TT + tt + 2, :, :], inc=("la%d" % (tt % 2), 16))
                    ld_R[(st, tt + 2)] = kb.mark("la%d" % (tt % 2))
            ts_dn_done[gi] = C["ts"]
            if u == 1:
                for tt in range(ST_TT):
                    W("sync", "vs", vs_evac[(gi, tt)])
                    O("sync", "dma_start",
                      out=d_y[(st * ST_TT + tt) * 128:(st * ST_TT + tt + 1) * 128, :],
                      in_=acc[:, tt * 1024:(tt + 1) * 1024], inc=("st", 16))
                st_full[st] = kb.mark("st")
                if st + 1 < NST:
                    for r0 in range(2):
                        W("sync", "vs", vs_evac[(gi - 1, ST_TT - 2 + r0)])
                        O("sync", "dma_start", out=R2[r0],
                          in_=d_rscr[(st + 1) * ST_TT + r0, :, :], inc=("la%d" % r0, 16))
                        ld_R[(st + 1, r0)] = kb.mark("la%d" % r0)
        else:
            gr = kind[1]
            for half in range(2):
                for i in range(2):
                    W("tensor", "vs", vs_h_done[gi])
                    if half == 0:
                        if gi >= 1:
                            pk = (gi - 1,
                                  (ST_TT if units[gi - 1][0] == "sh" else 4) - 2 + i)
                            W("tensor", "vs", vs_evac[pk])
                    else:
                        W("tensor", "vs", vs_evac[(gi, i)])
                    for ic in range(4):
                        O("tensor", "matmul", psYg[i],
                          hb2[par][:, ic * GTOK + i * 128:ic * GTOK + i * 128 + 128],
                          wd_bf[par][:, ic * 1024 + half * 512:ic * 1024 + half * 512 + 512],
                          start=(ic == 0), stop=(ic == 3),
                          inc=(("ts", 1) if ic == 3 else None))
                    ts_y = C["ts"]
                    W("vector", "ts", ts_y)
                    if half == 0:
                        W("vector", go_last[i])
                    O("vector", "tensor_copy",
                      ystage[i][:, half * 512:half * 512 + 512], psYg[i],
                      inc=("vs", 1))
                    vs_evac[(gi, half * 2 + i)] = C["vs"]
                    if half == 1:
                        W("sync", "vs", C["vs"])
                        O("sync", "dma_start",
                          out=d_ybig[gr * GTOK + i * 128:gr * GTOK + (i + 1) * 128, :],
                          in_=ystage[i], inc=("go%d" % i, 16))
                        go_last[i] = kb.mark("go%d" % i)
            ts_dn_done[gi] = C["ts"]

    for gi in range(NG):
        kind = units[gi]
        if kind[0] == "sh" and kind[2] == 0 and kind[1] + 1 < NST:
            st = kind[1]
            if st >= 1:
                W("sync", "ts", last_gu_of_st[st - 1])
            O("sync", "dma_start", out=xnst[(st + 1) % 2][:, :, :, :],
              in_=d_xnscr[:, (st + 1) * ST_TT:(st + 2) * ST_TT, :, :],
              inc=("lt%d" % ((st + 1) % 2), 16))
            ld_xnst[st + 1] = kb.mark("lt%d" % ((st + 1) % 2))
        if gi >= 1:
            emit_dn(gi - 1)
            if gi + 1 < NG:
                W("gpsimd", "ts", ts_dn_done[gi - 1])
                load_dn(gi + 1)
        if kind[0] == "gr" and kind[1] < 2:
            load_granule(gi)
        emit_gu(gi)
        if gi + 2 < NG:
            W("gpsimd", "ts", ts_gu_done[gi])
            load_gu(gi + 2)
            if units[gi + 2][0] == "gr" and units[gi + 2][1] >= 2:
                load_granule(gi + 2)
    emit_dn(NG - 1)
    L["go_all"] = [kb.mark("go0"), kb.mark("go1")]
    L["st_base"] = kb.mark("st")
    emit_final(kb, L)


def emit_final(kb, L):
    W, O, C = kb.W, kb.O, kb.cnt
    d_y, d_ybig = L["d_y"], L["d_ybig"]
    yb, outb, topkw = L["yb_sb"], L["outb"], L["topkw"]
    for m in L["go_all"]:
        W("gpsimd", m)
        W("sync", m)
    W("sync", L["st_base"])
    ts_all = kb.mark("ts")
    W("gpsimd", ts_all)
    vs_red = {}
    st_out = {}
    gk_m = {}
    la_m = {}
    for tt in range(NT):
        par = tt % 4
        if tt >= 4:
            W("gpsimd", "vs", vs_red[tt - 4])
        for k in range(8):
            O("gpsimd", "indirect_dma_start",
              out=yb[par][:, k * 1024:(k + 1) * 1024], out_offset=None,
              in_=d_ybig[:, :],
              in_offset=bass.IndirectOffsetOnAxis(
                  L["srowi"][:, tt * 8 + k:tt * 8 + k + 1], 0),
              inc=("gf%d" % (par % 2), 16))
            gk_m[(tt, k)] = kb.mark("gf%d" % (par % 2))
        if tt >= 2:
            W("sync", "vs", vs_red[tt - 2])
        O("sync", "dma_start", out=L["bstage"][tt % 2],
          in_=d_y[tt * 128:(tt + 1) * 128, :], inc=("la%d" % (tt % 2), 16))
        la_m[tt] = kb.mark("la%d" % (tt % 2))
        W("vector", la_m[tt])
        if tt >= 2:
            W("vector", st_out[tt - 2])
        W("vector", gk_m[(tt, 7)])
        for k in range(8):
            src = L["bstage"][tt % 2] if k == 0 else outb[tt % 2]
            W("vector", "vs")
            O("vector", "scalar_tensor_tensor", outb[tt % 2],
              yb[par][:, k * 1024:(k + 1) * 1024],
              topkw[:, tt * 8 + k:tt * 8 + k + 1], src,
              ALU.mult, ALU.add, inc=("vs", 1))
        vs_red[tt] = C["vs"]
        W("sync", "vs", C["vs"])
        O("sync", "dma_start", out=d_y[tt * 128:(tt + 1) * 128, :], in_=outb[tt % 2],
          inc=("st", 16))
        st_out[tt] = kb.mark("st")



def _prep(inputs):
    x = np.ascontiguousarray(np.asarray(inputs["L0_attn_encodings"], dtype=np.float32)
                             .reshape(NCORES, T, H))
    aw = np.asarray(inputs["L0_attn_attn_w"], dtype=np.float32)
    ab = np.asarray(inputs["L0_attn_attn_b"], dtype=np.float32)
    ow = np.asarray(inputs["L0_attn_out_w"], dtype=np.float32)
    obv = np.asarray(inputs["L0_attn_out_b"], dtype=np.float32)
    rww = np.asarray(inputs["L0_ffn_router_w"], dtype=np.float32)
    ebv = np.asarray(inputs["L0_ffn_e_bias"], dtype=np.float32)
    Wg = np.asarray(inputs["L0_ffn_Wg"], dtype=np.float32)
    Wu = np.asarray(inputs["L0_ffn_Wu"], dtype=np.float32)
    Wd = np.asarray(inputs["L0_ffn_Wd"], dtype=np.float32)
    SWg = np.asarray(inputs["L0_ffn_SWg"], dtype=np.float32)
    SWu = np.asarray(inputs["L0_ffn_SWu"], dtype=np.float32)
    SWd = np.asarray(inputs["L0_ffn_SWd"], dtype=np.float32)

    a3 = aw.reshape(16, 3, 64, H)
    WQ = a3[:, 0].reshape(H, H)
    WK = a3[:, 1].reshape(H, H)
    WV = a3[:, 2].reshape(H, H)
    b3 = ab.reshape(16, 3, 64)
    bq = b3[:, 0].reshape(H)
    bk = b3[:, 1].reshape(H)
    bv = b3[:, 2].reshape(H)

    def whp(Wm):
        return np.ascontiguousarray(Wm.reshape(8, 128, 8, 128).transpose(0, 3, 2, 1))

    wq = whp(WQ)
    wk = whp(WK)
    wv = whp(WV)
    wo = np.ascontiguousarray(ow.reshape(H, 8, 128).transpose(2, 1, 0))
    rw = np.ascontiguousarray(rww.reshape(64, 8, 128).transpose(2, 1, 0))
    bqk = np.ascontiguousarray(
        np.concatenate([bq.reshape(8, 128).T, bk.reshape(8, 128).T], axis=1))
    vb = np.ascontiguousarray(np.broadcast_to(bv, (128, H)))
    ob = np.ascontiguousarray(np.broadcast_to(obv, (128, H)))
    eb = np.ascontiguousarray(np.broadcast_to(ebv, (128, 64)))
    i64 = np.ascontiguousarray(
        np.broadcast_to(np.arange(64, dtype=np.float32) - 127.0, (128, 64)))
    i8 = np.ascontiguousarray(
        np.broadcast_to(np.arange(8, dtype=np.float32) - 127.0, (128, 8)))
    idf = np.eye(128, dtype=np.float32)
    idb = np.eye(128).astype(ml_dtypes.bfloat16)
    pr = np.arange(128)
    ntri = np.where(pr[None, :] <= pr[:, None], 0.0, -10000.0).astype(np.float32)

    swg_u = SWg.reshape(8, 128, 2, 512).transpose(2, 1, 0, 3)
    swu_u = SWu.reshape(8, 128, 2, 512).transpose(2, 1, 0, 3)
    swd_u = SWd.reshape(2, 4, 128, H).transpose(0, 2, 1, 3)
    wg_e = Wg.reshape(64, 8, 128, 512).transpose(0, 2, 1, 3)
    wu_e = Wu.reshape(64, 8, 128, 512).transpose(0, 2, 1, 3)
    wd_e = Wd.reshape(64, 4, 128, H).transpose(0, 2, 1, 3)
    wg = np.ascontiguousarray(np.concatenate([swg_u, wg_e], axis=0))
    wu = np.ascontiguousarray(np.concatenate([swu_u, wu_e], axis=0))
    wd = np.ascontiguousarray(np.concatenate([swd_u, wd_e], axis=0))

    shared = dict(wq=wq, wk=wk, wv=wv, wo=wo, rw=rw, bqk=bqk, vb=vb, ob=ob,
                  eb=eb, i64=i64, i8=i8, idf=idf, idb=idb, ntri=ntri)
    if KSTAGE not in ("ln", "attn", "route"):
        shared.update(wg=wg, wu=wu, wd=wd)
    if KMOE == "sparse" and KSTAGE == "full":
        kk = np.arange(128)
        ustr = (kk[:, None] < kk[None, :]).astype(np.float32)
        onesm = np.ones((128, 128), np.float32)
        eiv = np.ascontiguousarray(
            np.broadcast_to(np.arange(64, dtype=np.float32), (128, 64)))
        gio = np.ascontiguousarray(
            np.broadcast_to(np.arange(NGRAN, dtype=np.float32), (128, NGRAN)))
        iop = np.arange(128, dtype=np.float32).reshape(128, 1).copy()
        shared.update(ustr=ustr, onesm=onesm, ei=eiv, gio=gio, iop=iop)
    in_maps = []
    for c in range(NCORES):
        m = dict(shared)
        m["x"] = np.ascontiguousarray(x[c])
        in_maps.append(m)
    return in_maps


def kernel(**inputs):
    in_maps = _prep(inputs)
    nc = build_program()
    res = run_bass_kernel_spmd(nc, in_maps, core_ids=list(range(NCORES)))
    out = np.concatenate([np.asarray(r["y"], dtype=np.float32) for r in res.results],
                         axis=0)
    return out.reshape(16, 2048, 1024)


# version 22
# speedup vs baseline: 1.1166x; 1.0055x over previous
import os
import numpy as np
import ml_dtypes
import concourse.bass as bass
import concourse.mybir as mybir
from concourse.bass_utils import run_bass_kernel_spmd
from contextlib import ExitStack

F32 = mybir.dt.float32
BF16 = mybir.dt.bfloat16
AF = mybir.ActivationFunctionType
ALU = mybir.AluOpType
AX = mybir.AxisListType

NCORES = 8
H = 1024
T = 4096
NT = 32
NTS = 16
NST = 4
ST_TT = 8
FC = 8
NHP = 8
NU = 66
NEG = -1e30

KSTAGE = os.environ.get("KSTAGE", "full")
KMOE = os.environ.get("KMOE", "sparse")
NGRAN = 192
GTOK = 256
NSLAB = NGRAN * GTOK
KNT = int(os.environ.get("KNT", "0"))

ENGS = ("sync", "scalar", "vector", "tensor", "gpsimd")
SEMS = ("ts", "vs", "cs",
        "lc", "lx0", "lx1", "lxr", "lt0", "lt1", "la0", "la1",
        "gw0", "gw1", "gwo", "gm0", "gm1", "gn0", "gn1", "sx", "sr0", "sr1", "st",
        "sy0", "sy1", "sz", "gs0", "gs1", "gp0", "gp1", "ux0", "ux1",
        "ua0", "ua1", "ud0", "ud1", "go0", "go1", "pcw",
        "gg0", "gg1", "gh0", "gh1", "gf0", "gf1")


class KB:
    def __init__(self):
        self.ops = {e: [] for e in ENGS}
        self.cnt = {}
        self.sems = {}

    def sem(self, stack, nc, name):
        self.sems[name] = stack.enter_context(nc.semaphore(name))
        self.cnt[name] = 0

    def W(self, eng, sem, val=None):
        if isinstance(sem, tuple):
            sem, val = sem
        if val is None:
            val = self.cnt[sem]
        if not val or val <= 0:
            return
        s = self.sems[sem]
        self.ops[eng].append(lambda e, s=s, v=val: e.wait_ge(s, v))

    def O(self, eng, fname, *args, inc=None, **kw):
        semh, amt = None, 0
        if inc is not None:
            nm, amt = inc
            self.cnt[nm] += amt
            semh = self.sems[nm]

        def ap(a):
            if a is None or isinstance(a, (int, float, bass.AP)):
                return a
            if hasattr(a, "shape") and hasattr(a, "__getitem__"):
                return a[tuple(slice(None) for _ in a.shape)]
            return a

        args = tuple(ap(a) for a in args)
        kw = {k: (ap(v) if k in ("out", "in_", "bias", "scale", "accum_out") else v)
              for k, v in kw.items()}

        def f(e, fname=fname, args=args, kw=kw, semh=semh, amt=amt):
            ins = getattr(e, fname)(*args, **kw)
            if semh is not None:
                ins.then_inc(semh, amt)
        self.ops[eng].append(f)

    def mark(self, sem):
        return (sem, self.cnt[sem])

    def barrier(self):
        snap = {s: self.cnt[s] for s in SEMS}
        for e in ENGS:
            for s in SEMS:
                self.W(e, s, snap[s])


def prod(xs):
    r = 1
    for v in xs:
        r *= v
    return r


def _view(ap, shape):
    if len(shape) == 2:
        return ap
    if len(shape) == 3:
        return ap.rearrange("p (a b) -> p a b", a=shape[1])
    return ap.rearrange("p (a b c) -> p a b c", a=shape[1], b=shape[2])


def build_program():
    nc = bass.Bass("TRN2", target_bir_lowering=False)

    L = {}
    L["d_x"] = nc.dram_tensor("x", [T, H], F32, kind="ExternalInput")
    L["d_wq"] = nc.dram_tensor("wq", [NHP, 128, FC, 128], F32, kind="ExternalInput")
    L["d_wk"] = nc.dram_tensor("wk", [NHP, 128, FC, 128], F32, kind="ExternalInput")
    L["d_wv"] = nc.dram_tensor("wv", [NHP, 128, FC, 128], F32, kind="ExternalInput")
    L["d_wo"] = nc.dram_tensor("wo", [128, FC, H], F32, kind="ExternalInput")
    L["d_rw"] = nc.dram_tensor("rw", [128, FC, 64], F32, kind="ExternalInput")
    L["d_bqk"] = nc.dram_tensor("bqk", [128, 16], F32, kind="ExternalInput")
    L["d_vb"] = nc.dram_tensor("vb", [128, H], F32, kind="ExternalInput")
    L["d_ob"] = nc.dram_tensor("ob", [128, H], F32, kind="ExternalInput")
    L["d_eb"] = nc.dram_tensor("eb", [128, 64], F32, kind="ExternalInput")
    L["d_i64"] = nc.dram_tensor("i64", [128, 64], F32, kind="ExternalInput")
    L["d_i8"] = nc.dram_tensor("i8", [128, 8], F32, kind="ExternalInput")
    L["d_idf"] = nc.dram_tensor("idf", [128, 128], F32, kind="ExternalInput")
    L["d_idb"] = nc.dram_tensor("idb", [128, 128], BF16, kind="ExternalInput")
    L["d_ntri"] = nc.dram_tensor("ntri", [128, 128], F32, kind="ExternalInput")
    if KSTAGE not in ("ln", "attn", "route"):
        L["d_wg"] = nc.dram_tensor("wg", [NU, 128, FC, 512], F32, kind="ExternalInput")
        L["d_wu"] = nc.dram_tensor("wu", [NU, 128, FC, 512], F32, kind="ExternalInput")
        L["d_wd"] = nc.dram_tensor("wd", [NU, 128, 4, 1024], F32, kind="ExternalInput")
    if KMOE == "sparse" and KSTAGE == "full":
        L["d_ustr"] = nc.dram_tensor("ustr", [128, 128], F32, kind="ExternalInput")
        L["d_onesm"] = nc.dram_tensor("onesm", [128, 128], F32, kind="ExternalInput")
        L["d_ei"] = nc.dram_tensor("ei", [128, 64], F32, kind="ExternalInput")
        L["d_gio"] = nc.dram_tensor("gio", [128, NGRAN], F32, kind="ExternalInput")
        L["d_iop"] = nc.dram_tensor("iop", [128, 1], F32, kind="ExternalInput")
        L["d_xntm"] = nc.dram_tensor("xntm", [T, H], BF16, kind="Internal")
        L["d_slab"] = nc.dram_tensor("slab", [NSLAB, H], BF16, kind="Internal")
        L["d_ybig"] = nc.dram_tensor("ybig", [NSLAB, H], BF16, kind="Internal")
        L["d_wgub"] = nc.dram_tensor("wgub", [64 * 128, 8192], BF16, kind="Internal")
        L["d_wdb"] = nc.dram_tensor("wdb", [64 * 128, 4096], BF16, kind="Internal")
    L["d_xnscr"] = nc.dram_tensor("xnscr", [128, NT, FC, 128], BF16, kind="Internal")
    L["d_rscr"] = nc.dram_tensor("rscr", [NT, 128, H], F32, kind="Internal")
    L["d_y"] = nc.dram_tensor("y", [T, H], F32, kind="ExternalOutput")

    kb = KB()
    with ExitStack() as stk:
        def sb(name, shape, dt=F32):
            return stk.enter_context(nc.sbuf_tensor(name, shape, dt))

        for nm, shape, dt in (
            ("rw_sb", [128, FC * 64], F32), ("bqk_sb", [128, 16], F32),
            ("vb_sb", [128, H], F32), ("ob_sb", [128, H], F32),
            ("eb_sb", [128, 64], F32), ("i64_sb", [128, 64], F32),
            ("i8_sb", [128, 8], F32), ("idf_sb", [128, 128], F32),
            ("idb_sb", [128, 128], BF16), ("ntri_sb", [128, 128], F32),
            ("scores", [128, NT * 64], F32), ("coef", [128, NT * 64], F32),
            ("sfc", [128, 1024], F32), ("eqb", [128, 1024], F32),
            ("idxb", [128, 1024], F32),
            ("gm1", [128, 128], F32), ("gcnt", [128, 128], F32),
            ("gm2r", [128, 128], F32), ("gge", [128, 128], F32),
            ("gdd", [128, 128], F32), ("gsb", [128, 128], F32),
            ("gmk", [128, 128], F32), ("grA", [128, 128], F32),
            ("grB", [128, 128], F32),
            ("m16", [128, 16], F32), ("m16b", [128, 16], F32),
            ("ws16", [128, 16], F32),
            ("mxc", [128, 8], F32),
            ("ns2", [128, 2], F32), ("nm2", [128, 2], F32), ("mm2", [128, 2], F32),
            ("var2", [128, 2], F32), ("sd2", [128, 2], F32), ("rr2", [128, 2], F32),
            ("sq2", [128, 2], F32), ("nmx2", [128, 2], F32), ("rs2", [128, 2], F32),
            ("rc2", [128, 2], F32),
            ("topke", [128, 256], F32), ("topkw", [128, 256], F32),
            ("rk8", [128, 256], F32), ("rankb2", [128, 2048], F32),
            ("ustr_sb", [128, 128], F32), ("onesm_sb", [128, 128], F32),
            ("ei_sb", [128, 64], F32), ("gio_sb", [128, NGRAN], F32),
            ("iop_sb", [128, 1], F32),
        ):
            L[nm] = sb(nm, shape, dt)

        AR = 73728
        arena = sb("arena", [128, AR], BF16)

        def cbf(off, shape):
            return _view(arena[:, off:off + prod(shape[1:])], shape)

        def cf32(off, shape):
            n = prod(shape[1:])
            return _view(arena[:, off:off + 2 * n].bitcast(F32), shape)

        L["xn_sb"] = cbf(0, [128, NTS, FC, 128])
        L["wo_bf"] = cbf(16384, [128, FC * H])
        L["wq_bf"] = [cbf(24576 + i * 1024, [128, 1024]) for i in range(2)]
        L["wk_bf"] = [cbf(26624 + i * 1024, [128, 1024]) for i in range(2)]
        L["wv_bf"] = [cbf(28672 + i * 1024, [128, 1024]) for i in range(2)]
        L["qf"] = [cbf(30720 + i * 2048, [128, 2048]) for i in range(2)]
        L["kf"] = [cbf(34816 + i * 2048, [128, 2048]) for i in range(2)]
        L["vt"] = [cbf(38912 + i * 2048, [128, 2048]) for i in range(2)]
        L["Pb2"] = [cbf(43008, [128, 2048]), cbf(63488, [128, 2048])]
        L["PT2"] = [cbf(45056, [128, 2048]), cbf(71680, [128, 2048])]
        L["comb"] = cbf(47104, [128, NHP, NTS, 128])
        L["xn_stage"] = [cbf(63488 + i * 1024, [128, FC, 128]) for i in range(2)]
        L["x_tile"] = [cf32(65536 + i * 2048, [128, 1024]) for i in range(2)]
        L["xn_f32"] = cf32(69632, [128, 1024])
        L["xnT"] = cf32(71680, [128, 1024])
        L["R_dbg"] = [cf32(30720 + i * 2048, [128, 1024]) for i in range(2)]
        L["xtm"] = [cbf(69632 + i * 1024, [128, 1024]) for i in range(2)]
        L["srowi"] = L["eqb"][:, 0:256].bitcast(mybir.dt.int32)
        L["bs8"] = L["eqb"][:, 512:768]
        L["qb"] = L["sfc"][:, 0:256]
        L["rb2"] = L["sfc"][:, 256:512]
        L["srow"] = L["sfc"][:, 512:768]
        L["prow"] = L["sfc"][:, 768:1024]
        L["windexi"] = L["idxb"][:, 0:NGRAN].bitcast(mybir.dt.int32)
        L["xtm_ln"] = [cbf(43008 + i * 1024, [128, 1024]) for i in range(2)]
        L["acc"] = cf32(0, [128, ST_TT * 1024])
        L["xnst"] = [cbf(16384 + i * 8192, [128, ST_TT, FC, 128]) for i in range(2)]
        L["hbuf"] = [cbf(32768 + i * 4096, [128, 4096]) for i in range(2)]
        L["wg_bf"] = [cbf(40960 + i * 8192, [128, 4096]) for i in range(2)]
        L["wu_bf"] = [cbf(45056 + i * 8192, [128, 4096]) for i in range(2)]
        L["wgu_bf"] = [cbf(40960 + i * 8192, [128, 8192]) for i in range(2)]
        L["wd_bf"] = [cbf(57344 + i * 4096, [128, 4096]) for i in range(2)]
        L["sG"] = [cf32(65536 + i * 1024, [128, 512]) for i in range(2)]
        L["R2"] = [cf32(69632 + i * 2048, [128, 1024]) for i in range(2)]
        L["xnT_g"] = [cbf(i * 2048, [128, FC * GTOK]) for i in range(2)]
        L["hbuf2"] = [cbf(4096 + i * 1024, [128, 4 * GTOK]) for i in range(2)]
        L["ystage"] = [cbf(6144 + i * 1024, [128, 1024]) for i in range(2)]
        L["bstage"] = [cf32(11264 + i * 2048, [128, 1024]) for i in range(2)]
        L["xs"] = [cbf(16384 + i * 1024, [128, 1024]) for i in range(4)]
        L["yb_sb"] = [cbf(32768 + i * 8192, [128, 8192]) for i in range(4)]
        L["outb"] = [cf32(65536 + i * 2048, [128, 1024]) for i in range(2)]
        L["zini"] = cbf(47104, [128, 8192])

        psa = stk.enter_context(nc.psum_tensor("psa", [128, 4096], F32))
        L["psa_raw"] = psa
        L["pstr"] = [psa[:, 0:128], psa[:, 512:640]]
        L["psr"] = psa[:, 1024:1088]
        L["psS"] = psa[:, 0:2048]
        L["psq"] = psa[:, 2048:2560]
        L["psk"] = psa[:, 2560:3072]
        L["psVC"] = psa[:, 3072:3584]
        L["psT"] = psa[:, 3584:4096].bitcast(BF16)[:, 0:512]
        L["psG"] = [psa[:, 0:512], psa[:, 512:1024]]
        L["psU"] = [psa[:, 1024:1536], psa[:, 1536:2048]]
        L["psY"] = [psa[:, 2048:3072], psa[:, 3072:4096]]
        L["psYg"] = [psa[:, 2048:2560], psa[:, 2560:3072]]
        L["psTb"] = [psa[:, 3072:3584].bitcast(BF16),
                     psa[:, 3584:4096].bitcast(BF16)]

        for s in SEMS:
            kb.sem(stk, nc, s)

        emit_phase0(kb, L)
        emit_ln(kb, L)
        if KSTAGE != "ln":
            sparse = (KSTAGE == "full" and KMOE == "sparse")
            if sparse:
                L["hz"] = {}
                L["prep_gen"] = gen_sparse_prep(kb, L, L["hz"])
            emit_attention(kb, L)
            if sparse:
                g = L.get("prep_gen")
                while g is not None:
                    try:
                        next(g)
                    except StopIteration:
                        g = None
                        L["prep_gen"] = None
            else:
                emit_routing(kb, L)
            if KSTAGE == "route":
                emit_route_debug(kb, L)
            if KSTAGE == "full":
                kb.barrier()
                if sparse:
                    emit_moe_sparse(kb, L)
                else:
                    emit_moe(kb, L)
        for s_ in ("st", "sr0", "sr1", "sx"):
            kb.W("sync", s_, kb.cnt[s_])
            kb.W("gpsimd", s_, kb.cnt[s_])

        with nc.Block() as block:
            @block.sync
            def _(eng):
                for f in kb.ops["sync"]:
                    f(eng)

            @block.scalar
            def _(eng):
                for f in kb.ops["scalar"]:
                    f(eng)

            @block.vector
            def _(eng):
                for f in kb.ops["vector"]:
                    f(eng)

            @block.tensor
            def _(eng):
                for f in kb.ops["tensor"]:
                    f(eng)

            @block.gpsimd
            def _(eng):
                for f in kb.ops["gpsimd"]:
                    f(eng)

    return nc


def emit_phase0(kb, L):
    W, O, C = kb.W, kb.O, kb.cnt
    for dst, src in ((L["rw_sb"], L["d_rw"][:, :, :]), (L["bqk_sb"], L["d_bqk"][:, :]),
                     (L["vb_sb"], L["d_vb"][:, :]), (L["ob_sb"], L["d_ob"][:, :]),
                     (L["eb_sb"], L["d_eb"][:, :]), (L["i64_sb"], L["d_i64"][:, :]),
                     (L["i8_sb"], L["d_i8"][:, :]), (L["idf_sb"], L["d_idf"][:, :]),
                     (L["idb_sb"], L["d_idb"][:, :]), (L["ntri_sb"], L["d_ntri"][:, :])):
        O("sync", "dma_start", out=dst[:, :], in_=src, inc=("lc", 16))
    if KMOE == "sparse" and KSTAGE == "full":
        for dst, src in ((L["ustr_sb"], L["d_ustr"][:, :]), (L["onesm_sb"], L["d_onesm"][:, :]),
                         (L["ei_sb"], L["d_ei"][:, :]), (L["gio_sb"], L["d_gio"][:, :]),
                         (L["iop_sb"], L["d_iop"][:, :])):
            O("sync", "dma_start", out=dst[:, :], in_=src, inc=("lc", 16))
    L["lc_all"] = kb.mark("lc")
    if KSTAGE != "ln":
        O("gpsimd", "dma_start",
          out=_view(L["wo_bf"], [128, FC, H]), in_=L["d_wo"][:, :, :], inc=("gwo", 16))
        L["gwo_all"] = kb.mark("gwo")
    if KMOE == "sparse" and KSTAGE == "full":
        chunks = [(L["d_wg"], L["d_wgub"][:, 0:4096]),
                  (L["d_wu"], L["d_wgub"][:, 4096:8192]),
                  (L["d_wd"], L["d_wdb"][:, :])]
        pc_jobs = []
        for src, dst in chunks:
            s2 = src[:, :, :, :].rearrange("u p a b -> (u p) (a b)")
            for e2 in range(32):
                pc_jobs.append((dst[(2 * e2) * 128:(2 * e2 + 2) * 128, :],
                                s2[(2 + 2 * e2) * 128:(4 + 2 * e2) * 128, :]))
        pc_jobs = [pc_jobs[j + t * 32] for j in range(32) for t in range(3)]

        def pc_emit(n):
            for _ in range(n):
                if pc_jobs:
                    dst, src = pc_jobs.pop(0)
                    O("gpsimd", "dma_start", out=dst, in_=src, inc=("pcw", 16))
        L["pc_emit"] = pc_emit
        L["pcw_all"] = ("pcw", 96 * 16)
        gd01 = []
        for pi2 in range(2):
            g = "gw%d" % (pi2 % 2)
            for dd, bb in ((L["d_wq"], L["wq_bf"]), (L["d_wk"], L["wk_bf"]),
                           (L["d_wv"], L["wv_bf"])):
                O("gpsimd", "dma_start",
                  out=_view(bb[pi2 % 2], [128, FC, 128]), in_=dd[pi2, :, :, :],
                  inc=(g, 16))
            gd01.append(kb.mark(g))
        L["gd_w01"] = gd01
        pc_emit(8)
        emit_sparse_init(kb, L)


def emit_ln(kb, L):
    W, O, C = kb.W, kb.O, kb.cnt
    d_x, d_xnscr, d_y = L["d_x"], L["d_xnscr"], L["d_y"]
    x_tile, xn_f32, xnT, R_sb = L["x_tile"], L["xn_f32"], L["xnT"], L["R_dbg"]
    xn_sb, xn_stage = L["xn_sb"], L["xn_stage"]
    pstr, psr = L["pstr"], L["psr"]
    rw_sb, scores = L["rw_sb"], L["scores"]

    O("sync", "dma_start", out=x_tile[0], in_=d_x[0:128, :], inc=("lx0", 16))
    ld_x = [kb.mark("lx0")]
    cs_tile_end = []
    vs_tile_end = []
    ts_tile_end = []
    st_dbg = []
    sx_marks = []
    sy_marks = []
    vs_xtm = []

    ln_nt = KNT if (KNT and KSTAGE == "ln") else NT
    for tt in range(ln_nt):
        par = tt % 2
        seq = tt // NTS
        tloc = tt % NTS
        if tt + 1 < NT:
            if tt >= 1:
                W("sync", "cs", cs_tile_end[tt - 1])
                W("sync", "vs", vs_tile_end[tt - 1])
            O("sync", "dma_start", out=x_tile[(tt + 1) % 2],
              in_=d_x[(tt + 1) * 128:(tt + 2) * 128, :], inc=("lx%d" % ((tt + 1) % 2), 16))
            ld_x.append(kb.mark("lx%d" % ((tt + 1) % 2)))

        nsv = L["ns2"][:, par:par + 1]
        nmv = L["nm2"][:, par:par + 1]
        mmv = L["mm2"][:, par:par + 1]
        vav = L["var2"][:, par:par + 1]
        sdv = L["sd2"][:, par:par + 1]
        rrv = L["rr2"][:, par:par + 1]
        sqv = L["sq2"][:, par:par + 1]

        W("vector", ld_x[tt])
        O("vector", "reduce_sum", nsv, x_tile[par], axis=AX.X, negate=True, inc=("vs", 1))
        W("vector", "vs")
        O("vector", "tensor_scalar_mul", nmv, nsv, 1.0 / H, inc=("vs", 1))
        W("vector", "vs")
        O("vector", "tensor_scalar", mmv, nmv, nmv, -1e-5, ALU.mult, ALU.add,
          inc=("vs", 1))
        vs_m2 = C["vs"]
        W("scalar", ld_x[tt])
        if tt >= 1:
            W("scalar", "ts", ts_tile_end[tt - 1])
            if KSTAGE == "ln":
                W("scalar", "vs", vs_tile_end[tt - 1] + 1)
            if KMOE == "sparse" and KSTAGE == "full":
                W("scalar", "vs", vs_xtm[tt - 1])
        O("scalar", "activation", xn_f32, x_tile[par], AF.Square,
          accum_out=sqv, inc=("cs", 1))
        W("vector", "cs")
        W("vector", "vs", vs_m2)
        O("vector", "scalar_tensor_tensor", vav, sqv, 1.0 / H, mmv,
          ALU.mult, ALU.subtract, inc=("vs", 1))
        W("scalar", "vs")
        O("scalar", "activation", sdv, vav, AF.Sqrt, inc=("cs", 1))
        W("vector", "cs")
        O("vector", "reciprocal", rrv, sdv, inc=("vs", 1))
        W("vector", "vs")
        O("vector", "tensor_scalar", xn_f32, x_tile[par], nmv, rrv,
          ALU.add, ALU.mult, inc=("vs", 1))
        vs_xn = C["vs"]
        if KMOE == "sparse" and KSTAGE == "full":
            if tt >= 2:
                W("vector", sy_marks[tt - 2])
            W("vector", "vs", vs_xn)
            O("vector", "tensor_copy", L["xtm_ln"][par], xn_f32, inc=("vs", 1))
            vs_xtm.append(C["vs"])
            W("sync", "vs", C["vs"])
            O("sync", "dma_start", out=L["d_xntm"][tt * 128:(tt + 1) * 128, :],
              in_=L["xtm_ln"][par], inc=("sy%d" % par, 16))
            sy_marks.append(kb.mark("sy%d" % par))

        cs_copy = []
        vs_copy = []
        if tt == 0:
            W("tensor", L["lc_all"])
        if seq == 1 and tt >= NTS + 2:
            W("vector", "sx", sx_marks[tt - 1][1])
        for c in range(FC):
            W("tensor", "vs", vs_xn)
            if c >= 2:
                W("tensor", "vs", cs_copy[c - 2])
            elif tt >= 1:
                W("tensor", "cs", cs_tile_end[tt - 1])
                W("tensor", "vs", vs_tile_end[tt - 1])
            O("tensor", "transpose", pstr[c % 2], xn_f32[:, c * 128:(c + 1) * 128],
              L["idf_sb"][:, :], inc=("ts", 1))
            ts_tr = C["ts"]
            W("vector", "ts", ts_tr)
            O("vector", "tensor_copy", xnT[:, c * 128:(c + 1) * 128], pstr[c % 2],
              inc=("vs", 1))
            cs_copy.append(C["vs"])
            W("vector", "vs", cs_copy[-1])
            dst = xn_sb[:, tloc, c, :] if seq == 0 else xn_stage[par][:, c, :]
            O("vector", "tensor_copy", dst, xnT[:, c * 128:(c + 1) * 128], inc=("vs", 1))
            vs_copy.append(C["vs"])
        W("tensor", "vs", cs_copy[-1])
        if tt >= 1:
            W("tensor", "cs", cs_tile_end[tt - 1])
        W("tensor", L["lc_all"])
        for c in range(FC):
            O("tensor", "matmul", psr, xnT[:, c * 128:(c + 1) * 128],
              rw_sb[:, c * 64:(c + 1) * 64], start=(c == 0), stop=(c == FC - 1),
              inc=(("ts", 1) if c == FC - 1 else None))
        ts_tile_end.append(C["ts"])
        W("scalar", "ts")
        O("scalar", "copy", scores[:, tt * 64:(tt + 1) * 64], psr, inc=("cs", 1))
        cs_tile_end.append(C["cs"])
        W("sync", "vs", vs_copy[-1])
        src = xn_sb[:, tloc, :, :] if seq == 0 else xn_stage[par][:, :, :]
        O("sync", "dma_start", out=d_xnscr[:, tt, :, :], in_=src, inc=("sx", 16))
        sx_marks.append(kb.mark("sx"))
        vs_tile_end.append(C["vs"])

        if KSTAGE == "ln":
            if tt >= 2:
                W("vector", st_dbg[tt - 2])
            O("vector", "tensor_copy", R_sb[par], xn_f32, inc=("vs", 1))
            W("sync", "vs")
            O("sync", "dma_start", out=d_y[tt * 128:(tt + 1) * 128, :],
              in_=R_sb[par], inc=("sr%d" % par, 16))
            st_dbg.append(kb.mark("sr%d" % par))
        else:
            st_dbg.append(None)

    L["ln_cs"] = C["cs"]
    L["ln_vs"] = C["vs"]
    L["ln_ts"] = C["ts"]
    L["sx_all"] = kb.mark("sx")


def emit_attention(kb, L):
    W, O, C = kb.W, kb.O, kb.cnt
    d_x, d_wq, d_wk, d_wv = L["d_x"], L["d_wq"], L["d_wk"], L["d_wv"]
    d_xnscr, d_rscr, d_y = L["d_xnscr"], L["d_rscr"], L["d_y"]
    xn_sb, wq_bf, wk_bf, wv_bf, wo_bf = (L["xn_sb"], L["wq_bf"], L["wk_bf"],
                                         L["wv_bf"], L["wo_bf"])
    qf, kf, vt, Pb2, PT2, comb = L["qf"], L["kf"], L["vt"], L["Pb2"], L["PT2"], L["comb"]
    x_tile = L["x_tile"]
    bqk_sb, vb_sb, ob_sb, ntri_sb, idb_sb = (L["bqk_sb"], L["vb_sb"], L["ob_sb"],
                                             L["ntri_sb"], L["idb_sb"])
    psS, psq, psk, psVC, psT = L["psS"], L["psq"], L["psk"], L["psVC"], L["psT"]
    nmx2, rs2, rc2 = L["nmx2"], L["rs2"], L["rc2"]

    pairs = [(s, hp) for s in range(2) for hp in range(NHP)]
    gd_w = {}
    W("vector", L["lc_all"])
    W("scalar", L["sx_all"])
    if KSTAGE == "full" and KMOE == "sparse":
        W("scalar", ("sy0", kb.cnt["sy0"]))
        W("scalar", ("sy1", kb.cnt["sy1"]))
        W("scalar", L["sz_all"])

    def load_w(pi):
        s2, hp2 = pairs[pi]
        g = "gw%d" % (hp2 % 2)
        for dd, bb in ((d_wq, wq_bf), (d_wk, wk_bf), (d_wv, wv_bf)):
            O("gpsimd", "dma_start",
              out=_view(bb[hp2 % 2], [128, FC, 128]), in_=dd[hp2, :, :, :],
              inc=(g, 16))
        gd_w[pi] = kb.mark(g)

    if "gd_w01" in L:
        gd_w[0], gd_w[1] = L["gd_w01"]
    else:
        load_w(0)
        load_w(1)

    ts_qkv_done = {}
    ts_S_done = {}
    ts_pv_done = {}
    cs_last_exp = 0
    vs_last_psS = L["ln_vs"]
    cs_last_pt = 0
    cs_last_psc = 0
    prev_vs_qk = 0
    vs_R = []
    sr_R = []

    for pi, (s, hp) in enumerate(pairs):
        par = hp % 2
        if s == 1 and hp == 0:
            W("sync", "ts", ts_qkv_done[(0, NHP - 1)])
            W("sync", L["sx_all"])
            O("sync", "dma_start", out=xn_sb[:, :, :, :],
              in_=d_xnscr[:, NTS:NT, :, :], inc=("lxr", 16))
            L["lxr_all"] = kb.mark("lxr")

        W("tensor", gd_w[pi])
        if s == 0:
            W("tensor", "vs", L["ln_vs"])
            W("tensor", "cs", L["ln_cs"])
        else:
            W("tensor", L["lxr_all"])
        key2 = pairs[pi - 2] if pi >= 2 else None
        for tc in range(4):
            if tc > 0:
                W("tensor", "vs", C["vs"])
            else:
                W("tensor", "vs", max(prev_vs_qk, L.get("vs_hook", 0)))
            for c in range(FC):
                O("tensor", "matmul", psq, wq_bf[par][:, c * 128:(c + 1) * 128],
                  xn_sb[:, tc * 4:(tc + 1) * 4, c, :],
                  start=(c == 0), stop=(c == FC - 1),
                  inc=(("ts", 1) if c == FC - 1 else None))
            ts_q = C["ts"]
            for c in range(FC):
                O("tensor", "matmul", psk, wk_bf[par][:, c * 128:(c + 1) * 128],
                  xn_sb[:, tc * 4:(tc + 1) * 4, c, :],
                  start=(c == 0), stop=(c == FC - 1),
                  inc=(("ts", 1) if c == FC - 1 else None))
            ts_k = C["ts"]
            W("vector", "ts", ts_q)
            if key2 is not None and tc == 0:
                W("vector", "ts", ts_S_done[key2])
            O("vector", "tensor_scalar", qf[par][:, tc * 512:(tc + 1) * 512],
              psq, bqk_sb[:, hp:hp + 1], 0.125, ALU.add, ALU.mult, inc=("vs", 1))
            W("vector", "ts", ts_k)
            O("vector", "tensor_scalar_add", kf[par][:, tc * 512:(tc + 1) * 512],
              psk, bqk_sb[:, 8 + hp:8 + hp + 1], inc=("vs", 1))
        prev_vs_qk = C["vs"]
        W("tensor", "cs", cs_last_psc)
        vb4 = vb_sb[:, hp * 128:hp * 128 + 128] \
            .rearrange("p (o f) -> p o f", o=1).broadcast_to((128, 4, 128))
        vs_v_batch = []
        for ttv in range(NTS):
            slot = ttv % 4
            if slot == 0 and ttv >= 4:
                W("tensor", "vs", vs_v_batch[-1])
            for c in range(FC):
                O("tensor", "matmul", psVC[:, slot * 128:(slot + 1) * 128],
                  xn_sb[:, ttv, c, :], wv_bf[par][:, c * 128:(c + 1) * 128],
                  start=(c == 0), stop=(c == FC - 1),
                  inc=(("ts", 1) if c == FC - 1 else None))
            if slot == 3:
                ts_v = C["ts"]
                W("vector", "ts", ts_v)
                if key2 is not None and ttv == 3:
                    W("vector", "ts", ts_pv_done[key2])
                O("vector", "tensor_tensor",
                  _view(vt[par][:, (ttv - 3) * 128:(ttv + 1) * 128], [128, 4, 128]),
                  _view(psVC[:, 0:512], [128, 4, 128]),
                  vb4, ALU.add, inc=("vs", 1))
                vs_v_batch.append(C["vs"])
        ts_qkv_done[(s, hp)] = C["ts"]
        vs_v_last = C["vs"]
        if pi + 2 < len(pairs):
            W("gpsimd", "ts", ts_qkv_done[(s, hp)])
            load_w(pi + 2)
        if "pc_emit" in L:
            L["pc_emit"](6)

        vs_qk = prev_vs_qk
        pend = None

        def emit_ptpv(j0, h0, vsn0, pb, pp0):
            nonlocal cs_last_pt, cs_last_psc
            cs_pt = []
            ts_tr = 0
            for ki in range(j0 + 1):
                g = ki % 4
                if ki == 0:
                    W("tensor", "vs", vsn0)
                    W("tensor", "cs", cs_last_pt)
                if g == 0 and ki >= 4:
                    W("tensor", "cs", cs_pt[-1])
                O("tensor", "transpose", psT[:, g * 128:(g + 1) * 128],
                  pb[:, ki * 128:(ki + 1) * 128], idb_sb[:, :],
                  inc=(("ts", 1) if (g == 3 or ki == j0) else None))
                if g == 3 or ki == j0:
                    k0 = (ki // 4) * 4
                    nn = (ki - k0 + 1) * 128
                    ts_tr = C["ts"]
                    W("scalar", "ts", C["ts"])
                    O("scalar", "copy", PT2[pp0][:, k0 * 128:k0 * 128 + nn],
                      psT[:, 0:nn], inc=("cs", 1))
                    cs_pt.append(C["cs"])
            cs_last_pt = cs_pt[-1]
            if j0 == 0 and h0 == 0:
                W("tensor", "vs", vs_v_last)
            if h0 == 0:
                W("tensor", "cs", cs_last_psc)
            for ki in range(j0 + 1):
                if ki % 4 == 0:
                    W("tensor", "cs", cs_pt[ki // 4])
                O("tensor", "matmul", psVC[64 * h0:64 * h0 + 64, 0:128],
                  vt[par][:, ki * 128 + 64 * h0:ki * 128 + 64 * h0 + 64],
                  PT2[pp0][:, ki * 128:(ki + 1) * 128],
                  start=(ki == 0), stop=(ki == j0),
                  inc=(("ts", 1) if ki == j0 else None))
            if h0 == 1:
                W("scalar", "ts", C["ts"])
                O("scalar", "copy", comb[:, hp, j0, :], psVC[:, 0:128], inc=("cs", 1))
                cs_last_psc = C["cs"]
            return ts_tr

        mxc = L["mxc"]

        def pb_ap(si, j2, n2):
            if (j2 + 1) * 128 <= 1024:
                return Pb2[si % 2][:, ((si % 4) // 2) * 1024:
                                   ((si % 4) // 2) * 1024 + n2]
            return Pb2[si % 2][:, 0:n2]

        def emit_rcscale(si, pj, pmpar):
            W("vector", "cs", cs_last_exp)
            O("vector", "reciprocal", rc2[:, pmpar:pmpar + 1],
              rs2[:, pmpar:pmpar + 1], inc=("vs", 1))
            W("vector", "vs")
            O("vector", "tensor_scalar_mul", pb_ap(si, pj, (pj + 1) * 128),
              pb_ap(si, pj, (pj + 1) * 128), rc2[:, pmpar:pmpar + 1],
              inc=("vs", 1))
            return C["vs"]

        cs_exp_par = [cs_last_exp, cs_last_exp]
        vs_psS_par = [vs_last_psS, vs_last_psS]
        pendq = []
        vsn = {}
        ts_tr_end = {}
        for si, (j, h) in enumerate((j2, h2) for j2 in range(NTS)
                                    for h2 in range(2)):
            mpar = si % 2
            kmax = (j + 1) * 128
            base = mpar * 1024 if kmax <= 1024 else 0
            W("tensor", "vs", vs_qk)
            if kmax <= 1024:
                W("tensor", "cs", cs_exp_par[mpar])
                W("tensor", "vs", vs_psS_par[mpar])
            else:
                W("tensor", "cs", cs_last_exp)
                W("tensor", "vs", vs_last_psS)
            for kc in range(0, kmax, 512):
                n = min(512, kmax - kc)
                O("tensor", "matmul", psS[:, base + kc:base + kc + n],
                  qf[par][64 * h:64 * h + 64, j * 128:(j + 1) * 128],
                  kf[par][64 * h:64 * h + 64, kc:kc + n],
                  start=True, stop=True,
                  inc=(("ts", 1) if kc + n >= kmax else None))
            W("vector", "ts", C["ts"])
            O("vector", "tensor_tensor",
              psS[:, base + j * 128:base + j * 128 + 128],
              psS[:, base + j * 128:base + j * 128 + 128],
              ntri_sb[:, :], ALU.add, inc=("vs", 1))
            W("vector", "vs")
            O("vector", "reduce_max", nmx2[:, mpar:mpar + 1],
              psS[:, base:base + kmax], axis=AX.X, negate=True,
              inc=("vs", 1))
            vs_psS_par[mpar] = C["vs"]
            vs_last_psS = C["vs"]
            vs_comb = C["vs"]
            if pendq:
                psi, pj, ph = pendq[-1]
                vsn[psi] = emit_rcscale(psi, pj, psi % 2)
            if len(pendq) == 2:
                psi, pj, ph = pendq.pop(0)
                ts_tr_end[psi] = emit_ptpv(pj, ph, vsn[psi],
                                           pb_ap(psi, pj, (pj + 1) * 128),
                                           psi % 2)
            W("scalar", "vs", vs_comb)
            if si - 2 in ts_tr_end:
                W("scalar", "ts", ts_tr_end[si - 2])
            O("scalar", "activation", pb_ap(si, j, kmax),
              psS[:, base:base + kmax], AF.Exp,
              bias=nmx2[:, mpar:mpar + 1], accum_out=rs2[:, mpar:mpar + 1],
              inc=("cs", 1))
            cs_exp_par[mpar] = C["cs"]
            cs_last_exp = C["cs"]
            pendq.append((si, j, h))
        psi, pj, ph = pendq[-1]
        vsn[psi] = emit_rcscale(psi, pj, psi % 2)
        for psi, pj, ph in pendq:
            emit_ptpv(pj, ph, vsn[psi], pb_ap(psi, pj, (pj + 1) * 128), psi % 2)
        ts_S_done[(s, hp)] = C["ts"]
        ts_pv_done[(s, hp)] = C["ts"]
        gen = L.get("prep_gen")
        if gen is not None:
            hz = L["hz"]
            hz["prev_vs_qk"] = prev_vs_qk
            hz["cs_last_psc"] = cs_last_psc
            hz["cs_last_pt"] = cs_last_pt
            hz["cs_last_exp"] = cs_last_exp
            hz["vs_last_psS"] = vs_last_psS
            try:
                next(gen)
            except StopIteration:
                L["prep_gen"] = None
            L["vs_hook"] = C["vs"]
            cs_last_psc = max(cs_last_psc, hz.get("cs_last_psc", 0))
            vs_last_psS = max(vs_last_psS, hz.get("vs_last_psS", 0))

        if hp == NHP - 1:
            cs_comb = C["cs"]
            for tt in range(NTS):
                par2 = tt % 2
                nR = len(vs_R)
                if nR >= 2:
                    W("sync", "vs", vs_R[nR - 2])
                    W("sync", sr_R[nR - 2])
                else:
                    W("sync", "cs", L["ln_cs"])
                    W("sync", "vs", L["ln_vs"])
                O("sync", "dma_start", out=x_tile[par2],
                  in_=d_x[(s * NTS + tt) * 128:(s * NTS + tt + 1) * 128, :],
                  inc=("lx%d" % par2, 16))
                ld_xt = kb.mark("lx%d" % par2)
                psO = psS[:, (tt % 2) * 1024:(tt % 2) * 1024 + 1024]
                W("tensor", "cs", cs_comb)
                W("tensor", "cs", cs_last_exp)
                W("tensor", "vs", vs_last_psS)
                W("tensor", L["gwo_all"])
                if nR >= 2:
                    W("tensor", "vs", vs_R[nR - 2])
                for half in range(2):
                    for c in range(FC):
                        O("tensor", "matmul", psO[:, half * 512:half * 512 + 512],
                          comb[:, c, tt, :],
                          wo_bf[:, c * 1024 + half * 512:c * 1024 + half * 512 + 512],
                          start=(c == 0), stop=(c == FC - 1),
                          inc=(("ts", 1) if (c == FC - 1 and half == 1) else None))
                ts_o = C["ts"]
                W("vector", "ts", ts_o)
                W("vector", ld_xt)
                if nR >= 2:
                    W("vector", sr_R[nR - 2])
                O("vector", "tensor_tensor", psO, psO, ob_sb[:, :], ALU.add,
                  inc=("vs", 1))
                W("vector", "vs")
                O("vector", "tensor_tensor", x_tile[par2], psO, x_tile[par2],
                  ALU.add, inc=("vs", 1))
                vs_R.append(C["vs"])
                W("sync", "vs", C["vs"])
                O("sync", "dma_start", out=d_rscr[s * NTS + tt, :, :], in_=x_tile[par2],
                  inc=("sr%d" % par2, 16))
                if KSTAGE == "attn":
                    O("sync", "dma_start",
                      out=d_y[(s * NTS + tt) * 128:(s * NTS + tt + 1) * 128, :],
                      in_=x_tile[par2], inc=("sr%d" % par2, 16))
                sr_R.append(kb.mark("sr%d" % par2))
            if s == 0:
                vs_last_psS = max(vs_last_psS, vs_R[-1])
    L["sr_all"] = (kb.mark("sr0"), kb.mark("sr1"))


def emit_routing(kb, L):
    W, O, C = kb.W, kb.O, kb.cnt
    scores, coef = L["scores"], L["coef"]
    sfc, eqb, idxb = L["sfc"], L["eqb"], L["idxb"]
    gm1, gcnt, gm2r, gge, gdd, gsb, gmk, grA, grB = (
        L["gm1"], L["gcnt"], L["gm2r"], L["gge"], L["gdd"], L["gsb"], L["gmk"],
        L["grA"], L["grB"])
    m16, m16b, ws16 = L["m16"], L["m16b"], L["ws16"]
    eb_sb, i64_sb, i8_sb = L["eb_sb"], L["i64_sb"], L["i8_sb"]

    def _ap2(ap):
        return ap if isinstance(ap, bass.AP) else ap[:, :]

    def hv3(ap, a):
        return _ap2(ap).rearrange("p (a b) -> p a b", a=a)

    def hv4(ap, a, b):
        return _ap2(ap).rearrange("p (a b c) -> p a b c", a=a, b=b)

    def VO(fname, *args, **kw):
        W("vector", "vs")
        O("vector", fname, *args, inc=("vs", 1), **kw)

    W("vector", "cs", L["ln_cs"])
    W("vector", L["lc_all"])
    W("scalar", "cs", L["ln_cs"])
    W("scalar", "vs", L["ln_vs"])
    O("scalar", "activation", scores[:, 0:1024], scores[:, 0:1024], AF.Sigmoid,
      inc=("cs", 1))
    O("scalar", "activation", scores[:, 1024:2048], scores[:, 1024:2048],
      AF.Sigmoid, inc=("cs", 1))
    W("vector", "cs")
    for half in range(2):
        sc = scores[:, half * 1024:half * 1024 + 1024]
        cf = coef[:, half * 1024:half * 1024 + 1024]
        ebb = hv3(eb_sb, 1).broadcast_to((128, 16, 64))
        VO("tensor_tensor", hv3(sfc, 16), hv3(sc, 16), ebb, ALU.add)
        VO("tensor_reduce", hv4(gm1, 16, 8), hv4(sfc, 16, 8), AX.X, ALU.max)
        g1b = hv4(gm1, 16, 8).broadcast_to((128, 16, 8, 8))
        VO("tensor_tensor", hv4(eqb, 16, 8), hv4(sfc, 16, 8), g1b, ALU.is_equal)
        VO("tensor_reduce", hv4(gcnt, 16, 8), hv4(eqb, 16, 8), AX.X, ALU.add)
        VO("tensor_scalar_mul", idxb[:, :], eqb[:, :], NEG)
        VO("tensor_add", idxb[:, :], idxb[:, :], sfc[:, :])
        VO("tensor_reduce", hv4(gm2r, 16, 8), hv4(idxb, 16, 8), AX.X, ALU.max)
        VO("tensor_single_scalar", gcnt[:, :], gcnt[:, :], 1.5, ALU.is_ge)
        VO("tensor_sub", gdd[:, :], gm1[:, :], gm2r[:, :])
        VO("tensor_mul", gge[:, :], gcnt[:, :], gdd[:, :])
        VO("tensor_add", gm2r[:, :], gm2r[:, :], gge[:, :])
        VO("tensor_add", gsb[:, :], gm1[:, :], gm2r[:, :])
        VO("memset", gmk[:, :], 0.0)
        i8b = hv3(i8_sb, 1).broadcast_to((128, 16, 8))
        for it in range(4):
            VO("tensor_reduce", hv3(m16, 16), hv3(gsb, 16), AX.X, ALU.max)
            mb = hv3(m16, 16).broadcast_to((128, 16, 8))
            VO("tensor_tensor", hv3(grA, 16), hv3(gsb, 16), mb, ALU.is_equal)
            VO("tensor_tensor", hv3(grB, 16), hv3(grA, 16), i8b, ALU.mult)
            VO("tensor_scalar_add", grB[:, :], grB[:, :], 127.0)
            VO("tensor_reduce", hv3(m16b, 16), hv3(grB, 16), AX.X, ALU.min)
            mnb = hv3(m16b, 16).broadcast_to((128, 16, 8))
            VO("tensor_tensor", hv3(grA, 16), hv3(grB, 16), mnb, ALU.is_equal)
            VO("tensor_add", gmk[:, :], gmk[:, :], grA[:, :])
            VO("tensor_scalar_mul", grA[:, :], grA[:, :], NEG)
            VO("tensor_add", gsb[:, :], gsb[:, :], grA[:, :])
        gmb = hv4(gmk, 16, 8).broadcast_to((128, 16, 8, 8))
        VO("tensor_tensor", hv4(sfc, 16, 8), hv4(sfc, 16, 8), gmb, ALU.mult)
        VO("memset", cf, 0.0)
        VO("memset", ws16[:, :], 0.0)
        i64b = hv3(i64_sb, 1).broadcast_to((128, 16, 64))
        for it in range(8):
            VO("tensor_reduce", hv3(m16, 16), hv3(sfc, 16), AX.X, ALU.max)
            mb = hv3(m16, 16).broadcast_to((128, 16, 64))
            VO("tensor_tensor", hv3(eqb, 16), hv3(sfc, 16), mb, ALU.is_equal)
            VO("tensor_tensor", hv3(idxb, 16), hv3(eqb, 16), i64b, ALU.mult)
            VO("tensor_scalar_add", idxb[:, :], idxb[:, :], 127.0)
            VO("tensor_reduce", hv3(m16b, 16), hv3(idxb, 16), AX.X, ALU.min)
            mnb = hv3(m16b, 16).broadcast_to((128, 16, 64))
            VO("tensor_tensor", hv3(eqb, 16), hv3(idxb, 16), mnb, ALU.is_equal)
            VO("tensor_tensor", hv3(idxb, 16), hv3(sc, 16), hv3(eqb, 16), ALU.mult)
            VO("tensor_add", cf, cf, idxb[:, :])
            VO("tensor_reduce", hv3(m16, 16), hv3(idxb, 16), AX.X, ALU.add)
            VO("tensor_add", ws16[:, :], ws16[:, :], m16[:, :])
            VO("tensor_scalar_mul", eqb[:, :], eqb[:, :], NEG)
            VO("tensor_add", sfc[:, :], sfc[:, :], eqb[:, :])
        VO("tensor_scalar_add", m16[:, :], ws16[:, :], 1e-20)
        VO("reciprocal", m16b[:, :], m16[:, :])
        VO("tensor_scalar_mul", m16[:, :], m16b[:, :], 2.5)
        mcb = hv3(m16, 16).broadcast_to((128, 16, 64))
        VO("tensor_tensor", hv3(cf, 16), hv3(cf, 16), mcb, ALU.mult)
    L["vs_routing"] = C["vs"]


def emit_sparse_init(kb, L):
    W, O, C = kb.W, kb.O, kb.cnt
    O("vector", "memset", L["zini"], 0.0, inc=("vs", 1))
    vs_z = C["vs"]
    W("scalar", "vs", vs_z)
    for b in range(NSLAB * H // (128 * 8192)):
        O("scalar", "dma_start",
          out=L["d_slab"][b * 1024:(b + 1) * 1024, :]
          .rearrange("a b -> (a b)").rearrange("(p f) -> p f", p=128),
          in_=L["zini"], inc=("sz", 16))
    L["sz_all"] = kb.mark("sz")


def gen_sparse_prep(kb, L, hz):
    W, O, C = kb.W, kb.O, kb.cnt
    scores, coef = L["scores"], L["coef"]
    sfc, eqb, idxb = L["sfc"], L["eqb"], L["idxb"]
    gm1, gcnt, gm2r, gge, gdd, gsb, gmk, grA, grB = (
        L["gm1"], L["gcnt"], L["gm2r"], L["gge"], L["gdd"], L["gsb"], L["gmk"],
        L["grA"], L["grB"])
    m16, m16b, ws16 = L["m16"], L["m16b"], L["ws16"]
    eb_sb, i64_sb, i8_sb = L["eb_sb"], L["i64_sb"], L["i8_sb"]
    topke, topkw = L["topke"], L["topkw"]
    rankb = L["rankb2"][:, :]

    def _ap2(ap):
        return ap if isinstance(ap, bass.AP) else ap[:, :]

    def hv3(ap, a):
        return _ap2(ap).rearrange("p (a b) -> p a b", a=a)

    def hv4(ap, a, b):
        return _ap2(ap).rearrange("p (a b c) -> p a b c", a=a, b=b)

    def VO(fname, *args, **kw):
        W("vector", "vs")
        O("vector", fname, *args, inc=("vs", 1), **kw)

    W("vector", "cs", L["ln_cs"])
    W("vector", L["lc_all"])
    W("scalar", "cs", L["ln_cs"])
    W("scalar", "vs", L["ln_vs"])
    O("scalar", "activation", scores[:, 0:1024], scores[:, 0:1024], AF.Sigmoid,
      inc=("cs", 1))
    O("scalar", "activation", scores[:, 1024:2048], scores[:, 1024:2048],
      AF.Sigmoid, inc=("cs", 1))
    W("vector", "cs")
    for half in range(2):
        sc = scores[:, half * 1024:half * 1024 + 1024]
        cf = coef[:, half * 1024:half * 1024 + 1024]
        tke = hv4(topke, 2, 16)[:, half, :, :]
        tkw = hv4(topkw, 2, 16)[:, half, :, :]
        ebb = hv3(eb_sb, 1).broadcast_to((128, 16, 64))
        VO("tensor_tensor", hv3(sfc, 16), hv3(sc, 16), ebb, ALU.add)
        VO("tensor_reduce", hv4(gm1, 16, 8), hv4(sfc, 16, 8), AX.X, ALU.max)
        g1b = hv4(gm1, 16, 8).broadcast_to((128, 16, 8, 8))
        VO("tensor_tensor", hv4(eqb, 16, 8), hv4(sfc, 16, 8), g1b, ALU.is_equal)
        VO("tensor_reduce", hv4(gcnt, 16, 8), hv4(eqb, 16, 8), AX.X, ALU.add)
        VO("tensor_scalar_mul", idxb[:, :], eqb[:, :], NEG)
        VO("tensor_add", idxb[:, :], idxb[:, :], sfc[:, :])
        VO("tensor_reduce", hv4(gm2r, 16, 8), hv4(idxb, 16, 8), AX.X, ALU.max)
        VO("tensor_single_scalar", gcnt[:, :], gcnt[:, :], 1.5, ALU.is_ge)
        VO("tensor_sub", gdd[:, :], gm1[:, :], gm2r[:, :])
        VO("tensor_mul", gge[:, :], gcnt[:, :], gdd[:, :])
        VO("tensor_add", gm2r[:, :], gm2r[:, :], gge[:, :])
        VO("tensor_add", gsb[:, :], gm1[:, :], gm2r[:, :])
        VO("memset", gmk[:, :], 0.0)
        i8b = hv3(i8_sb, 1).broadcast_to((128, 16, 8))
        for it in range(4):
            VO("tensor_reduce", hv3(m16, 16), hv3(gsb, 16), AX.X, ALU.max)
            mb = hv3(m16, 16).broadcast_to((128, 16, 8))
            VO("tensor_tensor", hv3(grA, 16), hv3(gsb, 16), mb, ALU.is_equal)
            VO("tensor_tensor", hv3(grB, 16), hv3(grA, 16), i8b, ALU.mult)
            VO("tensor_scalar_add", grB[:, :], grB[:, :], 127.0)
            VO("tensor_reduce", hv3(m16b, 16), hv3(grB, 16), AX.X, ALU.min)
            mnb = hv3(m16b, 16).broadcast_to((128, 16, 8))
            VO("tensor_tensor", hv3(grA, 16), hv3(grB, 16), mnb, ALU.is_equal)
            VO("tensor_add", gmk[:, :], gmk[:, :], grA[:, :])
            VO("tensor_scalar_mul", grA[:, :], grA[:, :], NEG)
            VO("tensor_add", gsb[:, :], gsb[:, :], grA[:, :])
        yield
        gmb = hv4(gmk, 16, 8).broadcast_to((128, 16, 8, 8))
        VO("tensor_tensor", hv4(sfc, 16, 8), hv4(sfc, 16, 8), gmb, ALU.mult)
        VO("memset", cf, 0.0)
        VO("memset", ws16[:, :], 0.0)
        i64b = hv3(i64_sb, 1).broadcast_to((128, 16, 64))
        eib = hv3(L["ei_sb"], 1).broadcast_to((128, 16, 64))
        for it in range(8):
            VO("tensor_reduce", hv3(m16, 16), hv3(sfc, 16), AX.X, ALU.max)
            mb = hv3(m16, 16).broadcast_to((128, 16, 64))
            VO("tensor_tensor", hv3(eqb, 16), hv3(sfc, 16), mb, ALU.is_equal)
            VO("tensor_tensor", hv3(idxb, 16), hv3(eqb, 16), i64b, ALU.mult)
            VO("tensor_scalar_add", idxb[:, :], idxb[:, :], 127.0)
            VO("tensor_reduce", hv3(m16b, 16), hv3(idxb, 16), AX.X, ALU.min)
            mnb = hv3(m16b, 16).broadcast_to((128, 16, 64))
            VO("tensor_tensor", hv3(eqb, 16), hv3(idxb, 16), mnb, ALU.is_equal)
            VO("tensor_tensor", hv3(idxb, 16), hv3(eqb, 16), eib, ALU.mult)
            VO("tensor_reduce", tke[:, :, it:it + 1], hv3(idxb, 16), AX.X, ALU.add)
            VO("tensor_tensor", hv3(idxb, 16), hv3(sc, 16), hv3(eqb, 16), ALU.mult)
            VO("tensor_add", cf, cf, idxb[:, :])
            VO("tensor_reduce", tkw[:, :, it:it + 1], hv3(idxb, 16), AX.X, ALU.add)
            VO("tensor_tensor", ws16[:, :], ws16[:, :],
               tkw[:, :, it:it + 1].broadcast_to((128, 16, 1)), ALU.add)
            VO("tensor_scalar_mul", eqb[:, :], eqb[:, :], NEG)
            VO("tensor_add", sfc[:, :], sfc[:, :], eqb[:, :])
            if it == 3:
                yield
        VO("tensor_scalar_add", m16[:, :], ws16[:, :], 1e-20)
        VO("reciprocal", m16b[:, :], m16[:, :])
        VO("tensor_scalar_mul", m16[:, :], m16b[:, :], 2.5)
        mcb = hv3(m16, 16).broadcast_to((128, 16, 64))
        VO("tensor_tensor", hv3(cf, 16), hv3(cf, 16), mcb, ALU.mult)
        mkb = hv3(m16, 16).broadcast_to((128, 16, 8))
        VO("tensor_tensor", tkw, tkw, mkb, ALU.mult)
        yield
    L["vs_routing"] = C["vs"]

    psa = L["psa_raw"]
    psRank = [psa[:, 2048:2112], psa[:, 2560:2624]]
    psTot = [psa[:, 3072:3136], psa[:, 3584:3648]]
    tot_sb = grA[:, 0:64]
    selt = grB[:, 0:64]
    W("tensor", "vs", hz["prev_vs_qk"])
    W("tensor", "cs", hz["cs_last_psc"])
    W("tensor", "cs", hz["cs_last_pt"])
    W("tensor", L["lc_all"])
    VO("memset", tot_sb, 0.0)
    rank_evac = []
    tot_cp = []
    for tt in range(NT):
        W("vector", "vs")
        O("vector", "tensor_single_scalar", selt, coef[:, tt * 64:(tt + 1) * 64],
          0.0, ALU.is_gt, inc=("vs", 1))
        vs_sel = C["vs"]
        W("tensor", "vs", vs_sel)
        if tt >= 2:
            W("tensor", "vs", rank_evac[tt - 2])
        O("tensor", "matmul", psRank[tt % 2], L["ustr_sb"], selt,
          start=True, stop=True, inc=("ts", 1))
        if tt >= 1:
            W("tensor", "vs", tot_cp[tt - 1])
        O("tensor", "matmul", psTot[tt % 2], L["onesm_sb"], selt,
          start=True, stop=True, inc=("ts", 1))
        ts_mm = C["ts"]
        W("vector", "ts", ts_mm)
        O("vector", "tensor_add", rankb[:, tt * 64:(tt + 1) * 64],
          psRank[tt % 2], tot_sb, inc=("vs", 1))
        rank_evac.append(C["vs"])
        W("vector", "vs")
        O("vector", "tensor_add", tot_sb, tot_sb, psTot[tt % 2], inc=("vs", 1))
        tot_cp.append(C["vs"])
    hz["vs_last_psS"] = max(hz.get("vs_last_psS", 0), C["vs"])
    yield

    tot = grA[:, 0:64]
    gb = gm1[:, 0:64]
    tmp64 = gcnt[:, 0:64]
    icA = gm2r[:, 0:64]
    icB = gge[:, 0:64]
    so64 = gdd[:, 0:64]
    base64 = gsb[:, 0:64]
    VO("tensor_single_scalar", gb, tot, 0.5, ALU.is_ge)
    for m in range(1, 16):
        VO("tensor_single_scalar", tmp64, tot, 256.0 * m + 0.5, ALU.is_ge)
        VO("tensor_add", gb, gb, tmp64)
    cur, nxt = icA, icB
    VO("tensor_copy", cur, gb)
    for k in (1, 2, 4, 8, 16, 32):
        VO("tensor_copy", nxt[:, 0:k], cur[:, 0:k])
        VO("tensor_add", nxt[:, k:64], cur[:, k:64], cur[:, 0:64 - k])
        cur, nxt = nxt, cur
    fin = cur
    VO("tensor_sub", so64, fin, gb)
    VO("tensor_scalar_mul", base64, so64, float(GTOK))
    yield

    qb, rb2, srow, prow = L["qb"], L["rb2"], L["srow"], L["prow"]
    bs8v = L["bs8"]
    for k in range(8):
        for half in range(2):
            tkec = hv4(topke, 2, 16)[:, half, :, k:k + 1].broadcast_to((128, 16, 64))
            eib = hv3(L["ei_sb"], 1).broadcast_to((128, 16, 64))
            VO("tensor_tensor", hv3(sfc, 16), eib, tkec, ALU.is_equal)
            VO("tensor_tensor", hv3(idxb, 16), hv3(sfc, 16),
               hv3(rankb[:, half * 1024:half * 1024 + 1024], 16), ALU.mult)
            VO("tensor_reduce", hv4(L["rk8"], 2, 16)[:, half, :, k:k + 1],
               hv3(idxb, 16), AX.X, ALU.add)
            baseb = hv3(base64, 1).broadcast_to((128, 16, 64))
            VO("tensor_tensor", hv3(idxb, 16), hv3(sfc, 16), baseb, ALU.mult)
            VO("tensor_reduce", hv4(bs8v, 2, 16)[:, half, :, k:k + 1],
               hv3(idxb, 16), AX.X, ALU.add)
        if k == 3:
            yield
    yield
    VO("tensor_add", srow, bs8v, L["rk8"])
    VO("tensor_copy", L["srowi"], srow)
    vs_srow = C["vs"]
    yield

    W("sync", "ts", L["ln_ts"])
    W("sync", "cs", L["ln_cs"])
    W("sync", ("sy0", C["sy0"]))
    W("sync", ("sy1", C["sy1"]))
    W("gpsimd", "vs", vs_srow)
    W("gpsimd", L["sz_all"])
    gs_marks = {}
    for tt in range(NT):
        par = tt % 2
        if tt >= 2:
            W("sync", gs_marks[tt - 2])
        O("sync", "dma_start", out=L["xtm"][par],
          in_=L["d_xntm"][tt * 128:(tt + 1) * 128, :], inc=("ux%d" % par, 16))
        W("gpsimd", kb.mark("ux%d" % par))
        for k in range(8):
            O("gpsimd", "indirect_dma_start",
              out=L["d_slab"][:, :],
              out_offset=bass.IndirectOffsetOnAxis(L["srowi"][:, tt * 8 + k:tt * 8 + k + 1], 0),
              in_=L["xtm"][par], in_offset=None, inc=("gs%d" % par, 16))
        gs_marks[tt] = kb.mark("gs%d" % par)
        if tt % 4 == 3 and tt < NT - 1:
            yield
    L["scat_all"] = [kb.mark("gs0"), kb.mark("gs1")]

    eidb = rankb[:, 0:NGRAN]
    tmpe = rankb[:, NGRAN:2 * NGRAN]
    wdxf = rankb[:, 2 * NGRAN:3 * NGRAN]
    giov = hv3(L["gio_sb"], 1)
    VO("memset", eidb, 0.0)
    for e in range(64):
        icc = hv3(fin[:, e:e + 1], 1).broadcast_to((128, 1, NGRAN))
        VO("tensor_tensor", hv3(tmpe, 1), giov, icc, ALU.is_ge)
        VO("tensor_add", eidb, eidb, tmpe)
        if e % 32 == 31:
            yield
    VO("tensor_single_scalar", eidb, eidb, 63.0, ALU.min)
    iopb = hv3(L["iop_sb"], 1).broadcast_to((128, 1, NGRAN))
    VO("scalar_tensor_tensor", hv3(wdxf, 1), hv3(eidb, 1), 128.0, iopb,
       ALU.mult, ALU.add)
    VO("tensor_copy", L["windexi"], wdxf)
    L["vs_build"] = C["vs"]


def emit_route_debug(kb, L):
    W, O, C = kb.W, kb.O, kb.cnt
    W("vector", "sr0", C["sr0"])
    W("vector", "sr1", C["sr1"])
    W("vector", "vs")
    O("vector", "tensor_copy", L["R_dbg"][0], L["coef"][:, 0:1024], inc=("vs", 1))
    O("vector", "tensor_copy", L["R_dbg"][1], L["coef"][:, 1024:2048], inc=("vs", 1))
    W("sync", "vs")
    O("sync", "dma_start", out=L["d_y"][0:128, :], in_=L["R_dbg"][0], inc=("st", 16))
    O("sync", "dma_start", out=L["d_y"][128:256, :], in_=L["R_dbg"][1], inc=("st", 16))


def emit_moe(kb, L):
    W, O, C = kb.W, kb.O, kb.cnt
    d_wg, d_wu, d_wd = L["d_wg"], L["d_wu"], L["d_wd"]
    d_xnscr, d_rscr, d_y = L["d_xnscr"], L["d_rscr"], L["d_y"]
    xnst, hbuf, wg_bf, wu_bf, wd_bf = (L["xnst"], L["hbuf"], L["wg_bf"],
                                       L["wu_bf"], L["wd_bf"])
    acc, sGb, R2, coef = L["acc"], L["sG"], L["R2"], L["coef"]
    psG, psU, psY = L["psG"], L["psU"], L["psY"]

    seqU = [(st, u) for st in range(NST) for u in range(NU)]

    O("sync", "dma_start", out=xnst[0][:, :, :, :], in_=d_xnscr[:, 0:ST_TT, :, :],
      inc=("lt0", 16))
    ld_xnst = {0: kb.mark("lt0")}
    gd_u = {}

    gd_dn = {}

    def load_gu(gi):
        st2, u2 = seqU[gi]
        g = "gm%d" % (gi % 2)
        for dd, bb in ((d_wg, wg_bf), (d_wu, wu_bf)):
            O("gpsimd", "dma_start", out=bb[u2 % 2],
              in_=dd[u2, :, :, :].rearrange("p a b -> p (a b)"), inc=(g, 16))
        gd_u[gi] = kb.mark(g)

    def load_dn(gi):
        st2, u2 = seqU[gi]
        g = "gn%d" % (gi % 2)
        O("gpsimd", "dma_start", out=wd_bf[u2 % 2],
          in_=d_wd[u2, :, :, :].rearrange("p a b -> p (a b)"), inc=(g, 16))
        gd_dn[gi] = kb.mark(g)

    load_gu(0)
    load_gu(1)
    load_dn(0)
    load_dn(1)
    O("sync", "dma_start", out=R2[0], in_=d_rscr[0, :, :], inc=("la0", 16))
    ld_R = {(0, 0): kb.mark("la0")}
    O("sync", "dma_start", out=R2[1], in_=d_rscr[1, :, :], inc=("la1", 16))
    ld_R[(0, 1)] = kb.mark("la1")

    ts_gu_done = {}
    ts_dn_done = {}
    vs_h_done = {}
    vs_evac = {}
    st_full = {}
    cs_slot = [0, 0]
    vs_slot = [0, 0]
    last_gu_of_st = {}

    def emit_gu(gi):
        st, u = seqU[gi]
        par = u % 2
        W("tensor", gd_u[gi])
        W("tensor", ld_xnst[st])
        if gi == 0:
            W("tensor", "vs", L["vs_routing"])
        for ic in range(4):
            for tc in range(2):
                slot = (ic * 2 + tc) % 2
                W("tensor", "cs", cs_slot[slot])
                W("tensor", "vs", vs_slot[slot])
                for c in range(FC):
                    O("tensor", "matmul", psG[slot],
                      wg_bf[par][:, c * 512 + ic * 128:c * 512 + ic * 128 + 128],
                      xnst[st % 2][:, tc * 4:(tc + 1) * 4, c, :],
                      start=(c == 0), stop=(c == FC - 1),
                      inc=(("ts", 1) if c == FC - 1 else None))
                ts_g = C["ts"]
                for c in range(FC):
                    O("tensor", "matmul", psU[slot],
                      wu_bf[par][:, c * 512 + ic * 128:c * 512 + ic * 128 + 128],
                      xnst[st % 2][:, tc * 4:(tc + 1) * 4, c, :],
                      start=(c == 0), stop=(c == FC - 1),
                      inc=(("ts", 1) if c == FC - 1 else None))
                ts_u = C["ts"]
                W("scalar", "ts", ts_g)
                O("scalar", "activation", sGb[slot], psG[slot], AF.Silu, inc=("cs", 1))
                cs_slot[slot] = C["cs"]
                W("vector", "cs", cs_slot[slot])
                W("vector", "ts", ts_u)
                if gi >= 2:
                    W("vector", "ts", ts_dn_done[gi - 2])
                O("vector", "tensor_tensor",
                  hbuf[par][:, ic * 1024 + tc * 512:ic * 1024 + tc * 512 + 512],
                  sGb[slot], psU[slot], ALU.mult, inc=("vs", 1))
                vs_slot[slot] = C["vs"]
        ts_gu_done[gi] = C["ts"]
        vs_h_done[gi] = C["vs"]
        last_gu_of_st[st] = C["ts"]

    def emit_dn(gi):
        st, u = seqU[gi]
        par = u % 2
        e = u - 2
        W("tensor", gd_dn[gi])
        for tt in range(ST_TT):
            yp = tt % 2
            W("tensor", "vs", vs_h_done[gi])
            if (gi, tt - 2) in vs_evac:
                W("tensor", "vs", vs_evac[(gi, tt - 2)])
            if gi >= 1:
                W("tensor", "vs", vs_evac[(gi - 1, ST_TT - 2 + yp)])
            for ic in range(4):
                for half in range(2):
                    O("tensor", "matmul", psY[yp][:, half * 512:half * 512 + 512],
                      hbuf[par][:, ic * 1024 + tt * 128:ic * 1024 + tt * 128 + 128],
                      wd_bf[par][:, ic * 1024 + half * 512:ic * 1024 + half * 512 + 512],
                      start=(ic == 0), stop=(ic == 3),
                      inc=(("ts", 1) if (ic == 3 and half == 1) else None))
            ts_y = C["ts"]
            accs = acc[:, tt * 1024:(tt + 1) * 1024]
            W("vector", "ts", ts_y)
            tg = st * ST_TT + tt
            if u == 0:
                W("vector", ld_R[(st, tt)])
                if st > 0:
                    W("vector", st_full[st - 1])
                O("vector", "tensor_tensor", accs, psY[yp], R2[tt % 2], ALU.add,
                  inc=("vs", 1))
            elif u == 1:
                W("vector", "vs", vs_evac[(gi - 1, tt)])
                O("vector", "tensor_tensor", accs, psY[yp], accs, ALU.add,
                  inc=("vs", 1))
            else:
                W("vector", "vs", vs_evac[(gi - 1, tt)])
                O("vector", "scalar_tensor_tensor", accs, psY[yp],
                  coef[:, tg * 64 + e:tg * 64 + e + 1], accs, ALU.mult, ALU.add,
                  inc=("vs", 1))
            vs_evac[(gi, tt)] = C["vs"]
            if u == 0 and tt + 2 < ST_TT:
                W("sync", "vs", C["vs"])
                O("sync", "dma_start", out=R2[tt % 2],
                  in_=d_rscr[st * ST_TT + tt + 2, :, :], inc=("la%d" % (tt % 2), 16))
                ld_R[(st, tt + 2)] = kb.mark("la%d" % (tt % 2))
        ts_dn_done[gi] = C["ts"]
        if u == NU - 1:
            for tt in range(ST_TT):
                W("sync", "vs", vs_evac[(gi, tt)])
                O("sync", "dma_start",
                  out=d_y[(st * ST_TT + tt) * 128:(st * ST_TT + tt + 1) * 128, :],
                  in_=acc[:, tt * 1024:(tt + 1) * 1024], inc=("st", 16))
            st_full[st] = kb.mark("st")
            if st + 1 < NST:
                for r0 in range(2):
                    W("sync", "vs", vs_evac[(st * NU, ST_TT - 2 + r0)])
                    O("sync", "dma_start", out=R2[r0],
                      in_=d_rscr[(st + 1) * ST_TT + r0, :, :], inc=("la%d" % r0, 16))
                    ld_R[(st + 1, r0)] = kb.mark("la%d" % r0)

    for gi in range(len(seqU)):
        st, u = seqU[gi]
        if u == 0 and st + 1 < NST:
            if st >= 1:
                W("sync", "ts", last_gu_of_st[st - 1])
            O("sync", "dma_start", out=xnst[(st + 1) % 2][:, :, :, :],
              in_=d_xnscr[:, (st + 1) * ST_TT:(st + 2) * ST_TT, :, :],
              inc=("lt%d" % ((st + 1) % 2), 16))
            ld_xnst[st + 1] = kb.mark("lt%d" % ((st + 1) % 2))
        emit_gu(gi)
        if gi + 2 < len(seqU):
            W("gpsimd", "ts", ts_gu_done[gi])
            load_gu(gi + 2)
        if gi >= 1:
            emit_dn(gi - 1)
            if gi + 1 < len(seqU):
                W("gpsimd", "ts", ts_dn_done[gi - 1])
                load_dn(gi + 1)
    emit_dn(len(seqU) - 1)


def emit_moe_sparse(kb, L):
    W, O, C = kb.W, kb.O, kb.cnt
    d_wg, d_wu, d_wd = L["d_wg"], L["d_wu"], L["d_wd"]
    d_xnscr, d_rscr, d_y = L["d_xnscr"], L["d_rscr"], L["d_y"]
    xnst, wg_bf, wu_bf, wd_bf = L["xnst"], L["wg_bf"], L["wu_bf"], L["wd_bf"]
    acc, sGb, R2 = L["acc"], L["sG"], L["R2"]
    psG, psU, psY = L["psG"], L["psU"], L["psY"]
    hbuf, hb2 = L["hbuf"], L["hbuf2"]
    xnT_g, ystage, xs = L["xnT_g"], L["ystage"], L["xs"]
    psYg, psTb = L["psYg"], L["psTb"]
    idb_sb = L["idb_sb"]
    d_slab, d_ybig = L["d_slab"], L["d_ybig"]
    wgubv, wdbv = L["d_wgub"][:, :], L["d_wdb"][:, :]

    units = [("sh", st, u) for st in range(NST) for u in range(2)] + \
            [("gr", g, 0) for g in range(NGRAN)]
    NG = len(units)

    O("sync", "dma_start", out=xnst[0][:, :, :, :], in_=d_xnscr[:, 0:ST_TT, :, :],
      inc=("lt0", 16))
    ld_xnst = {0: kb.mark("lt0")}
    gd_u = {}
    gd_dn = {}

    def load_gu(gi):
        kind = units[gi]
        g = "gm%d" % (gi % 2)
        if kind[0] == "sh":
            for dd, bb in ((d_wg, wg_bf), (d_wu, wu_bf)):
                O("gpsimd", "dma_start", out=bb[gi % 2],
                  in_=dd[kind[2], :, :, :].rearrange("p a b -> p (a b)"), inc=(g, 16))
        else:
            gr = kind[1]
            g = "gg%d" % (gi % 2)
            if gr == 0:
                W("gpsimd", L["pcw_all"])
                W("gpsimd", "vs", L["vs_build"])
            O("gpsimd", "indirect_dma_start", out=L["wgu_bf"][gi % 2],
              out_offset=None, in_=wgubv,
              in_offset=bass.IndirectOffsetOnAxis(L["windexi"][:, gr:gr + 1], 0),
              inc=(g, 16))
        gd_u[gi] = kb.mark(g)

    def load_dn(gi):
        kind = units[gi]
        g = "gn%d" % (gi % 2)
        if kind[0] == "sh":
            O("gpsimd", "dma_start", out=wd_bf[gi % 2],
              in_=d_wd[kind[2], :, :, :].rearrange("p a b -> p (a b)"), inc=(g, 16))
        else:
            gr = kind[1]
            g = "gh%d" % (gi % 2)
            O("gpsimd", "indirect_dma_start", out=wd_bf[gi % 2], out_offset=None,
              in_=wdbv,
              in_offset=bass.IndirectOffsetOnAxis(L["windexi"][:, gr:gr + 1], 0),
              inc=(g, 16))
        gd_dn[gi] = kb.mark(g)

    load_gu(0)
    load_gu(1)
    load_dn(0)
    load_dn(1)
    O("sync", "dma_start", out=R2[0], in_=d_rscr[0, :, :], inc=("la0", 16))
    ld_R = {(0, 0): kb.mark("la0")}
    O("sync", "dma_start", out=R2[1], in_=d_rscr[1, :, :], inc=("la1", 16))
    ld_R[(0, 1)] = kb.mark("la1")

    ts_gu_done = {}
    ts_dn_done = {}
    vs_h_done = {}
    vs_evac = {}
    st_full = {}
    cs_slot = [0, 0]
    vs_slot = [0, 0]
    last_gu_of_st = {}
    ua_marks = {}
    tr_last = {0: 0, 1: 0}
    go_last = {0: ("go0", 0), 1: ("go1", 0)}

    def load_granule(gi):
        gr = units[gi][1]
        par = gi % 2
        if gr < 2:
            for m in L["scat_all"]:
                W("sync", m)
            W("sync", st_full[NST - 1])
        else:
            W("sync", "ts", ts_gu_done[gi - 2])
        g = "ua%d" % par
        for i in range(2):
            O("sync", "dma_start", out=xs[par * 2 + i],
              in_=d_slab[gr * GTOK + i * 128:gr * GTOK + (i + 1) * 128, :],
              inc=(g, 16))
        ua_marks[gi] = kb.mark(g)

    def emit_gu(gi):
        kind = units[gi]
        par = gi % 2
        if kind[0] == "sh":
            W("tensor", gd_u[gi])
            W("tensor", ld_xnst[kind[1]])
            ntc = 2
        else:
            W("tensor", ua_marks[gi])
            if units[gi - 1][0] == "sh":
                W("tensor", "vs", C["vs"])
            cs_tr = []
            for k2 in range(4):
                i, c0 = k2 // 2, (k2 % 2) * 4
                bank = psTb[k2 % 2]
                if k2 >= 2:
                    W("tensor", "cs", cs_tr[k2 - 2])
                else:
                    W("tensor", "cs", tr_last[k2 % 2])
                for cc in range(4):
                    O("tensor", "transpose", bank[:, cc * 128:(cc + 1) * 128],
                      xs[par * 2 + i][:, (c0 + cc) * 128:(c0 + cc + 1) * 128],
                      idb_sb[:, :], inc=(("ts", 1) if cc == 3 else None))
                W("scalar", "ts", C["ts"])
                O("scalar", "copy",
                  _view(xnT_g[par], [128, FC, GTOK])[:, c0:c0 + 4,
                                                     i * 128:(i + 1) * 128],
                  _view(bank[:, 0:512], [128, 4, 128]), inc=("cs", 1))
                cs_tr.append(C["cs"])
                tr_last[k2 % 2] = C["cs"]
            W("tensor", gd_u[gi])
            W("tensor", "cs", cs_tr[-1])
            ntc = 1
        for ic in range(4):
            for tc in range(ntc):
                slot = (ic * ntc + tc) % 2
                W("tensor", "cs", cs_slot[slot])
                W("tensor", "vs", vs_slot[slot])
                if kind[0] == "sh":
                    nel = 512
                else:
                    nel = GTOK
                for c in range(FC):
                    if kind[0] == "sh":
                        rr = xnst[kind[1] % 2][:, tc * 4:(tc + 1) * 4, c, :]
                    else:
                        rr = xnT_g[par][:, c * GTOK:(c + 1) * GTOK]
                    O("tensor", "matmul", psG[slot][:, 0:nel],
                      wg_bf[par][:, c * 512 + ic * 128:c * 512 + ic * 128 + 128],
                      rr, start=(c == 0), stop=(c == FC - 1),
                      inc=(("ts", 1) if c == FC - 1 else None))
                ts_g = C["ts"]
                for c in range(FC):
                    if kind[0] == "sh":
                        rr = xnst[kind[1] % 2][:, tc * 4:(tc + 1) * 4, c, :]
                    else:
                        rr = xnT_g[par][:, c * GTOK:(c + 1) * GTOK]
                    O("tensor", "matmul", psU[slot][:, 0:nel],
                      wu_bf[par][:, c * 512 + ic * 128:c * 512 + ic * 128 + 128],
                      rr, start=(c == 0), stop=(c == FC - 1),
                      inc=(("ts", 1) if c == FC - 1 else None))
                ts_u = C["ts"]
                W("scalar", "ts", ts_g)
                O("scalar", "activation", sGb[slot][:, 0:nel], psG[slot][:, 0:nel],
                  AF.Silu, inc=("cs", 1))
                cs_slot[slot] = C["cs"]
                W("vector", "cs", cs_slot[slot])
                W("vector", "ts", ts_u)
                if gi >= 2:
                    W("vector", "ts", ts_dn_done[gi - 2])
                if kind[0] == "sh":
                    hdst = hbuf[par][:, ic * 1024 + tc * 512:ic * 1024 + tc * 512 + 512]
                else:
                    hdst = hb2[par][:, ic * GTOK:(ic + 1) * GTOK]
                O("vector", "tensor_tensor", hdst, sGb[slot][:, 0:nel],
                  psU[slot][:, 0:nel], ALU.mult, inc=("vs", 1))
                vs_slot[slot] = C["vs"]
        ts_gu_done[gi] = C["ts"]
        vs_h_done[gi] = C["vs"]
        if kind[0] == "sh":
            last_gu_of_st[kind[1]] = C["ts"]

    def emit_dn(gi):
        kind = units[gi]
        par = gi % 2
        W("tensor", gd_dn[gi])
        if kind[0] == "sh":
            st, u = kind[1], kind[2]
            for tt in range(ST_TT):
                yp = tt % 2
                W("tensor", "vs", vs_h_done[gi])
                if (gi, tt - 2) in vs_evac:
                    W("tensor", "vs", vs_evac[(gi, tt - 2)])
                if gi >= 1:
                    W("tensor", "vs", vs_evac[(gi - 1, ST_TT - 2 + yp)])
                for ic in range(4):
                    for half in range(2):
                        O("tensor", "matmul", psY[yp][:, half * 512:half * 512 + 512],
                          hbuf[par][:, ic * 1024 + tt * 128:ic * 1024 + tt * 128 + 128],
                          wd_bf[par][:, ic * 1024 + half * 512:ic * 1024 + half * 512 + 512],
                          start=(ic == 0), stop=(ic == 3),
                          inc=(("ts", 1) if (ic == 3 and half == 1) else None))
                ts_y = C["ts"]
                accs = acc[:, tt * 1024:(tt + 1) * 1024]
                W("vector", "ts", ts_y)
                if u == 0:
                    W("vector", ld_R[(st, tt)])
                    if st > 0:
                        W("vector", st_full[st - 1])
                    O("vector", "tensor_tensor", accs, psY[yp], R2[tt % 2], ALU.add,
                      inc=("vs", 1))
                else:
                    W("vector", "vs", vs_evac[(gi - 1, tt)])
                    O("vector", "tensor_tensor", accs, psY[yp], accs, ALU.add,
                      inc=("vs", 1))
                vs_evac[(gi, tt)] = C["vs"]
                if u == 0 and tt + 2 < ST_TT:
                    W("sync", "vs", C["vs"])
                    O("sync", "dma_start", out=R2[tt % 2],
                      in_=d_rscr[st * ST_TT + tt + 2, :, :], inc=("la%d" % (tt % 2), 16))
                    ld_R[(st, tt + 2)] = kb.mark("la%d" % (tt % 2))
            ts_dn_done[gi] = C["ts"]
            if u == 1:
                for tt in range(ST_TT):
                    W("sync", "vs", vs_evac[(gi, tt)])
                    O("sync", "dma_start",
                      out=d_y[(st * ST_TT + tt) * 128:(st * ST_TT + tt + 1) * 128, :],
                      in_=acc[:, tt * 1024:(tt + 1) * 1024], inc=("st", 16))
                st_full[st] = kb.mark("st")
                if st + 1 < NST:
                    for r0 in range(2):
                        W("sync", "vs", vs_evac[(gi - 1, ST_TT - 2 + r0)])
                        O("sync", "dma_start", out=R2[r0],
                          in_=d_rscr[(st + 1) * ST_TT + r0, :, :], inc=("la%d" % r0, 16))
                        ld_R[(st + 1, r0)] = kb.mark("la%d" % r0)
        else:
            gr = kind[1]
            for half in range(2):
                for i in range(2):
                    W("tensor", "vs", vs_h_done[gi])
                    if half == 0:
                        if gi >= 1:
                            pk = (gi - 1,
                                  (ST_TT if units[gi - 1][0] == "sh" else 4) - 2 + i)
                            W("tensor", "vs", vs_evac[pk])
                    else:
                        W("tensor", "vs", vs_evac[(gi, i)])
                    for ic in range(4):
                        O("tensor", "matmul", psYg[i],
                          hb2[par][:, ic * GTOK + i * 128:ic * GTOK + i * 128 + 128],
                          wd_bf[par][:, ic * 1024 + half * 512:ic * 1024 + half * 512 + 512],
                          start=(ic == 0), stop=(ic == 3),
                          inc=(("ts", 1) if ic == 3 else None))
                    ts_y = C["ts"]
                    W("vector", "ts", ts_y)
                    if half == 0:
                        W("vector", go_last[i])
                    O("vector", "tensor_copy",
                      ystage[i][:, half * 512:half * 512 + 512], psYg[i],
                      inc=("vs", 1))
                    vs_evac[(gi, half * 2 + i)] = C["vs"]
                    if half == 1:
                        W("sync", "vs", C["vs"])
                        O("sync", "dma_start",
                          out=d_ybig[gr * GTOK + i * 128:gr * GTOK + (i + 1) * 128, :],
                          in_=ystage[i], inc=("go%d" % i, 16))
                        go_last[i] = kb.mark("go%d" % i)
            ts_dn_done[gi] = C["ts"]

    for gi in range(NG):
        kind = units[gi]
        if kind[0] == "sh" and kind[2] == 0 and kind[1] + 1 < NST:
            st = kind[1]
            if st >= 1:
                W("sync", "ts", last_gu_of_st[st - 1])
            O("sync", "dma_start", out=xnst[(st + 1) % 2][:, :, :, :],
              in_=d_xnscr[:, (st + 1) * ST_TT:(st + 2) * ST_TT, :, :],
              inc=("lt%d" % ((st + 1) % 2), 16))
            ld_xnst[st + 1] = kb.mark("lt%d" % ((st + 1) % 2))
        if gi >= 1:
            emit_dn(gi - 1)
            if gi + 1 < NG:
                W("gpsimd", "ts", ts_dn_done[gi - 1])
                load_dn(gi + 1)
        if kind[0] == "gr" and kind[1] < 2:
            load_granule(gi)
        emit_gu(gi)
        if gi + 2 < NG:
            W("gpsimd", "ts", ts_gu_done[gi])
            load_gu(gi + 2)
            if units[gi + 2][0] == "gr" and units[gi + 2][1] >= 2:
                load_granule(gi + 2)
    emit_dn(NG - 1)
    L["go_all"] = [kb.mark("go0"), kb.mark("go1")]
    L["st_base"] = kb.mark("st")
    emit_final(kb, L)


def emit_final(kb, L):
    W, O, C = kb.W, kb.O, kb.cnt
    d_y, d_ybig = L["d_y"], L["d_ybig"]
    yb, outb, topkw = L["yb_sb"], L["outb"], L["topkw"]
    for m in L["go_all"]:
        W("gpsimd", m)
        W("sync", m)
    W("sync", L["st_base"])
    ts_all = kb.mark("ts")
    W("gpsimd", ts_all)
    vs_red = {}
    st_out = {}
    gk_m = {}
    la_m = {}
    for tt in range(NT):
        par = tt % 4
        if tt >= 4:
            W("gpsimd", "vs", vs_red[tt - 4])
        for k in range(8):
            O("gpsimd", "indirect_dma_start",
              out=yb[par][:, k * 1024:(k + 1) * 1024], out_offset=None,
              in_=d_ybig[:, :],
              in_offset=bass.IndirectOffsetOnAxis(
                  L["srowi"][:, tt * 8 + k:tt * 8 + k + 1], 0),
              inc=("gf%d" % (par % 2), 16))
            gk_m[(tt, k)] = kb.mark("gf%d" % (par % 2))
        if tt >= 2:
            W("sync", "vs", vs_red[tt - 2])
        O("sync", "dma_start", out=L["bstage"][tt % 2],
          in_=d_y[tt * 128:(tt + 1) * 128, :], inc=("la%d" % (tt % 2), 16))
        la_m[tt] = kb.mark("la%d" % (tt % 2))
        W("vector", la_m[tt])
        if tt >= 2:
            W("vector", st_out[tt - 2])
        W("vector", gk_m[(tt, 7)])
        for k in range(8):
            src = L["bstage"][tt % 2] if k == 0 else outb[tt % 2]
            W("vector", "vs")
            O("vector", "scalar_tensor_tensor", outb[tt % 2],
              yb[par][:, k * 1024:(k + 1) * 1024],
              topkw[:, tt * 8 + k:tt * 8 + k + 1], src,
              ALU.mult, ALU.add, inc=("vs", 1))
        vs_red[tt] = C["vs"]
        W("sync", "vs", C["vs"])
        O("sync", "dma_start", out=d_y[tt * 128:(tt + 1) * 128, :], in_=outb[tt % 2],
          inc=("st", 16))
        st_out[tt] = kb.mark("st")



def _prep(inputs):
    x = np.ascontiguousarray(np.asarray(inputs["L0_attn_encodings"], dtype=np.float32)
                             .reshape(NCORES, T, H))
    aw = np.asarray(inputs["L0_attn_attn_w"], dtype=np.float32)
    ab = np.asarray(inputs["L0_attn_attn_b"], dtype=np.float32)
    ow = np.asarray(inputs["L0_attn_out_w"], dtype=np.float32)
    obv = np.asarray(inputs["L0_attn_out_b"], dtype=np.float32)
    rww = np.asarray(inputs["L0_ffn_router_w"], dtype=np.float32)
    ebv = np.asarray(inputs["L0_ffn_e_bias"], dtype=np.float32)
    Wg = np.asarray(inputs["L0_ffn_Wg"], dtype=np.float32)
    Wu = np.asarray(inputs["L0_ffn_Wu"], dtype=np.float32)
    Wd = np.asarray(inputs["L0_ffn_Wd"], dtype=np.float32)
    SWg = np.asarray(inputs["L0_ffn_SWg"], dtype=np.float32)
    SWu = np.asarray(inputs["L0_ffn_SWu"], dtype=np.float32)
    SWd = np.asarray(inputs["L0_ffn_SWd"], dtype=np.float32)

    a3 = aw.reshape(16, 3, 64, H)
    WQ = a3[:, 0].reshape(H, H)
    WK = a3[:, 1].reshape(H, H)
    WV = a3[:, 2].reshape(H, H)
    b3 = ab.reshape(16, 3, 64)
    bq = b3[:, 0].reshape(H)
    bk = b3[:, 1].reshape(H)
    bv = b3[:, 2].reshape(H)

    def whp(Wm):
        return np.ascontiguousarray(Wm.reshape(8, 128, 8, 128).transpose(0, 3, 2, 1))

    wq = whp(WQ)
    wk = whp(WK)
    wv = whp(WV)
    wo = np.ascontiguousarray(ow.reshape(H, 8, 128).transpose(2, 1, 0))
    rw = np.ascontiguousarray(rww.reshape(64, 8, 128).transpose(2, 1, 0))
    bqk = np.ascontiguousarray(
        np.concatenate([bq.reshape(8, 128).T, bk.reshape(8, 128).T], axis=1))
    vb = np.ascontiguousarray(np.broadcast_to(bv, (128, H)))
    ob = np.ascontiguousarray(np.broadcast_to(obv, (128, H)))
    eb = np.ascontiguousarray(np.broadcast_to(ebv, (128, 64)))
    i64 = np.ascontiguousarray(
        np.broadcast_to(np.arange(64, dtype=np.float32) - 127.0, (128, 64)))
    i8 = np.ascontiguousarray(
        np.broadcast_to(np.arange(8, dtype=np.float32) - 127.0, (128, 8)))
    idf = np.eye(128, dtype=np.float32)
    idb = np.eye(128).astype(ml_dtypes.bfloat16)
    pr = np.arange(128)
    ntri = np.where(pr[None, :] <= pr[:, None], 0.0, -10000.0).astype(np.float32)

    swg_u = SWg.reshape(8, 128, 2, 512).transpose(2, 1, 0, 3)
    swu_u = SWu.reshape(8, 128, 2, 512).transpose(2, 1, 0, 3)
    swd_u = SWd.reshape(2, 4, 128, H).transpose(0, 2, 1, 3)
    wg_e = Wg.reshape(64, 8, 128, 512).transpose(0, 2, 1, 3)
    wu_e = Wu.reshape(64, 8, 128, 512).transpose(0, 2, 1, 3)
    wd_e = Wd.reshape(64, 4, 128, H).transpose(0, 2, 1, 3)
    wg = np.ascontiguousarray(np.concatenate([swg_u, wg_e], axis=0))
    wu = np.ascontiguousarray(np.concatenate([swu_u, wu_e], axis=0))
    wd = np.ascontiguousarray(np.concatenate([swd_u, wd_e], axis=0))

    shared = dict(wq=wq, wk=wk, wv=wv, wo=wo, rw=rw, bqk=bqk, vb=vb, ob=ob,
                  eb=eb, i64=i64, i8=i8, idf=idf, idb=idb, ntri=ntri)
    if KSTAGE not in ("ln", "attn", "route"):
        shared.update(wg=wg, wu=wu, wd=wd)
    if KMOE == "sparse" and KSTAGE == "full":
        kk = np.arange(128)
        ustr = (kk[:, None] < kk[None, :]).astype(np.float32)
        onesm = np.ones((128, 128), np.float32)
        eiv = np.ascontiguousarray(
            np.broadcast_to(np.arange(64, dtype=np.float32), (128, 64)))
        gio = np.ascontiguousarray(
            np.broadcast_to(np.arange(NGRAN, dtype=np.float32), (128, NGRAN)))
        iop = np.arange(128, dtype=np.float32).reshape(128, 1).copy()
        shared.update(ustr=ustr, onesm=onesm, ei=eiv, gio=gio, iop=iop)
    in_maps = []
    for c in range(NCORES):
        m = dict(shared)
        m["x"] = np.ascontiguousarray(x[c])
        in_maps.append(m)
    return in_maps


def kernel(**inputs):
    in_maps = _prep(inputs)
    nc = build_program()
    res = run_bass_kernel_spmd(nc, in_maps, core_ids=list(range(NCORES)))
    out = np.concatenate([np.asarray(r["y"], dtype=np.float32) for r in res.results],
                         axis=0)
    return out.reshape(16, 2048, 1024)
